# version 11
# baseline (speedup 1.0000x reference)
import numpy as np
import ml_dtypes
from contextlib import ExitStack

import concourse.bass as bass
import concourse.mybir as mybir
from concourse.bass_utils import run_bass_kernel_spmd

F32 = mybir.dt.float32
BF16 = mybir.dt.bfloat16
I32 = mybir.dt.int32
AF = mybir.ActivationFunctionType
ALU = mybir.AluOpType
AX = mybir.AxisListType
BF = ml_dtypes.bfloat16

C, H, SD, NCH = 1024, 16, 64, 8
TH, L, TILE = 4096, 128, 512
NTIL, CPT = TH // TILE, TILE // L
FFH = 3072
EPS = 1e-5
GN_EPS = 64.0 * 1e-5
LEVELS = (64, 32, 16, 8, 4)
LOFF = {64: 0, 32: 512, 16: 1024, 8: 1536, 4: 2048}
WA = 40960


class B:
    def __init__(self, nc, stack):
        self.nc, self.st = nc, stack
        self.sems = {}
        self.q = {e: [] for e in ("sync", "gpsimd", "tensor", "vector", "scalar")}

    def sem(self, name):
        if name not in self.sems:
            self.sems[name] = [self.st.enter_context(self.nc.semaphore(name)), 0]
        return self.sems[name]

    def do(self, eng, fn, waits=(), inc=None, k=1):
        wl = []
        for wn in waits:
            e = self.sem(wn)
            if e[1] > 0:
                wl.append((e[0], e[1]))
        hk = None
        if inc is not None:
            e = self.sem(inc)
            e[1] += k
            hk = (e[0], k)

        def run(eh):
            for h, v in wl:
                eh.wait_ge(h, v)
            ins = fn(eh)
            if hk is not None:
                ins.then_inc(hk[0], hk[1])
        self.q[eng].append(run)

    def chainpt(self, name):
        e = self.sem(name)
        e[1] += 1
        return (e[0], e[1])

    def pre(self, name, k):
        e = self.sem(name)
        e[1] += k
        return e[0]

    def waitonly(self, eng, *names):
        wl = []
        for wn in names:
            e = self.sem(wn)
            if e[1] > 0:
                wl.append((e[0], e[1]))

        def run(eh):
            for h, v in wl:
                eh.wait_ge(h, v)
        self.q[eng].append(run)


def build_program(debug=False, trunc=0):
    nc = bass.Bass("TRN2", target_bir_lowering=False)
    st = ExitStack()
    b = B(nc, st)

    d_px = nc.dram_tensor("px", [C, TH], F32, kind="ExternalInput")
    d_mx = nc.dram_tensor("mx", [C, TH], F32, kind="ExternalInput")
    dW = {z: nc.dram_tensor(z, [C, C], F32, kind="ExternalInput")
          for z in ("wrT", "wkT", "wvT", "wgT", "woT")}
    d_fg = nc.dram_tensor("fgT", [C, FFH], I32, kind="ExternalInput")
    d_fu = nc.dram_tensor("fuT", [C, FFH], I32, kind="ExternalInput")
    d_fd = nc.dram_tensor("fdT", [FFH, C], I32, kind="ExternalInput")
    d_mix = nc.dram_tensor("mix", [4, C], F32, kind="ExternalInput")
    d_dec = nc.dram_tensor("dec", [C], F32, kind="ExternalInput")
    d_faa = nc.dram_tensor("faa", [C], F32, kind="ExternalInput")
    d_lnw = nc.dram_tensor("lnw", [C], F32, kind="ExternalInput")
    d_lnb = nc.dram_tensor("lnb", [C], F32, kind="ExternalInput")
    d_con = nc.dram_tensor("consts", [128, 1024], F32, kind="ExternalInput")
    d_x1 = nc.dram_tensor("x1buf", [C, TH], F32)
    d_out = nc.dram_tensor("out", [C, TH], F32, kind="ExternalOutput")
    ddbg = {}
    if debug:
        for nm in ("dbr", "dbk", "dbg", "dby", "dbyg", "dbkt", "dbvt"):
            ddbg[nm] = nc.dram_tensor(nm, [C, TH], F32, kind="ExternalOutput")

    wbig = st.enter_context(nc.sbuf_tensor("wbig", [128, 73728], BF16))
    t2 = st.enter_context(nc.sbuf_tensor("t2", [128, 19200], BF16))
    smf = st.enter_context(nc.sbuf_tensor("smf", [128, 132], F32))
    rowsA = st.enter_context(nc.sbuf_tensor("rowsA", [128, 1280], F32))
    rowsG = st.enter_context(nc.sbuf_tensor("rowsG", [128, 80], F32))
    ident = st.enter_context(nc.sbuf_tensor("identb", [128, 128], BF16))
    zsb = st.enter_context(nc.sbuf_tensor("zsb", [128, 384], BF16))
    trim = st.enter_context(nc.sbuf_tensor("trim", [128, 128], BF16))
    cval = st.enter_context(nc.sbuf_tensor("cval", [128, 64], F32))
    stgm = st.enter_context(nc.sbuf_tensor("stgm", [128, 64], F32))
    selband = st.enter_context(nc.sbuf_tensor("selband", [128, 512], BF16))
    pqs = {m: st.enter_context(nc.sbuf_tensor(f"pq{m}", [128, 8 * m], BF16))
           for m in LEVELS}
    pks = {m: st.enter_context(nc.sbuf_tensor(f"pk{m}", [128, 8 * m], BF16))
           for m in LEVELS}
    pq_int = st.enter_context(nc.sbuf_tensor("pq_int", [128, 1024], BF16))
    pk_tm = st.enter_context(nc.sbuf_tensor("pk_tm", [128, 1024], BF16))
    sstate = st.enter_context(nc.sbuf_tensor("sstate", [128, 512], F32))
    sstateb = st.enter_context(nc.sbuf_tensor("sstateb", [128, 2048], BF16))
    bcolb = st.enter_context(nc.sbuf_tensor("bcolb", [128, 16], BF16))
    attf = st.enter_context(nc.sbuf_tensor("attf", [128, 512], F32))
    onesb = st.enter_context(nc.sbuf_tensor("onesb", [128, 1], BF16))
    zscat = st.enter_context(nc.sbuf_tensor("zscat", [128, 128], BF16))
    gnsq = st.enter_context(nc.sbuf_tensor("gnsq", [128, 512], BF16))
    ps = [st.enter_context(nc.psum_tensor(f"ps{i}", [128, 512], F32)) for i in range(7)]
    pbt = st.enter_context(nc.psum_tensor("pbt", [128, 1024], BF16))

    def wvv(off, n):
        return wbig[:, off:off + n]

    W_r, W_k, W_v, W_g, W_o = (wvv(i * 8192, 8192) for i in range(5))
    xh = wvv(WA, 4096)
    mixr = wvv(WA + 4096, 4096)
    mixg = wvv(WA + 8192, 4096)
    rF = wvv(WA + 12288, 4096)
    kF = wvv(WA + 16384, 4096)
    gF = wvv(WA + 20480, 4096)
    ygF = wvv(WA + 24576, 4096)
    tri = wvv(WA + 28672, 2048)
    stgb = wvv(WA + 30720, 2048)
    Fg, Fu, Fd = wvv(0, 24576), wvv(24576, 24576), wvv(49152, 24576)
    sqL_o, skL_o, sqi_o, pdb_o = 0, 16384, 5120, 6144
    LOFF2 = {64: 0, 32: 1024, 16: 2048, 8: 3072, 4: 4096}
    SLOFF = {64: 0, 32: 512, 16: 1024, 8: 1792, 4: 2304}
    mixk = t2[:, 7168:11264]
    mixvc = t2[:, 11264:12288]
    kTc = t2[:, 12288:13312]
    vTc = t2[:, 13312:14336]
    vsh = t2[:, 14336:15360]
    csb = t2[:, 15360:15424]
    sqi = t2[:, 5120:6144]
    pdb = t2[:, 6144:7168]
    x1b = t2[:, 0:4096]
    hb = t2[:, 4096:16384]

    M_R, M_K, M_V, M_G = (smf[:, 8 * i:8 * i + 8] for i in range(4))
    LNW, LNB = smf[:, 32:40], smf[:, 40:48]
    DEC, UU = smf[:, 48:56], smf[:, 56:64]
    LW, W1 = smf[:, 64:72], smf[:, 72:80]
    W2, W128 = smf[:, 80:88], smf[:, 88:96]
    M1 = {0: smf[:, 96:104], 1: smf[:, 104:112], 2: smf[:, 112:120], 3: smf[:, 120:128]}
    MM = {0: M_R, 1: M_K, 2: M_V, 3: M_G}

    def t2ap(off, dims, p0=0, np_=128):
        return bass.AP(t2, p0 * 19200 + off, [[19200, np_]] + dims)

    def wbap(off, dims, p0=0, np_=128):
        return bass.AP(wbig, p0 * 73728 + off, [[73728, np_]] + dims)

    def setup():
        def f(g):
            g.memset(t2[:, 15424:16384], 0)
            g.memset(t2[:, 0:5120], 0)
            g.memset(t2[:, 17408:18176], 0)
            g.memset(smf[:, 130:131], 1.0)
            g.memset(stgb[:, :], 0)
            g.memset(selband[:, :], 0)
            g.memset(pdb[:, :], 0)
            g.memset(csb[:, :], 0)
            g.memset(bcolb[:, :], 0)
            g.memset(rowsA[0:1, 1024:1152], 1.0)
            g.memset(smf[:, 128:129], EPS)
            g.memset(smf[:, 129:130], GN_EPS)
            g.memset(rowsA[0:1, 1152:1280], -1.0)
            return g.engine_nop()
        b.do("gpsimd", f, waits=("s_small",), inc="s_zero")

        def f(v):
            v.memset(sstate[:, :], 0)
            v.memset(sstateb[:, :], 0)
            for i in range(7):
                v.memset(ps[i][:, :], 0)
            return v.engine_nop()
        b.do("vector", f, inc="s_zero")

        h_small = b.pre("s_small", 16 * 11)

        def f(s):
            s.dma_start(cval[:, :], bass.AP(d_con, 256, [[1024, 128], [1, 64]])).then_inc(h_small, 16)
            s.dma_start(stgm[:, :], bass.AP(d_con, 960, [[1024, 128], [1, 64]])).then_inc(h_small, 16)
            for z in range(4):
                s.dma_start(smf[:, z * 8:z * 8 + 8],
                            bass.AP(d_mix, z * C, [[1, 128], [128, 8]]),
                            allow_slow_non_contiguous=True).then_inc(h_small, 16)
            s.dma_start(LNW, bass.AP(d_lnw, 0, [[1, 128], [128, 8]]), allow_slow_non_contiguous=True).then_inc(h_small, 16)
            s.dma_start(LNB, bass.AP(d_lnb, 0, [[1, 128], [128, 8]]), allow_slow_non_contiguous=True).then_inc(h_small, 16)
            s.dma_start(DEC, bass.AP(d_dec, 0, [[1, 128], [128, 8]]), allow_slow_non_contiguous=True).then_inc(h_small, 16)
            s.dma_start(UU, bass.AP(d_faa, 0, [[1, 128], [128, 8]]), allow_slow_non_contiguous=True).then_inc(h_small, 16)
            s.dma_start(rowsA[0:1, 0:1024], bass.AP(d_dec, 0, [[0, 1], [1, 1024]])).then_inc(h_small, 16)
            return None

        b.q["sync"].append(f)

        b.do("sync", lambda s: s.dma_start(d_x1.ap(), d_mx.ap()), inc="s_x1pre", k=16)

        h_wl = b.pre("s_wload", 16 * 7)
        h_wkv = b.pre("s_wkv", 16 * 2)

        def f(g):
            sems_w = b.sems["s_small"]
            g.wait_ge(sems_w[0], sems_w[1])
            for dram, view in ((dW["wkT"], W_k), (dW["wvT"], W_v)):
                g.dma_start(view.rearrange("p (a c) -> p a c", c=1024),
                            bass.AP(dram, 0, [[1024, 128], [1024 * 128, 8], [1, 1024]])).then_inc(h_wkv, 16)
            g.dma_start(ident[:, :], bass.AP(d_con, 0, [[1024, 128], [1, 128]])).then_inc(h_wl, 16)
            g.dma_start(zsb[:, :].rearrange("p (a c) -> p a c", c=128),
                        bass.AP(d_con, 448, [[1024, 128], [128, 3], [1, 128]])).then_inc(h_wl, 16)
            g.dma_start(zscat[:, :], bass.AP(d_con, 832, [[1024, 128], [1, 128]])).then_inc(h_wl, 16)
            g.dma_start(trim[:, :], bass.AP(d_con, 128, [[1024, 128], [1, 128]])).then_inc(h_wl, 16)
            for dram, view in ((dW["wrT"], W_r), (dW["wgT"], W_g), (dW["woT"], W_o)):
                g.dma_start(view.rearrange("p (a c) -> p a c", c=1024),
                            bass.AP(dram, 0, [[1024, 128], [1024 * 128, 8], [1, 1024]])).then_inc(h_wl, 16)
        b.q["gpsimd"].append(f)

        p3 = b.chainpt("c_act")

        def f(a):
            a.activation(rowsA[0:1, 0:1024], rowsA[0:1, 0:1024],
                         AF.Exp).then_inc(p3[0], 1)
            a.wait_ge(p3[0], p3[1])
            return a.mul(rowsA[0:1, 0:1024], rowsA[0:1, 0:1024], -1.0)
        b.do("scalar", f, waits=("s_small",), inc="s_lwrow")

        def f(g):
            return g.iota(rowsG[:, 64:65], [[0, 1]], base=127, channel_multiplier=-1,
                          allow_small_or_imprecise_dtypes=True)
        b.do("gpsimd", f, inc="s_riota")

        for piece in (1, 0):
            def fmm(t, piece=piece):
                return t.matmul(ps[6][:, 0:512], rowsA[0:1, 1024:1152],
                                rowsA[0:1, piece * 512:piece * 512 + 512],
                                start=True, stop=True)
            b.do("tensor", fmm, waits=("s_lwrow", "s_lwbc", "s_zero"),
                 inc="s_rbmm")

            def fcp(a, piece=piece):
                return a.activation(rowsA[:, 512:1024], ps[6][:, 0:512], AF.Copy)
            b.do("scalar", fcp, waits=("s_rbmm", "s_pktm"), inc="s_lwbc")

            def f(a, piece=piece):
                return a.activation(pk_tm[:, piece * 512:piece * 512 + 512],
                                    rowsA[:, 512:1024], AF.Exp,
                                    scale=rowsG[:, 64:65])
            b.do("scalar", f, waits=("s_lwbc", "s_riota"), inc="s_pktm")

        p1 = b.chainpt("c_act")
        p2 = b.chainpt("c_act")

        def f(a):
            a.activation(rowsA[:, 128:136], DEC, AF.Exp).then_inc(p1[0], 1)
            a.wait_ge(p1[0], p1[1])
            a.mul(LW, rowsA[:, 128:136], -1.0).then_inc(p2[0], 1)
            a.wait_ge(p2[0], p2[1])
            a.activation(W1, LW, AF.Exp)
            a.activation(W2, LW, AF.Exp, scale=2.0)
            return a.activation(W128, LW, AF.Exp, scale=128.0)
        b.do("scalar", f, waits=("s_small", "s_pktm"), inc="s_lw")

        def f(v):
            for i in range(4):
                v.tensor_scalar(M1[i], MM[i], -1.0, 1.0, ALU.mult, ALU.add)
            return v.engine_nop()
        b.do("vector", f, waits=("s_small",), inc="s_m1")

        ROFF = {64: 0, 32: 64, 16: 96, 8: 112, 4: 120}

        pm1 = b.chainpt("c_gp")

        def f(g):
            g.memset(rowsA[0:1, 128:256], 0).then_inc(pm1[0], 1)
            g.wait_ge(pm1[0], pm1[1])
            for m in LEVELS:
                g.iota(rowsA[0:1, 128 + ROFF[m]:128 + ROFF[m] + m], [[0, 1], [1, m]],
                       channel_multiplier=0, allow_small_or_imprecise_dtypes=True)
            return g.engine_nop()
        b.do("gpsimd", f, waits=("s_small", "s_lw", "s_pktm"), inc="s_ramp")

        def fmm(t):
            return t.matmul(ps[6][:, 0:128], rowsA[0:1, 1024:1152],
                            rowsA[0:1, 128:256], start=True, stop=True)
        b.do("tensor", fmm, waits=("s_ramp", "s_zero", "s_lwbc"), inc="s_rbmm")

        def fcp(a):
            return a.activation(rowsA[:, 256:384], ps[6][:, 0:128], AF.Copy)
        b.do("scalar", fcp, waits=("s_rbmm", "s_pktm"), inc="s_rbc")

        def f(a):
            for m in LEVELS:
                for ch in range(NCH):
                    ins = a.activation(pqs[m][:, ch * m:(ch + 1) * m],
                                       rowsA[:, 256 + ROFF[m]:256 + ROFF[m] + m],
                                       AF.Exp, scale=LW[:, ch:ch + 1])
            return ins
        b.do("scalar", f, waits=("s_rbc", "s_lw"), inc="s_pow")

        pm2 = b.chainpt("c_gp")

        def f(g):
            g.memset(rowsA[0:1, 128:256], 0).then_inc(pm2[0], 1)
            g.wait_ge(pm2[0], pm2[1])
            for m in LEVELS:
                g.iota(rowsA[0:1, 128 + ROFF[m]:128 + ROFF[m] + m], [[0, 1], [-1, m]],
                       base=m - 1, channel_multiplier=0,
                       allow_small_or_imprecise_dtypes=True)
            return g.engine_nop()
        b.do("gpsimd", f, waits=("s_pow",), inc="s_ramp")

        def fmm(t):
            return t.matmul(ps[6][:, 0:128], rowsA[0:1, 1024:1152],
                            rowsA[0:1, 128:256], start=True, stop=True)
        b.do("tensor", fmm, waits=("s_ramp", "s_pow"), inc="s_rbmm")

        def fcp(a):
            return a.activation(rowsA[:, 256:384], ps[6][:, 0:128], AF.Copy)
        b.do("scalar", fcp, waits=("s_rbmm", "s_pow"), inc="s_rbc")

        def f(a):
            for m in LEVELS:
                for ch in range(NCH):
                    ins = a.activation(pks[m][:, ch * m:(ch + 1) * m],
                                       rowsA[:, 256 + ROFF[m]:256 + ROFF[m] + m],
                                       AF.Exp, scale=LW[:, ch:ch + 1])
            return ins
        b.do("scalar", f, waits=("s_rbc",), inc="s_pow")

        def f(g):
            return g.iota(rowsA[0:1, 128:256], [[0, 1], [1, 128]],
                          channel_multiplier=0, allow_small_or_imprecise_dtypes=True)
        b.do("gpsimd", f, waits=("s_pow",), inc="s_ramp")

        def fmm(t):
            return t.matmul(ps[6][:, 0:128], rowsA[0:1, 1024:1152],
                            rowsA[0:1, 128:256], start=True, stop=True)
        b.do("tensor", fmm, waits=("s_ramp", "s_pow"), inc="s_rbmm")

        def fcp(a):
            return a.activation(rowsA[:, 256:384], ps[6][:, 0:128], AF.Copy)
        b.do("scalar", fcp, waits=("s_rbmm", "s_pow"), inc="s_rbc")

        def f(a):
            for ch in range(NCH):
                ins = a.activation(pq_int[:, ch * 128:(ch + 1) * 128],
                                   rowsA[:, 256:384], AF.Exp,
                                   scale=LW[:, ch:ch + 1])
            return ins
        b.do("scalar", f, waits=("s_rbc",), inc="s_pow")

        b.do("vector", lambda v: v.engine_nop(), waits=("s_pow",), inc="f_ps6")

        def f(v):
            for ch in range(NCH):
                for d in range(4):
                    base = ch * 64 + d * 16
                    for half in range(2):
                        hh = 2 * ch + half
                        o = selband[64 * half:64 * half + 64, base + hh:base + hh + 1]
                        if d == 0:
                            v.tensor_copy(o, UU[64 * half:64 * half + 64, ch:ch + 1])
                        elif d == 1:
                            v.tensor_copy(o, smf[64 * half:64 * half + 64, 130:131])
                        elif d == 2:
                            v.tensor_copy(o, W1[64 * half:64 * half + 64, ch:ch + 1])
                        else:
                            v.tensor_copy(o, W2[64 * half:64 * half + 64, ch:ch + 1])
            return v.engine_nop()
        b.do("vector", f, waits=("s_zero", "s_lw"), inc="s_sel")

    def ln_xhat(ti, d_src, ph, dst, dst_o, scratch, scratch_o, arena):
        t0 = ti * TILE

        def f(g):
            return g.dma_start(
                dst.rearrange("p (a c) -> p a c", c=512),
                bass.AP(d_src, t0, [[TH, 128], [TH * 128, 8], [1, 512]]))
        b.do("gpsimd", f,
             waits=("f_xh",) + (("s_wload",) if ph == "mn" else ()),
             inc="s_xl", k=16)

        if ph == "fn":
            for ch in range(NCH):
                def f(a, ch=ch):
                    return a.activation(gnsq[:, :], dst[:, ch * 512:(ch + 1) * 512],
                                        AF.Square)
                b.do("scalar", f, waits=("s_xl", "s_fsqm", "s_gnr"), inc="s_fsq")

                def f(t, ch=ch):
                    return t.matmul(ps[6][32:33, 0:512], onesb[:, 0:1], gnsq[:, :],
                                    start=(ch == 0), stop=(ch == NCH - 1),
                                    tile_position=(0, 32))
                b.do("tensor", f, waits=("s_fsq", "f_ps6" if ch == 0 else "s_fsq",),
                     inc="s_fsqm")
        else:
            def f(a):
                return a.activation(scratch[:, :], dst[:, :], AF.Square)
            b.do("scalar", f, waits=("s_xl", "f_scr", "s_dbg"), inc="s_xsq")

        if ph == "pre":
            stat, ab, bb = ps[6], ps[4], ps[6]
            w_stat = ("s_xsq", "f_ps6", "s_onesb")
            w_musq = ("s_lrv", "f_ps4")
            w_bc = ("s_lrv", "f_ps4", "f_ps6")
            rel = ("f_ps4", "f_ps6")
        elif ph == "mn":
            stat, ab, bb = ps[2], ps[3], ps[2]
            w_stat = ("s_xsq", "s_vT", "s_onesb")
            w_musq = ("s_lrv", "s_vT")
            w_bc = ("s_lrv",)
            rel = ()
        else:
            stat, ab, bb = ps[6], ps[5], ps[4]
            w_stat = ("f_ps6", "s_onesb")
            w_musq = ("s_lrv", "f_ps5")
            w_bc = ("s_lrv", "f_ps4", "f_ps5")
            rel = ("f_ps4", "f_ps5", "f_ps6")

        def f(t):
            for ch in range(NCH):
                ins = t.matmul(stat[0:1, 0:512], onesb[:, 0:1],
                               dst[:, ch * 512:(ch + 1) * 512],
                               start=(ch == 0), stop=(ch == NCH - 1))
            if ph != "fn":
                for ch in range(NCH):
                    ins = t.matmul(stat[32:33, 0:512], onesb[:, 0:1],
                                   scratch[:, ch * 512:(ch + 1) * 512],
                                   start=(ch == 0), stop=(ch == NCH - 1),
                                   tile_position=(0, 32))
            return ins
        b.do("tensor", f, waits=w_stat + (("s_fsqm",) if ph == "fn" else ()),
             inc="s_lnmm")

        def f(v):
            v.tensor_scalar_mul(rowsA[0:1, 0:512], stat[0:1, 0:512], 1.0 / C)
            return v.tensor_scalar_mul(rowsA[0:1, 512:1024], stat[32:33, 0:512], 1.0 / C)
        b.do("vector", f, waits=("s_lnmm", "s_pktm"), inc="s_lrv")
        if ph == "fn":
            b.do("vector", lambda v: v.engine_nop(), inc="f_ps6")

        def f(a):
            return a.activation(ab[0:1, 0:512], rowsA[0:1, 0:512], AF.Square)
        b.do("scalar", f, waits=w_musq, inc="s_lra")

        def f(v):
            v.tensor_sub(rowsA[0:1, 512:1024], rowsA[0:1, 512:1024], ab[0:1, 0:512])
            return v.engine_nop()
        b.do("vector", f, waits=("s_lra",), inc="s_lrv")

        def f(a):
            return a.activation(rowsA[0:1, 512:1024], rowsA[0:1, 512:1024],
                                AF.Sqrt, bias=smf[0:1, 128:129])
        b.do("scalar", f, waits=("s_lrv",), inc="s_lra")

        pc = b.chainpt("c_dve")

        def f(v):
            v.reciprocal(rowsA[0:1, 512:1024], rowsA[0:1, 512:1024]).then_inc(pc[0], 1)
            v.wait_ge(pc[0], pc[1])
            return v.tensor_mul(rowsA[0:1, 0:512], rowsA[0:1, 0:512],
                                rowsA[0:1, 512:1024])
        b.do("vector", f, waits=("s_lra",), inc="s_lrv")

        def f(t):
            t.matmul(ab[:, 0:512], rowsA[0:1, 1024:1152], rowsA[0:1, 512:1024],
                     start=True, stop=True)
            return t.matmul(bb[:, 0:512], rowsA[0:1, 1152:1280], rowsA[0:1, 0:512],
                            start=True, stop=True)
        b.do("tensor", f, waits=w_bc, inc="s_lnbc")

        pch = b.chainpt("c_dve")

        def f(v):
            a_bc = bass.AP(ab, 0, [[512, 128], [0, 8], [1, 512]])
            b_bc = bass.AP(bb, 0, [[512, 128], [0, 8], [1, 512]])
            d3 = dst.rearrange("p (a c) -> p a c", c=512)
            v.tensor_tensor(d3, d3, a_bc, ALU.mult).then_inc(pch[0], 1)
            v.wait_ge(pch[0], pch[1])
            return v.tensor_tensor(d3, d3, b_bc, ALU.add)
        b.do("vector", f, waits=("s_lnbc",), inc="s_xhat")
        for rr in rel:
            b.do("vector", lambda v: v.engine_nop(), waits=("s_xhat",), inc=rr)

    def mix_full(ph, out, mi, waitx, inc="s_mixf", extraw=(), bslot=0):
        def fa(a):
            o3 = out.rearrange("p (a c) -> p a c", c=512)
            x3 = xh.rearrange("p (a c) -> p a c", c=512)
            for ch in range(NCH):
                ins = a.activation(o3[:, ch:ch + 1, :], x3[:, ch:ch + 1, :],
                                   AF.Copy, scale=MM[mi][:, ch:ch + 1])
            return ins
        b.do("scalar", fa, waits=(waitx, "s_m1", "s_tmproj") + extraw, inc="s_mixa")

        def fv(v):
            o3 = out.rearrange("p (a c) -> p a c", c=512)
            x3 = xh.rearrange("p (a c) -> p a c", c=512)
            for ch in range(NCH):
                v.scalar_tensor_tensor(o3[:, ch:ch + 1, 1:512], x3[:, ch:ch + 1, 0:511],
                                       M1[mi][:, ch:ch + 1], o3[:, ch:ch + 1, 1:512],
                                       ALU.mult, ALU.add)
                ins = v.scalar_tensor_tensor(
                    o3[:, ch:ch + 1, 0:1],
                    bcolb[:, bslot * 8 + ch:bslot * 8 + ch + 1].unsqueeze(2),
                    M1[mi][:, ch:ch + 1], o3[:, ch:ch + 1, 0:1], ALU.mult, ALU.add)
            return ins
        b.do("vector", fv, waits=("s_mixa",), inc=inc)

    def fm_proj(ph, nm, W, src, dst, act, waits):
        for co in range(NCH):
            bank = co % 2

            def f(t, co=co, bank=bank):
                for ci in range(NCH):
                    ins = t.matmul(ps[bank][:, 0:512],
                                   W[:, ci * 1024 + co * 128:ci * 1024 + co * 128 + 128],
                                   src[:, ci * 512:(ci + 1) * 512],
                                   start=(ci == 0), stop=(ci == NCH - 1))
                return ins
            b.do("tensor", f, waits=waits + (f"f_pj{bank}",), inc="s_mm")

            def f(a, co=co, bank=bank):
                return a.activation(dst[:, co * 512:(co + 1) * 512],
                                    ps[bank][:, 0:512], act)
            b.do("scalar", f, waits=("s_mm",), inc=f"f_pj{bank}")
        b.do("tensor", lambda t: t.nop(), waits=("f_pj0", "f_pj1"),
             inc="s_pjd")

    def scan_chunk(cc, ci, ph, full, pending=None):
        def cut(n):
            return full and trunc == 40 + n
        c0 = ci * 128
        par = cc % 2

        def f(t):
            for cin in range(NCH):
                stk = mixk[:, cin * 512 + c0:cin * 512 + c0 + 128]
                for gs in range(2):
                    ins = t.matmul(ps[gs][:, 0:512], stk,
                                   W_k[:, cin * 1024 + gs * 512:cin * 1024 + gs * 512 + 512],
                                   start=(cin == 0), stop=(cin == NCH - 1))
            return ins
        b.do("tensor", f, waits=("s_mixk", "f_sc", "f_pj0", "f_pj1", "s_wkv"),
             inc="s_tmk")

        def f(t):
            for cin in range(NCH):
                stv = mixr[:, cin * 512 + c0:cin * 512 + c0 + 128]
                for gs in range(2):
                    ins = t.matmul(ps[2 + gs][:, 0:512], stv,
                                   W_v[:, cin * 1024 + gs * 512:cin * 1024 + gs * 512 + 512],
                                   start=(cin == 0), stop=(cin == NCH - 1))
            return ins
        b.do("tensor", f, waits=("s_mixv", "s_tmk"), inc="s_tmproj")

        def f(v):
            v.tensor_tensor(kTc[:, 0:512], ps[0][:, 0:512], pk_tm[:, 0:512], ALU.mult)
            return v.tensor_tensor(kTc[:, 512:1024], ps[1][:, 0:512],
                                   pk_tm[:, 512:1024], ALU.mult)
        b.do("vector", f, waits=("s_tmk", "f_kvT", "s_pktm", "s_dbg"), inc="s_kT")

        def f(a):
            a.activation(vTc[:, 0:512], ps[2][:, 0:512], AF.Copy)
            return a.activation(vTc[:, 512:1024], ps[3][:, 0:512], AF.Copy)
        b.do("scalar", f, waits=("s_tmproj", "f_kvT", "s_dbg"), inc="s_vT")

        def f(t):
            ins = None
            for p in range(8):
                for half in range(2):
                    hh = 2 * p + half
                    ins = t.matmul(
                        ps[5][64 * half:64 * half + 64, p * 64:(p + 1) * 64],
                        kTc[:, hh * 64:(hh + 1) * 64], vTc[:, hh * 64:(hh + 1) * 64],
                        start=True, stop=True,
                        tile_position=(0, 64 * half))
            return ins
        b.do("tensor", f, waits=("s_kT", "s_vT", "f_ps5"), inc="s_umm")

        mirror = full or cc == 31
        pch2 = b.chainpt("c_dve") if mirror else None

        def f(v):
            for p in range(8):
                ins = v.scalar_tensor_tensor(
                    sstate[:, p * 64:(p + 1) * 64], sstate[:, p * 64:(p + 1) * 64],
                    W128[:, p:p + 1], ps[5][:, p * 64:(p + 1) * 64], ALU.mult, ALU.add)
            if mirror:
                ins.then_inc(pch2[0], 1)
                v.wait_ge(pch2[0], pch2[1])
                npo = (1 - par) * 1024
                v.tensor_copy(bass.AP(sstateb, npo, [[2048, 64], [128, 8], [1, 64]]),
                              bass.AP(sstate, 0, [[512, 64], [64, 8], [1, 64]]))
                ins = v.tensor_copy(
                    bass.AP(sstateb, 64 * 2048 + npo + 64,
                            [[2048, 64], [128, 8], [1, 64]]),
                    bass.AP(sstate, 64 * 512, [[512, 64], [64, 8], [1, 64]]))
            return ins
        b.do("vector", f, waits=("s_umm", "s_intery"), inc="s_chain")
        b.do("vector", lambda v: v.engine_nop(), inc="f_ps5")

        if not full:
            b.do("scalar", lambda a: a.nop(), waits=("s_umm",), inc="f_kvT")
            b.do("scalar", lambda a: a.nop(), waits=("s_kT", "s_vT"), inc="f_sc")
            if ci == CPT - 1:
                b.do("vector", lambda v: v.engine_nop(), waits=("s_chain",),
                     inc="f_scr")
            return

        def f(g):
            g.tensor_tensor(t2ap(sqi_o, [[128, 8], [1, 128]]),
                            wbap(WA + 12288 + c0, [[512, 8], [1, 128]]),
                            bass.AP(pq_int, 0, [[1024, 128], [128, 8], [1, 128]]),
                            ALU.mult)
            for m in (64, 32):
                nb = 64 // m
                for half in range(2):
                    g.tensor_tensor(
                        t2ap(LOFF2[m] + half * m, [[128, 8], [2 * m, nb], [1, m]],
                             p0=64 * half, np_=64),
                        wbap(WA + 12288 + c0 + m, [[512, 8], [2 * m, nb], [1, m]],
                             p0=64 * half, np_=64),
                        bass.AP(pqs[m], 64 * half * 8 * m,
                                [[8 * m, 64], [m, 8], [0, nb], [1, m]]),
                        ALU.mult)
            return g.engine_nop()
        b.do("gpsimd", f, waits=("s_pjd", "f_sq", "s_pow", "s_zero"), inc="s_sq1")

        def f(v):
            for m in (16, 8, 4):
                nb = 64 // m
                for half in range(2):
                    v.tensor_tensor(
                        t2ap(LOFF2[m] + half * m, [[128, 8], [2 * m, nb], [1, m]],
                             p0=64 * half, np_=64),
                        wbap(WA + 12288 + c0 + m, [[512, 8], [2 * m, nb], [1, m]],
                             p0=64 * half, np_=64),
                        bass.AP(pqs[m], 64 * half * 8 * m,
                                [[8 * m, 64], [m, 8], [0, nb], [1, m]]),
                        ALU.mult)
            for m in (64, 32, 8, 4):
                nb = 64 // m
                v.tensor_tensor(
                    t2ap(skL_o + SLOFF[m], [[64, 8], [m, nb], [1, m]]),
                    wbap(WA + 16384 + c0, [[512, 8], [2 * m, nb], [1, m]]),
                    bass.AP(pks[m], 0, [[8 * m, 128], [m, 8], [0, nb], [1, m]]),
                    ALU.mult)
            v.tensor_tensor(
                t2ap(skL_o + SLOFF[16], [[96, 8], [16, 3], [1, 16]]),
                wbap(WA + 16384 + c0, [[512, 8], [32, 3], [1, 16]]),
                bass.AP(pks[16], 0, [[128, 128], [16, 8], [0, 3], [1, 16]]),
                ALU.mult)
            v.tensor_tensor(
                t2ap(skL_o + SLOFF[16] + 80, [[96, 8], [1, 16]]),
                wbap(WA + 16384 + c0 + 96, [[512, 8], [1, 16]]),
                bass.AP(pks[16], 0, [[128, 128], [16, 8], [1, 16]]),
                ALU.mult)
            return v.engine_nop()
        b.do("vector", f, waits=("s_pjd", "f_sq", "s_pow", "s_zero"), inc="s_sq2")
        if cut(1):
            return

        for d in range(4):
            def f(g, d=d):
                r3 = rF.rearrange("p (a c) -> p a c", c=512)
                k3 = kF.rearrange("p (a c) -> p a c", c=512)
                return g.tensor_tensor(
                    pdb.rearrange("p (a c) -> p a c", c=128)[:, :, d:128],
                    r3[:, :, c0 + d:c0 + 128], k3[:, :, c0:c0 + 128 - d], ALU.mult)
            b.do("gpsimd", f, waits=("s_pjd", "s_pjd",
                                     "f_pdf"), inc="s_pd")

            def f(t, d=d):
                for ch in range(NCH):
                    ins = t.matmul(ps[6][0:128 - d, d * 16:(d + 1) * 16],
                                   pdb[:, ch * 128 + d:ch * 128 + 128],
                                   selband[:, ch * 64 + d * 16:ch * 64 + (d + 1) * 16],
                                   start=(ch == 0), stop=(ch == NCH - 1))
                return ins
            b.do("tensor", f, waits=("s_pd", "f_ps6", "s_sel"), inc="s_cmm")
            b.do("gpsimd", lambda g: g.engine_nop(), waits=("s_cmm",),
                 inc="f_pdf")

        def f(v):
            return v.tensor_tensor(csb[:, :], ps[6][:, 0:64], cval[:, :], ALU.mult)
        b.do("vector", f, waits=("s_cmm",), inc="s_csb")
        b.do("vector", lambda v: v.engine_nop(), inc="f_ps6")
        if cut(2):
            return

        if pending is not None:
            pending()

        def f(t):
            for ch in range(8):
                pb = ps[4] if ch < 4 else ps[6]
                col = (ch % 4) * 128
                ins = t.matmul(pb[:, col:col + 128],
                               t2ap(sqi_o + ch * 128, [[1, 128]]),
                               sstateb[:, par * 1024 + ch * 128:
                                       par * 1024 + ch * 128 + 128],
                               start=(ch % 4 == 0), stop=False)
            return ins
        b.do("tensor", f, waits=("s_sq1", "s_csb", "f_ps4", "f_ps6", "s_chain"),
             inc="s_intery")

        for d in range(4):
            vb = vsh if d % 2 == 0 else t2[:, 11264:12288]
            vbo = 14336 if d % 2 == 0 else 11264

            def f(g, d=d, vb=vb):
                return g.tensor_tensor(
                    vb.rearrange("p (a c) -> p a c", c=64),
                    vTc.rearrange("p (a c) -> p a c", c=64),
                    t2ap(15360 + d * 16, [[1, 16], [0, 64]]), ALU.mult)
            b.do("gpsimd", f, waits=("s_csb", "s_vT", f"f_vs{d % 2}"),
                 inc="s_vsh")

            def f(t, d=d, vb=vb):
                z = ident[:, :] if d == 0 else zsb[:, (d - 1) * 128:d * 128]
                for gs in range(2):
                    pb = ps[4] if gs == 0 else ps[6]
                    ins = t.matmul(pb[:, 0:512], z, vb[:, gs * 512:(gs + 1) * 512],
                                   start=False, stop=False)
                return ins
            b.do("tensor", f, waits=("s_vsh", "s_intery"), inc="s_bmm")
            b.do("gpsimd", lambda g: g.engine_nop(), waits=("s_bmm",),
                 inc=f"f_vs{d % 2}")
        if cut(3):
            return

        def f(t):
            ins = None
            for li, m in enumerate((8, 4)):
                for ch in range(8):
                    pb = ps[ch // 4]
                    col = (ch % 4) * 128
                    lh = t2ap(skL_o + SLOFF[m] + ch * 64, [[1, 64]])
                    rh = t2ap(LOFF2[m] + ch * 128, [[1, 128]])
                    ins = t.matmul(pb[li * 64:(li + 1) * 64, col:col + 128],
                                   lh, rh, start=True, stop=True,
                                   tile_position=(0, li * 64))
            return ins
        b.do("tensor", f, waits=("s_sq1", "s_sq2", "s_kT", "s_vT"), inc="s_stg")

        def f(v):
            ins = None
            for li, m in ((0, 8), (1, 4)):
                nq = 128 // (2 * m)
                for bk in range(2):
                    for half in range(2):
                        oap = bass.AP(wbig, (li * 64) * 73728 + WA + 30720
                                      + bk * 1024 + half * 128 + m,
                                      [[73728, 64], [256, 4], [2 * m, nq], [1, m]])
                        iap = bass.AP(ps[bk], (li * 64) * 512 + half * m,
                                      [[512, 64], [128, 4], [2 * m, nq], [1, m]])
                        map_ = bass.AP(stgm, (li * 64) * 64,
                                       [[64, 64], [0, 4], [m, nq], [1, m]])
                        ins = v.tensor_tensor(oap, iap, map_, ALU.mult)
            return ins
        b.do("vector", f, waits=("s_stg",), inc="s_stge")
        if cut(4):
            return

        def f(t):
            ins = None
            for bank in range(4):
                hhs = list(range(bank * 4, bank * 4 + 4))

                def scat_mm(hh, start, stop):
                    sg = stgb[:, hh * 128:(hh + 1) * 128]
                    hcol = (hh % 4) * 128
                    return t.matmul(ps[bank][:, hcol:hcol + 128], zscat[:, :], sg,
                                    start=start, stop=stop)

                def rects(hh, close):
                    ch, half = hh // 2, hh % 2
                    hcol = (hh % 4) * 128
                    for m in (64, 32):
                        nb = 64 // m
                        for qq in range(nb):
                            j0, i0 = qq * 2 * m, qq * 2 * m + m
                            lh = t2ap(skL_o + SLOFF[m] + ch * 64 + qq * m, [[1, m]])
                            rh = t2ap(LOFF2[m] + ch * 128 + qq * 2 * m + half * m,
                                      [[1, m]])
                            t.matmul(ps[bank][j0:j0 + m, hcol + i0:hcol + i0 + m],
                                     lh, rh, start=False, stop=False,
                                     tile_position=(0, j0))
                    for qq in range(3):
                        j0, i0 = qq * 32, qq * 32 + 16
                        lh = t2ap(skL_o + SLOFF[16] + ch * 96 + qq * 16, [[1, 16]])
                        rh = t2ap(LOFF2[16] + ch * 128 + qq * 32 + half * 16,
                                  [[1, 16]])
                        t.matmul(ps[bank][j0:j0 + 16, hcol + i0:hcol + i0 + 16],
                                 lh, rh, start=False, stop=False,
                                 tile_position=(0, j0))
                    lh = t2ap(skL_o + SLOFF[16] + ch * 96 + 48, [[1, 48]])
                    rh = t2ap(LOFF2[16] + ch * 128 + 96 + half * 16, [[1, 16]])
                    return t.matmul(ps[bank][64:112, hcol + 112:hcol + 128],
                                    lh, rh, start=False, stop=close,
                                    tile_position=(0, 64))

                ins = scat_mm(hhs[0], True, False)
                ins = scat_mm(hhs[1], False, False)
                ins = scat_mm(hhs[2], False, False)
                for hh in hhs[:3]:
                    ins = rects(hh, False)
                ins = scat_mm(hhs[3], False, False)
                ins = rects(hhs[3], False)
                ins = t.matmul(ps[bank][:, 0:1], t2[0:1, 15424:15552],
                               t2[0:1, 15424:15425], start=False, stop=True)
            return ins
        b.do("tensor", f, waits=("s_sq1", "s_sq2", "s_kT", "s_vT", "s_stge"),
             inc="s_scores")
        if cut(5):
            return

        def f(v):
            ins = None
            for bank in range(4):
                ins = v.tensor_tensor(
                    tri[:, bank * 512:(bank + 1) * 512].rearrange(
                        "p (a c) -> p a c", c=128),
                    ps[bank][:, 0:512].rearrange("p (a c) -> p a c", c=128),
                    bass.AP(trim, 0, [[128, 128], [0, 4], [1, 128]]), ALU.mult)
            return ins
        b.do("vector", f, waits=("s_scores", "f_tri", "s_dbg"), inc="s_tri")
        b.do("vector", lambda v: v.engine_nop(), inc="f_sc")
        b.do("vector", lambda v: v.engine_nop(), inc="f_sq")

        def f(t):
            ins = None
            for hh in range(16):
                pb = ps[4] if hh < 8 else ps[6]
                col = (hh % 8) * 64
                ins = t.matmul(pb[:, col:col + 64],
                               wbap(WA + 28672 + hh * 128, [[1, 128]]),
                               vTc[:, hh * 64:(hh + 1) * 64],
                               start=False, stop=(hh % 8 == 7))
            return ins
        b.do("tensor", f, waits=("s_tri", "s_bmm"), inc="s_apply")
        b.do("vector", lambda v: v.engine_nop(), waits=("s_apply",), inc="f_kvT")
        b.do("vector", lambda v: v.engine_nop(), waits=("s_apply",), inc="f_mixv")

        def f(v):
            v.tensor_copy(tri[:, 0:512], ps[4][:, 0:512])
            return v.tensor_copy(tri[:, 512:1024], ps[6][:, 0:512])
        b.do("vector", f, waits=("s_apply", "s_ytr"), inc="s_ytm")
        b.do("vector", lambda v: v.engine_nop(), inc="f_ps4")
        b.do("vector", lambda v: v.engine_nop(), inc="f_ps6")
        if cut(6):
            return

        if debug:
            t0d = (cc - 32) * L
            hdbg = b.pre("s_dbg", 48)
            def fd(g, t0d=t0d, hdbg=hdbg):
                ccc = cc - 32
                doff = (ccc // 4) * 128 * TH + (ccc % 4) * 1024
                g.dma_start(bass.AP(ddbg["dby"], doff, [[TH, 128], [1, 1024]]),
                            tri[:, 0:1024]).then_inc(hdbg, 16)
                g.dma_start(bass.AP(ddbg["dbkt"], doff, [[TH, 128], [1, 1024]]),
                            kTc[:, :]).then_inc(hdbg, 16)
                g.dma_start(bass.AP(ddbg["dbvt"], doff, [[TH, 128], [1, 1024]]),
                            vTc[:, :]).then_inc(hdbg, 16)
            wl1 = [(b.sems["s_ytm"][0], b.sems["s_ytm"][1]),
                   (b.sems["s_kT"][0], b.sems["s_kT"][1]),
                   (b.sems["s_vT"][0], b.sems["s_vT"][1])]
            def wdbg(g, wl1=wl1, fd=fd):
                for h, v in wl1:
                    g.wait_ge(h, v)
                fd(g)
            b.q["gpsimd"].append(wdbg)


        def tail():
            def f(a):
                return a.activation(gnsq[:, :], tri[:, 0:512], AF.Square)
            b.do("scalar", f, waits=("s_ytm", "s_gnr"), inc="s_ysq")

            def f(v):
                v.tensor_reduce(rowsG[:, 0:16].rearrange("p (a c) -> p a c", c=1),
                                wbap(WA + 28672, [[64, 16], [1, 64]]), AX.X, ALU.add)
                return v.tensor_reduce(rowsG[:, 16:24].rearrange("p (a c) -> p a c", c=1),
                                       bass.AP(gnsq, 0, [[512, 128], [64, 8], [1, 64]]),
                                       AX.X, ALU.add)
            b.do("vector", f, waits=("s_ysq",), inc="s_gnr")

            def f(a):
                return a.activation(gnsq[:, :], tri[:, 512:1024], AF.Square)
            b.do("scalar", f, waits=("s_ytm", "s_gnr"), inc="s_ysq")

            def f(v):
                return v.tensor_reduce(rowsG[:, 24:32].rearrange("p (a c) -> p a c", c=1),
                                       bass.AP(gnsq, 0, [[512, 128], [64, 8], [1, 64]]),
                                       AX.X, ALU.add)
            b.do("vector", f, waits=("s_ysq",), inc="s_gnr")

            def f(v):
                v.tensor_scalar_mul(rowsG[:, 32:48], rowsG[:, 0:16], 1.0 / SD)
                return v.tensor_scalar_mul(rowsG[:, 48:64], rowsG[:, 16:32], 1.0 / SD)
            b.do("vector", f, waits=("s_gnr",), inc="s_gnv")

            def f(a):
                return a.activation(rowsG[:, 64:80], rowsG[:, 32:48], AF.Square)
            b.do("scalar", f, waits=("s_gnv",), inc="s_gna")

            def f(v):
                v.tensor_sub(rowsG[:, 48:64], rowsG[:, 48:64], rowsG[:, 64:80])
                return v.engine_nop()
            b.do("vector", f, waits=("s_gna",), inc="s_gnv")

            def f(a):
                return a.activation(rowsG[:, 48:64], rowsG[:, 48:64], AF.Sqrt,
                                    bias=smf[:, 129:130])
            b.do("scalar", f, waits=("s_gnv", "s_zero"), inc="s_gna")

            pg = b.chainpt("c_dve")

            def f(v):
                v.reciprocal(rowsG[:, 48:64], rowsG[:, 48:64]).then_inc(pg[0], 1)
                v.wait_ge(pg[0], pg[1])
                return v.tensor_mul(rowsG[:, 32:48], rowsG[:, 32:48], rowsG[:, 48:64])
            b.do("vector", f, waits=("s_gna",), inc="s_gnv")
            if cut(7):
                return

            pga = b.chainpt("c_gp")

            def f(g):
                y3 = wbap(WA + 28672, [[64, 16], [1, 64]])
                g.tensor_tensor(y3, y3, bass.AP(rowsG, 48, [[80, 128], [1, 16], [0, 64]]),
                                ALU.mult).then_inc(pga[0], 1)
                g.wait_ge(pga[0], pga[1])
                return g.tensor_tensor(y3, y3,
                                       bass.AP(rowsG, 32, [[80, 128], [1, 16], [0, 64]]),
                                       ALU.subtract)
            b.do("gpsimd", f, waits=("s_gnv", "s_dbg"), inc="s_gnap")

            def f(t):
                ins = None
                for sl in range(8):
                    ins = t.transpose(pbt[:, sl * 128:(sl + 1) * 128],
                                      tri[:, sl * 128:(sl + 1) * 128], ident[:, :])
                return ins
            b.do("tensor", f, waits=("s_gnap", "f_pbt"), inc="s_ytr")

            def f(a):
                return a.activation(tri[:, 1024:2048], pbt[:, 0:1024], AF.Copy)
            b.do("scalar", f, waits=("s_ytr",), inc="s_yfm")
            b.do("scalar", lambda a: a.nop(), inc="f_pbt")

            def f(a):
                ins = None
                for ch in range(NCH):
                    sl = tri[:, 1024 + ch * 128:1024 + (ch + 1) * 128]
                    ins = a.activation(sl, sl, AF.Identity, bias=LNB[:, ch:ch + 1],
                                       scale=LNW[:, ch:ch + 1])
                return ins
            b.do("scalar", f, waits=("s_yfm",), inc="s_yga")

            def f(g):
                ins = None
                for ch in range(NCH):
                    sl = tri[:, 1024 + ch * 128:1024 + (ch + 1) * 128]
                    ins = g.tensor_tensor(
                        ygF[:, ch * 512 + c0:ch * 512 + c0 + 128], sl,
                        gF[:, ch * 512 + c0:ch * 512 + c0 + 128], ALU.mult)
                return ins
            b.do("gpsimd", f, waits=("s_yga", "s_pjd", "f_ygF"), inc="s_yg")
            b.do("gpsimd", lambda g: g.engine_nop(), inc="f_tri")

            if ci == CPT - 1:
                b.do("vector", lambda v: v.engine_nop(), waits=("s_yg",), inc="f_scr")

        return tail
    def oproj_tile(ti, ph):
        t0 = ti * TILE
        for co in range(NCH):
            bank = co % 2

            def f(t, co=co, bank=bank):
                for ci in range(NCH):
                    ins = t.matmul(ps[bank][:, 0:512],
                                   W_o[:, ci * 1024 + co * 128:ci * 1024 + co * 128 + 128],
                                   ygF[:, ci * 512:(ci + 1) * 512],
                                   start=(ci == 0), stop=(ci == NCH - 1))
                return ins
            b.do("tensor", f, waits=("s_yg", f"f_pj{bank}", "f_sc"), inc="s_mm")

            def f(v, co=co, bank=bank):
                return v.tensor_copy(attf[:, 0:512], ps[bank][:, 0:512])
            b.do("vector", f, waits=("s_mm", "f_at"), inc="s_ev")
            b.do("vector", lambda v: v.engine_nop(), inc=f"f_pj{bank}")

            def f(g, co=co, bank=bank):
                return g.dma_start(
                    bass.AP(d_x1, co * 128 * TH + t0, [[TH, 128], [1, 512]]),
                    attf[:, 0:512], accum_op=ALU.add)
            b.do("gpsimd", f, waits=("s_ev", "s_x1pre"), inc="s_x1acc", k=16)
            b.do("gpsimd", lambda g: g.engine_nop(), waits=("s_x1acc",),
                 inc="f_at")
        if debug:
            hdbg2 = b.pre("s_dbg", 64)
            wl2 = [(b.sems["s_yg"][0], b.sems["s_yg"][1]),
                   (b.sems["s_pjd"][0], b.sems["s_pjd"][1])]
            def fd(g, t0=t0, hdbg2=hdbg2, wl2=wl2):
                for h, v in wl2:
                    g.wait_ge(h, v)
                ap3 = [[TH, 128], [TH * 128, 8], [1, 512]]
                g.dma_start(bass.AP(ddbg["dbr"], t0, ap3),
                            rF.rearrange("p (a c) -> p a c", c=512)).then_inc(hdbg2, 16)
                g.dma_start(bass.AP(ddbg["dbk"], t0, ap3),
                            kF.rearrange("p (a c) -> p a c", c=512)).then_inc(hdbg2, 16)
                g.dma_start(bass.AP(ddbg["dbg"], t0, ap3),
                            gF.rearrange("p (a c) -> p a c", c=512)).then_inc(hdbg2, 16)
                g.dma_start(bass.AP(ddbg["dbyg"], t0, ap3),
                            ygF.rearrange("p (a c) -> p a c", c=512)).then_inc(hdbg2, 16)
            b.q["gpsimd"].append(fd)
        b.do("tensor", lambda t: t.nop(), waits=("s_mm",), inc="f_ygF")
        b.do("tensor", lambda t: t.nop(), waits=("s_mm",), inc="f_mix")

        def f(s):
            return s.dma_start(bass.AP(d_out, t0, [[TH, 128], [TH * 128, 8], [1, 512]]),
                               bass.AP(d_x1, t0, [[TH, 128], [TH * 128, 8], [1, 512]]))
        b.do("sync", f, waits=("s_x1acc",), inc="s_outpre", k=16)

    def ffn_tile(ti):
        t0 = ti * TILE
        ph = "fn"
        ln_xhat(ti, d_x1, ph, x1b, 0, hb[:, 0:4096], 4096, "t2")

        for co in range(24):
            bk = co % 2

            def f(t, co=co, bk=bk):
                for ci in range(NCH):
                    ins = t.matmul(ps[bk][:, 0:512],
                                   Fg[:, ci * FFH + co * 128:ci * FFH + co * 128 + 128],
                                   x1b[:, ci * 512:(ci + 1) * 512],
                                   start=(ci == 0), stop=(ci == NCH - 1))
                return ins
            b.do("tensor", f, waits=("s_xhat", f"f_fg{bk}", "s_fw",
                                     "f_sc", "f_pj0", "f_pj1"),
                 inc="s_mm")

            def f(a, co=co, bk=bk):
                return a.activation(hb[:, co * 512:(co + 1) * 512], ps[bk][:, 0:512],
                                    AF.Silu, scale=1.0 / 127.0)
            b.do("scalar", f, waits=("s_mm",), inc=f"f_fg{bk}")

            def f(t, co=co, bk=bk):
                for ci in range(NCH):
                    ins = t.matmul(ps[2 + bk][:, 0:512],
                                   Fu[:, ci * FFH + co * 128:ci * FFH + co * 128 + 128],
                                   x1b[:, ci * 512:(ci + 1) * 512],
                                   start=(ci == 0), stop=(ci == NCH - 1))
                return ins
            b.do("tensor", f, waits=("s_xhat", f"f_fu{bk}", "s_fw",
                                     "s_kT", "s_vT"),
                 inc="s_mm")

            def f(v, co=co, bk=bk):
                sl = hb[:, co * 512:(co + 1) * 512]
                return v.tensor_tensor(sl, sl, ps[2 + bk][:, 0:512], ALU.mult)
            b.do("vector", f, waits=("s_mm", f"f_fg{bk}"), inc=f"f_fu{bk}")
        b.do("vector", lambda v: v.engine_nop(), waits=(f"f_fu0", "f_fu1"),
             inc="s_h")

        for co in range(NCH):
            bk = 4 + co % 2

            def f(t, co=co, bk=bk):
                for ci in range(24):
                    ins = t.matmul(ps[bk][:, 0:512],
                                   Fd[:, ci * 1024 + co * 128:ci * 1024 + co * 128 + 128],
                                   hb[:, ci * 512:(ci + 1) * 512],
                                   start=(ci == 0), stop=(ci == 23))
                return ins
            b.do("tensor", f, waits=("s_h", f"f_fd{bk}", "f_ps4", "f_ps5"),
                 inc="s_mm")

            def f(v, co=co, bk=bk):
                return v.tensor_scalar_mul(attf[:, 0:512],
                                           ps[bk][:, 0:512], 1.0 / (127.0 * 127.0))
            b.do("vector", f, waits=("s_mm", "f_fat"), inc="s_ev")
            b.do("vector", lambda v: v.engine_nop(), inc=f"f_fd{bk}")

            def f(g, co=co, bk=bk):
                return g.dma_start(
                    bass.AP(d_out, co * 128 * TH + t0, [[TH, 128], [1, 512]]),
                    attf[:, 0:512], accum_op=ALU.add)
            b.do("gpsimd", f, waits=("s_ev", "s_outpre"), inc="s_outacc", k=16)
            b.do("gpsimd", lambda g: g.engine_nop(), waits=("s_outacc",),
                 inc="f_fat")
        b.do("vector", lambda v: v.engine_nop(), waits=("s_h",),
             inc="f_xh")
        b.do("vector", lambda v: v.engine_nop(), waits=("s_mm",),
             inc="f_scr")

    setup()
    b.do("gpsimd", lambda g: g.memset(onesb[:, :], 1.0), inc="s_onesb")

    cc = 0
    npre = NTIL if trunc != 1 else 0
    for ti in range(npre):
        ln_xhat(ti, d_px, "pre", xh, 0, rF, 0, "wb")

        def fbc(g, sl=ti % 2):
            return g.tensor_copy(bcolb[:, sl * 8:sl * 8 + 8], xh[:, 511::512])
        b.do("gpsimd", fbc, waits=("s_xhat",), inc="s_bcol")
        mix_full("pre", mixk, 1, "s_xhat", inc="s_mixk", bslot=(ti + 1) % 2,
                 extraw=("s_bcol",))
        mix_full("pre", mixr, 2, "s_xhat", inc="s_mixv", bslot=(ti + 1) % 2)
        b.do("gpsimd", lambda g: g.engine_nop(),
             waits=("s_mixk", "s_mixv"), inc="f_xh")
        for ci in range(CPT):
            scan_chunk(cc, ci, "pre", False)
            cc += 1
        b.do("tensor", lambda t: t.nop(), waits=("s_tmproj",), inc="f_mix")

    if trunc in (1, 2):
        nmain = 0
    elif 31 <= trunc <= 49:
        nmain = 1
    else:
        nmain = NTIL
    pend = None
    for ti in range(nmain):
        ln_xhat(ti, d_mx, "mn", xh, 0, rF, 0, "wb")

        def fbc(g, sl=ti % 2):
            return g.tensor_copy(bcolb[:, sl * 8:sl * 8 + 8], xh[:, 511::512])
        b.do("gpsimd", fbc, waits=("s_xhat",), inc="s_bcol")
        mix_full("mn", mixr, 0, "s_xhat", inc="s_mixr", bslot=(ti + 1) % 2,
                 extraw=("s_bcol",))
        mix_full("mn", mixg, 3, "s_xhat", inc="s_mixg", bslot=(ti + 1) % 2)
        mix_full("mn", mixk, 1, "s_xhat", inc="s_mixk", bslot=(ti + 1) % 2)
        if trunc == 31:
            break
        fm_proj("mn", "r", W_r, mixr, rF, AF.Copy, ("s_mixr", "f_sc"))
        fm_proj("mn", "g", W_g, mixg, gF, AF.Silu, ("s_mixg", "f_sc"))
        fm_proj("mn", "k", W_k, mixk, kF, AF.Copy, ("s_mixk", "f_sc"))
        mix_full("mn", mixr, 2, "s_xhat", inc="s_mixv", bslot=(ti + 1) % 2,
                 extraw=("s_pjd",))
        b.do("gpsimd", lambda g: g.engine_nop(),
             waits=("s_mixr", "s_mixg", "s_mixk", "s_mixv"), inc="f_xh")
        if trunc == 32:
            break
        nchunks = 1 if (trunc == 33 or 41 <= trunc <= 48) else CPT
        for ci in range(nchunks):
            pend = scan_chunk(cc, ci, "mn", True, pending=pend)
            cc += 1
        if trunc in (33, 34) or 41 <= trunc <= 48:
            break
        if pend is not None:
            pend()
            pend = None
        oproj_tile(ti, "mn")

    if trunc in (0, 4):
        fw_w = ("s_yg", "f_tri", "s_mm", "s_dbg")
        b.do("gpsimd", lambda g: g.dma_start(
            Fg.rearrange("p (a c) -> p a c", c=FFH),
            bass.AP(d_fg, 0, [[FFH, 128], [FFH * 128, 8], [1, FFH]])),
            waits=("s_tmproj", "s_pjd"), inc="s_fw", k=16)
        b.do("gpsimd", lambda g: g.dma_start(
            Fu.rearrange("p (a c) -> p a c", c=FFH),
            bass.AP(d_fu, 0, [[FFH, 128], [FFH * 128, 8], [1, FFH]])),
            waits=("s_pjd", "s_mm"), inc="s_fw", k=16)
        b.do("gpsimd", lambda g: g.dma_start(
            Fd.rearrange("p (a c) -> p a c", c=1024),
            bass.AP(d_fd, 0, [[1024, 128], [1024 * 128, 24], [1, 1024]])),
            waits=fw_w, inc="s_fw", k=16)
        b.do("vector", lambda v: v.engine_nop(), waits=("s_x1acc",), inc="f_xh")
        b.do("vector", lambda v: v.engine_nop(), waits=("s_x1acc",), inc="f_scr")

    nffn = NTIL if trunc == 0 else 0
    for ti in range(nffn):
        ffn_tile(ti)

    if trunc:
        b.waitonly("sync", *list(b.sems))
    else:
        b.waitonly("sync", "s_outacc", "s_outpre", "s_x1acc")

    with nc.Block() as blk:
        blk.sync(lambda e: [f(e) for f in b.q["sync"]] and None)
        blk.gpsimd(lambda e: [f(e) for f in b.q["gpsimd"]] and None)
        blk.tensor(lambda e: [f(e) for f in b.q["tensor"]] and None)
        blk.vector(lambda e: [f(e) for f in b.q["vector"]] and None)
        blk.scalar(lambda e: [f(e) for f in b.q["scalar"]] and None)
    st.close()
    return nc


def _consts():
    con = np.zeros((128, 1024), np.float32)
    con[:, 0:128] = np.eye(128, dtype=np.float32)
    tm = np.zeros((128, 128), np.float32)
    for m in LEVELS:
        for q in range(0, 128, 2 * m):
            tm[q:q + m, q + m:q + 2 * m] = 1.0
    con[:, 128:256] = tm
    cv = np.zeros((128, 64), np.float32)
    t = np.arange(128)
    for d in range(4):
        if d == 0:
            val = np.ones(128)
        else:
            val = ((t + d) <= 127) & (((t + d) % 4) >= d)
        cv[:, d * 16:(d + 1) * 16] = val.astype(np.float32)[:, None]
    con[:, 256:320] = cv
    for d in range(4):
        z = np.zeros((128, 128), np.float32)
        j = np.arange(128 - d)
        z[j, j + d] = 1.0
        con[:, 320 + d * 128:320 + (d + 1) * 128] = z
    zs = np.zeros((128, 128), np.float32)
    for li, m in enumerate((8, 4)):
        for p in range(64):
            q, jj = p // m, p % m
            zs[li * 64 + p, 2 * m * q + jj] = 1.0
    con[:, 832:960] = zs
    sm = np.zeros((128, 64), np.float32)
    for li, m in enumerate((8, 4)):
        for p in range(64):
            qj = p // m
            sm[li * 64 + p, qj * m:(qj + 1) * m] = 1.0
    con[:, 960:1024] = sm
    return con


_CACHE = {}


def kernel(**inputs):
    x0 = np.ascontiguousarray(np.asarray(inputs["L0_attn_xq"], np.float32))
    wT = {z: np.ascontiguousarray(np.asarray(inputs[f"L0_attn_W{z}"], np.float32).T)
          for z in ("r", "k", "v", "g", "o")}
    fgT = np.ascontiguousarray(np.asarray(inputs["L0_ffn_w_gate"], np.int32).T)
    fuT = np.ascontiguousarray(np.asarray(inputs["L0_ffn_w_up"], np.int32).T)
    fdT = np.ascontiguousarray(np.asarray(inputs["L0_ffn_w_down"], np.int32).T)
    mix = np.ascontiguousarray(np.stack([
        np.asarray(inputs["L0_attn_time_mix_r"], np.float32).ravel(),
        np.asarray(inputs["L0_attn_time_mix_k"], np.float32).ravel(),
        np.asarray(inputs["L0_attn_time_mix_v"], np.float32).ravel(),
        np.asarray(inputs["L0_attn_time_mix_g"], np.float32).ravel()]))
    dec = np.ascontiguousarray(np.asarray(inputs["L0_attn_time_decay"], np.float32).ravel())
    faa = np.ascontiguousarray(np.asarray(inputs["L0_attn_time_faaaa"], np.float32).ravel())
    lnw = np.ascontiguousarray(np.asarray(inputs["L0_attn_ln_w"], np.float32).ravel())
    lnb = np.ascontiguousarray(np.asarray(inputs["L0_attn_ln_b"], np.float32).ravel())
    con = _consts()

    if "nc" not in _CACHE:
        _CACHE["nc"] = build_program()
    nc = _CACHE["nc"]

    zero = np.zeros((C, TH), np.float32)
    in_maps = []
    for core in range(8):
        bb, j = core // 2, core % 2
        xb = x0[bb]
        px = zero if j == 0 else np.ascontiguousarray(xb[0:TH].T)
        mx = np.ascontiguousarray(xb[j * TH:(j + 1) * TH].T)
        in_maps.append(dict(px=px, mx=mx,
                            wrT=wT["r"], wkT=wT["k"], wvT=wT["v"], wgT=wT["g"],
                            woT=wT["o"], fgT=fgT, fuT=fuT, fdT=fdT, mix=mix,
                            dec=dec, faa=faa, lnw=lnw, lnb=lnb, consts=con))
    res = run_bass_kernel_spmd(nc, in_maps, core_ids=list(range(8)))
    out = np.empty((4, 8192, 1024), np.float32)
    for core in range(8):
        bb, j = core // 2, core % 2
        out[bb, j * TH:(j + 1) * TH] = res.results[core]["out"].T
    return out


if __name__ == "__main__":
    nc = build_program()
    print("built ok; instructions:", nc.n_instructions())


# version 12
# speedup vs baseline: 1.0054x; 1.0054x over previous
import numpy as np
import ml_dtypes
from contextlib import ExitStack

import concourse.bass as bass
import concourse.mybir as mybir
from concourse.bass_utils import run_bass_kernel_spmd

F32 = mybir.dt.float32
BF16 = mybir.dt.bfloat16
I32 = mybir.dt.int32
AF = mybir.ActivationFunctionType
ALU = mybir.AluOpType
AX = mybir.AxisListType
BF = ml_dtypes.bfloat16

C, H, SD, NCH = 1024, 16, 64, 8
TH, L, TILE = 4096, 128, 512
NTIL, CPT = TH // TILE, TILE // L
FFH = 3072
EPS = 1e-5
GN_EPS = 64.0 * 1e-5
LEVELS = (64, 32, 16, 8, 4)
LOFF = {64: 0, 32: 512, 16: 1024, 8: 1536, 4: 2048}
WA = 40960


class B:
    def __init__(self, nc, stack):
        self.nc, self.st = nc, stack
        self.sems = {}
        self.q = {e: [] for e in ("sync", "gpsimd", "tensor", "vector", "scalar")}

    def sem(self, name):
        if name not in self.sems:
            self.sems[name] = [self.st.enter_context(self.nc.semaphore(name)), 0]
        return self.sems[name]

    def do(self, eng, fn, waits=(), inc=None, k=1):
        wl = []
        for wn in waits:
            e = self.sem(wn)
            if e[1] > 0:
                wl.append((e[0], e[1]))
        hk = None
        if inc is not None:
            e = self.sem(inc)
            e[1] += k
            hk = (e[0], k)

        def run(eh):
            for h, v in wl:
                eh.wait_ge(h, v)
            ins = fn(eh)
            if hk is not None:
                ins.then_inc(hk[0], hk[1])
        self.q[eng].append(run)

    def chainpt(self, name):
        e = self.sem(name)
        e[1] += 1
        return (e[0], e[1])

    def pre(self, name, k):
        e = self.sem(name)
        e[1] += k
        return e[0]

    def waitonly(self, eng, *names):
        wl = []
        for wn in names:
            e = self.sem(wn)
            if e[1] > 0:
                wl.append((e[0], e[1]))

        def run(eh):
            for h, v in wl:
                eh.wait_ge(h, v)
        self.q[eng].append(run)


def build_program(debug=False, trunc=0):
    nc = bass.Bass("TRN2", target_bir_lowering=False)
    st = ExitStack()
    b = B(nc, st)

    d_px = nc.dram_tensor("px", [C, TH], F32, kind="ExternalInput")
    d_mx = nc.dram_tensor("mx", [C, TH], F32, kind="ExternalInput")
    dW = {z: nc.dram_tensor(z, [C, C], F32, kind="ExternalInput")
          for z in ("wrT", "wkT", "wvT", "wgT", "woT")}
    d_fg = nc.dram_tensor("fgT", [C, FFH], I32, kind="ExternalInput")
    d_fu = nc.dram_tensor("fuT", [C, FFH], I32, kind="ExternalInput")
    d_fd = nc.dram_tensor("fdT", [FFH, C], I32, kind="ExternalInput")
    d_mix = nc.dram_tensor("mix", [4, C], F32, kind="ExternalInput")
    d_dec = nc.dram_tensor("dec", [C], F32, kind="ExternalInput")
    d_faa = nc.dram_tensor("faa", [C], F32, kind="ExternalInput")
    d_lnw = nc.dram_tensor("lnw", [C], F32, kind="ExternalInput")
    d_lnb = nc.dram_tensor("lnb", [C], F32, kind="ExternalInput")
    d_con = nc.dram_tensor("consts", [128, 1024], F32, kind="ExternalInput")
    d_x1 = nc.dram_tensor("x1buf", [C, TH], F32)
    d_out = nc.dram_tensor("out", [C, TH], F32, kind="ExternalOutput")
    ddbg = {}
    if debug:
        for nm in ("dbr", "dbk", "dbg", "dby", "dbyg", "dbkt", "dbvt"):
            ddbg[nm] = nc.dram_tensor(nm, [C, TH], F32, kind="ExternalOutput")

    wbig = st.enter_context(nc.sbuf_tensor("wbig", [128, 73728], BF16))
    t2 = st.enter_context(nc.sbuf_tensor("t2", [128, 19200], BF16))
    smf = st.enter_context(nc.sbuf_tensor("smf", [128, 132], F32))
    rowsA = st.enter_context(nc.sbuf_tensor("rowsA", [128, 1280], F32))
    rowsG = st.enter_context(nc.sbuf_tensor("rowsG", [128, 80], F32))
    ident = st.enter_context(nc.sbuf_tensor("identb", [128, 128], BF16))
    zsb = st.enter_context(nc.sbuf_tensor("zsb", [128, 384], BF16))
    trim = st.enter_context(nc.sbuf_tensor("trim", [128, 128], BF16))
    cval = st.enter_context(nc.sbuf_tensor("cval", [128, 64], F32))
    stgm = st.enter_context(nc.sbuf_tensor("stgm", [128, 64], F32))
    selband = st.enter_context(nc.sbuf_tensor("selband", [128, 512], BF16))
    pqs = {m: st.enter_context(nc.sbuf_tensor(f"pq{m}", [128, 8 * m], BF16))
           for m in LEVELS}
    pks = {m: st.enter_context(nc.sbuf_tensor(f"pk{m}", [128, 8 * m], BF16))
           for m in LEVELS}
    pq_int = st.enter_context(nc.sbuf_tensor("pq_int", [128, 1024], BF16))
    pk_tm = st.enter_context(nc.sbuf_tensor("pk_tm", [128, 1024], BF16))
    sstate = st.enter_context(nc.sbuf_tensor("sstate", [128, 512], F32))
    sstateb = st.enter_context(nc.sbuf_tensor("sstateb", [128, 2048], BF16))
    bcolb = st.enter_context(nc.sbuf_tensor("bcolb", [128, 16], BF16))
    attf = st.enter_context(nc.sbuf_tensor("attf", [128, 512], F32))
    onesb = st.enter_context(nc.sbuf_tensor("onesb", [128, 1], BF16))
    zscat = st.enter_context(nc.sbuf_tensor("zscat", [128, 128], BF16))
    gnsq = st.enter_context(nc.sbuf_tensor("gnsq", [128, 512], BF16))
    ps = [st.enter_context(nc.psum_tensor(f"ps{i}", [128, 512], F32)) for i in range(7)]
    pbt = st.enter_context(nc.psum_tensor("pbt", [128, 1024], BF16))

    def wvv(off, n):
        return wbig[:, off:off + n]

    W_r, W_k, W_v, W_g, W_o = (wvv(i * 8192, 8192) for i in range(5))
    xh = wvv(WA, 4096)
    mixr = wvv(WA + 4096, 4096)
    mixg = wvv(WA + 8192, 4096)
    rF = wvv(WA + 12288, 4096)
    kF = wvv(WA + 16384, 4096)
    gF = wvv(WA + 20480, 4096)
    ygF = wvv(WA + 24576, 4096)
    tri = wvv(WA + 28672, 2048)
    stgb = wvv(WA + 30720, 2048)
    Fg, Fu, Fd = wvv(0, 24576), wvv(24576, 24576), wvv(49152, 24576)
    sqL_o, skL_o, sqi_o, pdb_o = 0, 16384, 5120, 6144
    LOFF2 = {64: 0, 32: 1024, 16: 2048, 8: 3072, 4: 4096}
    SLOFF = {64: 0, 32: 512, 16: 1024, 8: 1792, 4: 2304}
    mixk = t2[:, 7168:11264]
    mixvc = t2[:, 11264:12288]
    kTc = t2[:, 12288:13312]
    vTc = t2[:, 13312:14336]
    vsh = t2[:, 14336:15360]
    csb = t2[:, 15360:15424]
    sqi = t2[:, 5120:6144]
    pdb = t2[:, 6144:7168]
    x1b = t2[:, 0:4096]
    hb = t2[:, 4096:16384]

    M_R, M_K, M_V, M_G = (smf[:, 8 * i:8 * i + 8] for i in range(4))
    LNW, LNB = smf[:, 32:40], smf[:, 40:48]
    DEC, UU = smf[:, 48:56], smf[:, 56:64]
    LW, W1 = smf[:, 64:72], smf[:, 72:80]
    W2, W128 = smf[:, 80:88], smf[:, 88:96]
    M1 = {0: smf[:, 96:104], 1: smf[:, 104:112], 2: smf[:, 112:120], 3: smf[:, 120:128]}
    MM = {0: M_R, 1: M_K, 2: M_V, 3: M_G}

    def t2ap(off, dims, p0=0, np_=128):
        return bass.AP(t2, p0 * 19200 + off, [[19200, np_]] + dims)

    def wbap(off, dims, p0=0, np_=128):
        return bass.AP(wbig, p0 * 73728 + off, [[73728, np_]] + dims)

    def setup():
        def f(g):
            g.memset(t2[:, 15424:16384], 0)
            g.memset(t2[:, 0:5120], 0)
            g.memset(t2[:, 17408:18176], 0)
            g.memset(smf[:, 130:131], 1.0)
            g.memset(stgb[:, :], 0)
            g.memset(selband[:, :], 0)
            g.memset(pdb[:, :], 0)
            g.memset(csb[:, :], 0)
            g.memset(bcolb[:, :], 0)
            g.memset(rowsA[0:1, 1024:1152], 1.0)
            g.memset(smf[:, 128:129], EPS)
            g.memset(smf[:, 129:130], GN_EPS)
            g.memset(rowsA[0:1, 1152:1280], -1.0)
            return g.engine_nop()
        b.do("gpsimd", f, waits=("s_small",), inc="s_zero")

        def f(v):
            v.memset(sstate[:, :], 0)
            v.memset(sstateb[:, :], 0)
            for i in range(7):
                v.memset(ps[i][:, :], 0)
            return v.engine_nop()
        b.do("vector", f, inc="s_zero")

        h_small = b.pre("s_small", 16 * 11)

        def f(s):
            s.dma_start(cval[:, :], bass.AP(d_con, 256, [[1024, 128], [1, 64]])).then_inc(h_small, 16)
            s.dma_start(stgm[:, :], bass.AP(d_con, 960, [[1024, 128], [1, 64]])).then_inc(h_small, 16)
            for z in range(4):
                s.dma_start(smf[:, z * 8:z * 8 + 8],
                            bass.AP(d_mix, z * C, [[1, 128], [128, 8]]),
                            allow_slow_non_contiguous=True).then_inc(h_small, 16)
            s.dma_start(LNW, bass.AP(d_lnw, 0, [[1, 128], [128, 8]]), allow_slow_non_contiguous=True).then_inc(h_small, 16)
            s.dma_start(LNB, bass.AP(d_lnb, 0, [[1, 128], [128, 8]]), allow_slow_non_contiguous=True).then_inc(h_small, 16)
            s.dma_start(DEC, bass.AP(d_dec, 0, [[1, 128], [128, 8]]), allow_slow_non_contiguous=True).then_inc(h_small, 16)
            s.dma_start(UU, bass.AP(d_faa, 0, [[1, 128], [128, 8]]), allow_slow_non_contiguous=True).then_inc(h_small, 16)
            s.dma_start(rowsA[0:1, 0:1024], bass.AP(d_dec, 0, [[0, 1], [1, 1024]])).then_inc(h_small, 16)
            return None

        b.q["sync"].append(f)

        b.do("sync", lambda s: s.dma_start(d_x1.ap(), d_mx.ap()), inc="s_x1pre", k=16)

        h_wl = b.pre("s_wload", 16 * 9)

        def f(g):
            sems_w = b.sems["s_small"]
            g.wait_ge(sems_w[0], sems_w[1])
            g.dma_start(ident[:, :], bass.AP(d_con, 0, [[1024, 128], [1, 128]])).then_inc(h_wl, 16)
            g.dma_start(zsb[:, :].rearrange("p (a c) -> p a c", c=128),
                        bass.AP(d_con, 448, [[1024, 128], [128, 3], [1, 128]])).then_inc(h_wl, 16)
            g.dma_start(zscat[:, :], bass.AP(d_con, 832, [[1024, 128], [1, 128]])).then_inc(h_wl, 16)
            g.dma_start(trim[:, :], bass.AP(d_con, 128, [[1024, 128], [1, 128]])).then_inc(h_wl, 16)
            for dram, view in ((dW["wrT"], W_r), (dW["wkT"], W_k), (dW["wvT"], W_v),
                               (dW["wgT"], W_g), (dW["woT"], W_o)):
                g.dma_start(view.rearrange("p (a c) -> p a c", c=1024),
                            bass.AP(dram, 0, [[1024, 128], [1024 * 128, 8], [1, 1024]])).then_inc(h_wl, 16)
        b.q["gpsimd"].append(f)

        p3 = b.chainpt("c_act")

        def f(a):
            a.activation(rowsA[0:1, 0:1024], rowsA[0:1, 0:1024],
                         AF.Exp).then_inc(p3[0], 1)
            a.wait_ge(p3[0], p3[1])
            return a.mul(rowsA[0:1, 0:1024], rowsA[0:1, 0:1024], -1.0)
        b.do("scalar", f, waits=("s_small",), inc="s_lwrow")

        def f(g):
            return g.iota(rowsG[:, 64:65], [[0, 1]], base=127, channel_multiplier=-1,
                          allow_small_or_imprecise_dtypes=True)
        b.do("gpsimd", f, inc="s_riota")

        for piece in (1, 0):
            def fmm(t, piece=piece):
                return t.matmul(ps[6][:, 0:512], rowsA[0:1, 1024:1152],
                                rowsA[0:1, piece * 512:piece * 512 + 512],
                                start=True, stop=True)
            b.do("tensor", fmm, waits=("s_lwrow", "s_lwbc", "s_zero"),
                 inc="s_rbmm")

            def fcp(a, piece=piece):
                return a.activation(rowsA[:, 512:1024], ps[6][:, 0:512], AF.Copy)
            b.do("scalar", fcp, waits=("s_rbmm", "s_pktm"), inc="s_lwbc")

            def f(a, piece=piece):
                return a.activation(pk_tm[:, piece * 512:piece * 512 + 512],
                                    rowsA[:, 512:1024], AF.Exp,
                                    scale=rowsG[:, 64:65])
            b.do("scalar", f, waits=("s_lwbc", "s_riota"), inc="s_pktm")

        p1 = b.chainpt("c_act")
        p2 = b.chainpt("c_act")

        def f(a):
            a.activation(rowsA[:, 128:136], DEC, AF.Exp).then_inc(p1[0], 1)
            a.wait_ge(p1[0], p1[1])
            a.mul(LW, rowsA[:, 128:136], -1.0).then_inc(p2[0], 1)
            a.wait_ge(p2[0], p2[1])
            a.activation(W1, LW, AF.Exp)
            a.activation(W2, LW, AF.Exp, scale=2.0)
            return a.activation(W128, LW, AF.Exp, scale=128.0)
        b.do("scalar", f, waits=("s_small", "s_pktm"), inc="s_lw")

        def f(v):
            for i in range(4):
                v.tensor_scalar(M1[i], MM[i], -1.0, 1.0, ALU.mult, ALU.add)
            return v.engine_nop()
        b.do("vector", f, waits=("s_small",), inc="s_m1")

        ROFF = {64: 0, 32: 64, 16: 96, 8: 112, 4: 120}

        pm1 = b.chainpt("c_gp")

        def f(g):
            g.memset(rowsA[0:1, 128:256], 0).then_inc(pm1[0], 1)
            g.wait_ge(pm1[0], pm1[1])
            for m in LEVELS:
                g.iota(rowsA[0:1, 128 + ROFF[m]:128 + ROFF[m] + m], [[0, 1], [1, m]],
                       channel_multiplier=0, allow_small_or_imprecise_dtypes=True)
            return g.engine_nop()
        b.do("gpsimd", f, waits=("s_small", "s_lw", "s_pktm"), inc="s_ramp")

        def fmm(t):
            return t.matmul(ps[6][:, 0:128], rowsA[0:1, 1024:1152],
                            rowsA[0:1, 128:256], start=True, stop=True)
        b.do("tensor", fmm, waits=("s_ramp", "s_zero", "s_lwbc"), inc="s_rbmm")

        def fcp(a):
            return a.activation(rowsA[:, 256:384], ps[6][:, 0:128], AF.Copy)
        b.do("scalar", fcp, waits=("s_rbmm", "s_pktm"), inc="s_rbc")

        def f(a):
            for m in LEVELS:
                for ch in range(NCH):
                    ins = a.activation(pqs[m][:, ch * m:(ch + 1) * m],
                                       rowsA[:, 256 + ROFF[m]:256 + ROFF[m] + m],
                                       AF.Exp, scale=LW[:, ch:ch + 1])
            return ins
        b.do("scalar", f, waits=("s_rbc", "s_lw"), inc="s_pow")

        pm2 = b.chainpt("c_gp")

        def f(g):
            g.memset(rowsA[0:1, 128:256], 0).then_inc(pm2[0], 1)
            g.wait_ge(pm2[0], pm2[1])
            for m in LEVELS:
                g.iota(rowsA[0:1, 128 + ROFF[m]:128 + ROFF[m] + m], [[0, 1], [-1, m]],
                       base=m - 1, channel_multiplier=0,
                       allow_small_or_imprecise_dtypes=True)
            return g.engine_nop()
        b.do("gpsimd", f, waits=("s_pow",), inc="s_ramp")

        def fmm(t):
            return t.matmul(ps[6][:, 0:128], rowsA[0:1, 1024:1152],
                            rowsA[0:1, 128:256], start=True, stop=True)
        b.do("tensor", fmm, waits=("s_ramp", "s_pow"), inc="s_rbmm")

        def fcp(a):
            return a.activation(rowsA[:, 256:384], ps[6][:, 0:128], AF.Copy)
        b.do("scalar", fcp, waits=("s_rbmm", "s_pow"), inc="s_rbc")

        def f(a):
            for m in LEVELS:
                for ch in range(NCH):
                    ins = a.activation(pks[m][:, ch * m:(ch + 1) * m],
                                       rowsA[:, 256 + ROFF[m]:256 + ROFF[m] + m],
                                       AF.Exp, scale=LW[:, ch:ch + 1])
            return ins
        b.do("scalar", f, waits=("s_rbc",), inc="s_pow")

        def f(g):
            return g.iota(rowsA[0:1, 128:256], [[0, 1], [1, 128]],
                          channel_multiplier=0, allow_small_or_imprecise_dtypes=True)
        b.do("gpsimd", f, waits=("s_pow",), inc="s_ramp")

        def fmm(t):
            return t.matmul(ps[6][:, 0:128], rowsA[0:1, 1024:1152],
                            rowsA[0:1, 128:256], start=True, stop=True)
        b.do("tensor", fmm, waits=("s_ramp", "s_pow"), inc="s_rbmm")

        def fcp(a):
            return a.activation(rowsA[:, 256:384], ps[6][:, 0:128], AF.Copy)
        b.do("scalar", fcp, waits=("s_rbmm", "s_pow"), inc="s_rbc")

        def f(a):
            for ch in range(NCH):
                ins = a.activation(pq_int[:, ch * 128:(ch + 1) * 128],
                                   rowsA[:, 256:384], AF.Exp,
                                   scale=LW[:, ch:ch + 1])
            return ins
        b.do("scalar", f, waits=("s_rbc",), inc="s_pow")

        b.do("vector", lambda v: v.engine_nop(), waits=("s_pow",), inc="f_ps6")

        def f(v):
            for ch in range(NCH):
                for d in range(4):
                    base = ch * 64 + d * 16
                    for half in range(2):
                        hh = 2 * ch + half
                        o = selband[64 * half:64 * half + 64, base + hh:base + hh + 1]
                        if d == 0:
                            v.tensor_copy(o, UU[64 * half:64 * half + 64, ch:ch + 1])
                        elif d == 1:
                            v.tensor_copy(o, smf[64 * half:64 * half + 64, 130:131])
                        elif d == 2:
                            v.tensor_copy(o, W1[64 * half:64 * half + 64, ch:ch + 1])
                        else:
                            v.tensor_copy(o, W2[64 * half:64 * half + 64, ch:ch + 1])
            return v.engine_nop()
        b.do("vector", f, waits=("s_zero", "s_lw"), inc="s_sel")

    def ln_xhat(ti, d_src, ph, dst, dst_o, scratch, scratch_o, arena):
        t0 = ti * TILE

        def f(g):
            return g.dma_start(
                dst.rearrange("p (a c) -> p a c", c=512),
                bass.AP(d_src, t0, [[TH, 128], [TH * 128, 8], [1, 512]]))
        b.do("gpsimd", f, waits=("f_xh",), inc="s_xl", k=16)

        if ph == "fn":
            for ch in range(NCH):
                def f(a, ch=ch):
                    return a.activation(gnsq[:, :], dst[:, ch * 512:(ch + 1) * 512],
                                        AF.Square)
                b.do("scalar", f, waits=("s_xl", "s_fsqm", "s_gnr"), inc="s_fsq")

                def f(t, ch=ch):
                    return t.matmul(ps[6][32:33, 0:512], onesb[:, 0:1], gnsq[:, :],
                                    start=(ch == 0), stop=(ch == NCH - 1),
                                    tile_position=(0, 32))
                b.do("tensor", f, waits=("s_fsq", "f_ps6" if ch == 0 else "s_fsq",),
                     inc="s_fsqm")
        else:
            def f(a):
                return a.activation(scratch[:, :], dst[:, :], AF.Square)
            b.do("scalar", f, waits=("s_xl", "f_scr", "s_dbg"), inc="s_xsq")

        if ph == "pre":
            stat, ab, bb = ps[6], ps[4], ps[6]
            w_stat = ("s_xsq", "f_ps6", "s_onesb")
            w_musq = ("s_lrv", "f_ps4")
            w_bc = ("s_lrv", "f_ps4", "f_ps6")
            rel = ("f_ps4", "f_ps6")
        elif ph == "mn":
            stat, ab, bb = ps[2], ps[3], ps[2]
            w_stat = ("s_xsq", "s_vT", "s_onesb")
            w_musq = ("s_lrv", "s_vT")
            w_bc = ("s_lrv",)
            rel = ()
        else:
            stat, ab, bb = ps[6], ps[5], ps[4]
            w_stat = ("f_ps6", "s_onesb")
            w_musq = ("s_lrv", "f_ps5")
            w_bc = ("s_lrv", "f_ps4", "f_ps5")
            rel = ("f_ps4", "f_ps5", "f_ps6")

        def f(t):
            for ch in range(NCH):
                ins = t.matmul(stat[0:1, 0:512], onesb[:, 0:1],
                               dst[:, ch * 512:(ch + 1) * 512],
                               start=(ch == 0), stop=(ch == NCH - 1))
            if ph != "fn":
                for ch in range(NCH):
                    ins = t.matmul(stat[32:33, 0:512], onesb[:, 0:1],
                                   scratch[:, ch * 512:(ch + 1) * 512],
                                   start=(ch == 0), stop=(ch == NCH - 1),
                                   tile_position=(0, 32))
            return ins
        b.do("tensor", f, waits=w_stat + (("s_fsqm",) if ph == "fn" else ()),
             inc="s_lnmm")

        def f(v):
            v.tensor_scalar_mul(rowsA[0:1, 0:512], stat[0:1, 0:512], 1.0 / C)
            return v.tensor_scalar_mul(rowsA[0:1, 512:1024], stat[32:33, 0:512], 1.0 / C)
        b.do("vector", f, waits=("s_lnmm", "s_pktm"), inc="s_lrv")
        if ph == "fn":
            b.do("vector", lambda v: v.engine_nop(), inc="f_ps6")

        def f(a):
            return a.activation(ab[0:1, 0:512], rowsA[0:1, 0:512], AF.Square)
        b.do("scalar", f, waits=w_musq, inc="s_lra")

        def f(v):
            v.tensor_sub(rowsA[0:1, 512:1024], rowsA[0:1, 512:1024], ab[0:1, 0:512])
            return v.engine_nop()
        b.do("vector", f, waits=("s_lra",), inc="s_lrv")

        pca = b.chainpt("c_act")

        def f(a):
            a.activation(rowsA[0:1, 512:1024], rowsA[0:1, 512:1024],
                         AF.Ln, bias=smf[0:1, 128:129]).then_inc(pca[0], 1)
            a.wait_ge(pca[0], pca[1])
            return a.activation(rowsA[0:1, 512:1024], rowsA[0:1, 512:1024],
                                AF.Exp, scale=-0.5)
        b.do("scalar", f, waits=("s_lrv",), inc="s_lra")

        def f(v):
            return v.tensor_mul(rowsA[0:1, 0:512], rowsA[0:1, 0:512],
                                rowsA[0:1, 512:1024])
        b.do("vector", f, waits=("s_lra",), inc="s_lrv")

        def f(t):
            t.matmul(ab[:, 0:512], rowsA[0:1, 1024:1152], rowsA[0:1, 512:1024],
                     start=True, stop=True)
            return t.matmul(bb[:, 0:512], rowsA[0:1, 1152:1280], rowsA[0:1, 0:512],
                            start=True, stop=True)
        b.do("tensor", f, waits=w_bc, inc="s_lnbc")

        pch = b.chainpt("c_dve")

        def f(v):
            a_bc = bass.AP(ab, 0, [[512, 128], [0, 8], [1, 512]])
            b_bc = bass.AP(bb, 0, [[512, 128], [0, 8], [1, 512]])
            d3 = dst.rearrange("p (a c) -> p a c", c=512)
            v.tensor_tensor(d3, d3, a_bc, ALU.mult).then_inc(pch[0], 1)
            v.wait_ge(pch[0], pch[1])
            return v.tensor_tensor(d3, d3, b_bc, ALU.add)
        b.do("vector", f, waits=("s_lnbc",), inc="s_xhat")
        for rr in rel:
            b.do("vector", lambda v: v.engine_nop(), waits=("s_xhat",), inc=rr)

    def mix_full(ph, out, mi, waitx, inc="s_mixf", extraw=(), bslot=0):
        def fa(a):
            o3 = out.rearrange("p (a c) -> p a c", c=512)
            x3 = xh.rearrange("p (a c) -> p a c", c=512)
            for ch in range(NCH):
                ins = a.activation(o3[:, ch:ch + 1, :], x3[:, ch:ch + 1, :],
                                   AF.Copy, scale=MM[mi][:, ch:ch + 1])
            return ins
        b.do("scalar", fa, waits=(waitx, "s_m1", "s_tmproj") + extraw, inc="s_mixa")

        def fv(v):
            o3 = out.rearrange("p (a c) -> p a c", c=512)
            x3 = xh.rearrange("p (a c) -> p a c", c=512)
            for ch in range(NCH):
                v.scalar_tensor_tensor(o3[:, ch:ch + 1, 1:512], x3[:, ch:ch + 1, 0:511],
                                       M1[mi][:, ch:ch + 1], o3[:, ch:ch + 1, 1:512],
                                       ALU.mult, ALU.add)
                ins = v.scalar_tensor_tensor(
                    o3[:, ch:ch + 1, 0:1],
                    bcolb[:, bslot * 8 + ch:bslot * 8 + ch + 1].unsqueeze(2),
                    M1[mi][:, ch:ch + 1], o3[:, ch:ch + 1, 0:1], ALU.mult, ALU.add)
            return ins
        b.do("vector", fv, waits=("s_mixa",), inc=inc)

    def fm_proj(ph, nm, W, src, dst, act, waits):
        for co in range(NCH):
            bank = co % 2

            def f(t, co=co, bank=bank):
                for ci in range(NCH):
                    ins = t.matmul(ps[bank][:, 0:512],
                                   W[:, ci * 1024 + co * 128:ci * 1024 + co * 128 + 128],
                                   src[:, ci * 512:(ci + 1) * 512],
                                   start=(ci == 0), stop=(ci == NCH - 1))
                return ins
            b.do("tensor", f, waits=waits + (f"f_pj{bank}",), inc="s_mm")

            def f(a, co=co, bank=bank):
                return a.activation(dst[:, co * 512:(co + 1) * 512],
                                    ps[bank][:, 0:512], act)
            b.do("scalar", f, waits=("s_mm",), inc=f"f_pj{bank}")
        b.do("tensor", lambda t: t.nop(), waits=("f_pj0", "f_pj1"),
             inc="s_pjd")

    def scan_chunk(cc, ci, ph, full, pending=None):
        def cut(n):
            return full and trunc == 40 + n
        c0 = ci * 128
        par = cc % 2

        def f(t):
            for cin in range(NCH):
                stk = mixk[:, cin * 512 + c0:cin * 512 + c0 + 128]
                for gs in range(2):
                    ins = t.matmul(ps[gs][:, 0:512], stk,
                                   W_k[:, cin * 1024 + gs * 512:cin * 1024 + gs * 512 + 512],
                                   start=(cin == 0), stop=(cin == NCH - 1))
            return ins
        b.do("tensor", f, waits=("s_mixk", "f_sc", "f_pj0", "f_pj1", "s_wload"),
             inc="s_tmk")

        def f(t):
            for cin in range(NCH):
                stv = mixr[:, cin * 512 + c0:cin * 512 + c0 + 128]
                for gs in range(2):
                    ins = t.matmul(ps[2 + gs][:, 0:512], stv,
                                   W_v[:, cin * 1024 + gs * 512:cin * 1024 + gs * 512 + 512],
                                   start=(cin == 0), stop=(cin == NCH - 1))
            return ins
        b.do("tensor", f, waits=("s_mixv", "s_tmk"), inc="s_tmproj")

        def f(v):
            v.tensor_tensor(kTc[:, 0:512], ps[0][:, 0:512], pk_tm[:, 0:512], ALU.mult)
            return v.tensor_tensor(kTc[:, 512:1024], ps[1][:, 0:512],
                                   pk_tm[:, 512:1024], ALU.mult)
        b.do("vector", f, waits=("s_tmk", "f_kvT", "s_pktm", "s_dbg"), inc="s_kT")

        def f(a):
            a.activation(vTc[:, 0:512], ps[2][:, 0:512], AF.Copy)
            return a.activation(vTc[:, 512:1024], ps[3][:, 0:512], AF.Copy)
        b.do("scalar", f, waits=("s_tmproj", "f_kvT", "s_dbg"), inc="s_vT")

        def f(t):
            ins = None
            for p in range(8):
                for half in range(2):
                    hh = 2 * p + half
                    ins = t.matmul(
                        ps[5][64 * half:64 * half + 64, p * 64:(p + 1) * 64],
                        kTc[:, hh * 64:(hh + 1) * 64], vTc[:, hh * 64:(hh + 1) * 64],
                        start=True, stop=True,
                        tile_position=(0, 64 * half))
            return ins
        b.do("tensor", f, waits=("s_kT", "s_vT", "f_ps5"), inc="s_umm")

        mirror = full or cc == 31
        pch2 = b.chainpt("c_dve") if mirror else None

        def f(v):
            for p in range(8):
                ins = v.scalar_tensor_tensor(
                    sstate[:, p * 64:(p + 1) * 64], sstate[:, p * 64:(p + 1) * 64],
                    W128[:, p:p + 1], ps[5][:, p * 64:(p + 1) * 64], ALU.mult, ALU.add)
            if mirror:
                ins.then_inc(pch2[0], 1)
                v.wait_ge(pch2[0], pch2[1])
                npo = (1 - par) * 1024
                v.tensor_copy(bass.AP(sstateb, npo, [[2048, 64], [128, 8], [1, 64]]),
                              bass.AP(sstate, 0, [[512, 64], [64, 8], [1, 64]]))
                ins = v.tensor_copy(
                    bass.AP(sstateb, 64 * 2048 + npo + 64,
                            [[2048, 64], [128, 8], [1, 64]]),
                    bass.AP(sstate, 64 * 512, [[512, 64], [64, 8], [1, 64]]))
            return ins
        b.do("vector", f, waits=("s_umm", "s_intery"), inc="s_chain")
        b.do("vector", lambda v: v.engine_nop(), inc="f_ps5")

        if not full:
            b.do("scalar", lambda a: a.nop(), waits=("s_umm",), inc="f_kvT")
            b.do("scalar", lambda a: a.nop(), waits=("s_kT", "s_vT"), inc="f_sc")
            if ci == CPT - 1:
                b.do("vector", lambda v: v.engine_nop(), waits=("s_chain",),
                     inc="f_scr")
            return

        def f(g):
            g.tensor_tensor(t2ap(sqi_o, [[128, 8], [1, 128]]),
                            wbap(WA + 12288 + c0, [[512, 8], [1, 128]]),
                            bass.AP(pq_int, 0, [[1024, 128], [128, 8], [1, 128]]),
                            ALU.mult)
            for m in (64, 32):
                nb = 64 // m
                for half in range(2):
                    g.tensor_tensor(
                        t2ap(LOFF2[m] + half * m, [[128, 8], [2 * m, nb], [1, m]],
                             p0=64 * half, np_=64),
                        wbap(WA + 12288 + c0 + m, [[512, 8], [2 * m, nb], [1, m]],
                             p0=64 * half, np_=64),
                        bass.AP(pqs[m], 64 * half * 8 * m,
                                [[8 * m, 64], [m, 8], [0, nb], [1, m]]),
                        ALU.mult)
            return g.engine_nop()
        b.do("gpsimd", f, waits=("s_pjd", "f_sq", "s_pow", "s_zero"), inc="s_sq1")

        def f(v):
            for m in (16, 8, 4):
                nb = 64 // m
                for half in range(2):
                    v.tensor_tensor(
                        t2ap(LOFF2[m] + half * m, [[128, 8], [2 * m, nb], [1, m]],
                             p0=64 * half, np_=64),
                        wbap(WA + 12288 + c0 + m, [[512, 8], [2 * m, nb], [1, m]],
                             p0=64 * half, np_=64),
                        bass.AP(pqs[m], 64 * half * 8 * m,
                                [[8 * m, 64], [m, 8], [0, nb], [1, m]]),
                        ALU.mult)
            for m in (64, 32, 8, 4):
                nb = 64 // m
                v.tensor_tensor(
                    t2ap(skL_o + SLOFF[m], [[64, 8], [m, nb], [1, m]]),
                    wbap(WA + 16384 + c0, [[512, 8], [2 * m, nb], [1, m]]),
                    bass.AP(pks[m], 0, [[8 * m, 128], [m, 8], [0, nb], [1, m]]),
                    ALU.mult)
            v.tensor_tensor(
                t2ap(skL_o + SLOFF[16], [[96, 8], [16, 3], [1, 16]]),
                wbap(WA + 16384 + c0, [[512, 8], [32, 3], [1, 16]]),
                bass.AP(pks[16], 0, [[128, 128], [16, 8], [0, 3], [1, 16]]),
                ALU.mult)
            v.tensor_tensor(
                t2ap(skL_o + SLOFF[16] + 80, [[96, 8], [1, 16]]),
                wbap(WA + 16384 + c0 + 96, [[512, 8], [1, 16]]),
                bass.AP(pks[16], 0, [[128, 128], [16, 8], [1, 16]]),
                ALU.mult)
            return v.engine_nop()
        b.do("vector", f, waits=("s_pjd", "f_sq", "s_pow", "s_zero"), inc="s_sq2")
        if cut(1):
            return

        for d in range(4):
            def f(g, d=d):
                r3 = rF.rearrange("p (a c) -> p a c", c=512)
                k3 = kF.rearrange("p (a c) -> p a c", c=512)
                return g.tensor_tensor(
                    pdb.rearrange("p (a c) -> p a c", c=128)[:, :, d:128],
                    r3[:, :, c0 + d:c0 + 128], k3[:, :, c0:c0 + 128 - d], ALU.mult)
            b.do("gpsimd", f, waits=("s_pjd", "s_pjd",
                                     "f_pdf"), inc="s_pd")

            def f(t, d=d):
                for ch in range(NCH):
                    ins = t.matmul(ps[6][0:128 - d, d * 16:(d + 1) * 16],
                                   pdb[:, ch * 128 + d:ch * 128 + 128],
                                   selband[:, ch * 64 + d * 16:ch * 64 + (d + 1) * 16],
                                   start=(ch == 0), stop=(ch == NCH - 1))
                return ins
            b.do("tensor", f, waits=("s_pd", "f_ps6", "s_sel"), inc="s_cmm")
            b.do("gpsimd", lambda g: g.engine_nop(), waits=("s_cmm",),
                 inc="f_pdf")

        def f(v):
            return v.tensor_tensor(csb[:, :], ps[6][:, 0:64], cval[:, :], ALU.mult)
        b.do("vector", f, waits=("s_cmm",), inc="s_csb")
        b.do("vector", lambda v: v.engine_nop(), inc="f_ps6")
        if cut(2):
            return

        if pending is not None:
            pending()

        def f(t):
            for ch in range(8):
                pb = ps[4] if ch < 4 else ps[6]
                col = (ch % 4) * 128
                ins = t.matmul(pb[:, col:col + 128],
                               t2ap(sqi_o + ch * 128, [[1, 128]]),
                               sstateb[:, par * 1024 + ch * 128:
                                       par * 1024 + ch * 128 + 128],
                               start=(ch % 4 == 0), stop=False)
            return ins
        b.do("tensor", f, waits=("s_sq1", "s_csb", "f_ps4", "f_ps6", "s_chain"),
             inc="s_intery")

        for d in range(4):
            vb = vsh if d % 2 == 0 else t2[:, 11264:12288]
            vbo = 14336 if d % 2 == 0 else 11264

            def f(g, d=d, vb=vb):
                return g.tensor_tensor(
                    vb.rearrange("p (a c) -> p a c", c=64),
                    vTc.rearrange("p (a c) -> p a c", c=64),
                    t2ap(15360 + d * 16, [[1, 16], [0, 64]]), ALU.mult)
            b.do("gpsimd", f, waits=("s_csb", "s_vT", f"f_vs{d % 2}"),
                 inc="s_vsh")

            def f(t, d=d, vb=vb):
                z = ident[:, :] if d == 0 else zsb[:, (d - 1) * 128:d * 128]
                for gs in range(2):
                    pb = ps[4] if gs == 0 else ps[6]
                    ins = t.matmul(pb[:, 0:512], z, vb[:, gs * 512:(gs + 1) * 512],
                                   start=False, stop=False)
                return ins
            b.do("tensor", f, waits=("s_vsh", "s_intery"), inc="s_bmm")
            b.do("gpsimd", lambda g: g.engine_nop(), waits=("s_bmm",),
                 inc=f"f_vs{d % 2}")
        if cut(3):
            return

        def f(t):
            ins = None
            for li, m in enumerate((8, 4)):
                for ch in range(8):
                    pb = ps[ch // 4]
                    col = (ch % 4) * 128
                    lh = t2ap(skL_o + SLOFF[m] + ch * 64, [[1, 64]])
                    rh = t2ap(LOFF2[m] + ch * 128, [[1, 128]])
                    ins = t.matmul(pb[li * 64:(li + 1) * 64, col:col + 128],
                                   lh, rh, start=True, stop=True,
                                   tile_position=(0, li * 64))
            return ins
        b.do("tensor", f, waits=("s_sq1", "s_sq2", "s_kT", "s_vT"), inc="s_stg")

        def f(v):
            ins = None
            for li, m in ((0, 8), (1, 4)):
                nq = 128 // (2 * m)
                for bk in range(2):
                    for half in range(2):
                        oap = bass.AP(wbig, (li * 64) * 73728 + WA + 30720
                                      + bk * 1024 + half * 128 + m,
                                      [[73728, 64], [256, 4], [2 * m, nq], [1, m]])
                        iap = bass.AP(ps[bk], (li * 64) * 512 + half * m,
                                      [[512, 64], [128, 4], [2 * m, nq], [1, m]])
                        map_ = bass.AP(stgm, (li * 64) * 64,
                                       [[64, 64], [0, 4], [m, nq], [1, m]])
                        ins = v.tensor_tensor(oap, iap, map_, ALU.mult)
            return ins
        b.do("vector", f, waits=("s_stg",), inc="s_stge")
        if cut(4):
            return

        def f(t):
            ins = None
            for bank in range(4):
                hhs = list(range(bank * 4, bank * 4 + 4))

                def scat_mm(hh, start, stop):
                    sg = stgb[:, hh * 128:(hh + 1) * 128]
                    hcol = (hh % 4) * 128
                    return t.matmul(ps[bank][:, hcol:hcol + 128], zscat[:, :], sg,
                                    start=start, stop=stop)

                def rects(hh, close):
                    ch, half = hh // 2, hh % 2
                    hcol = (hh % 4) * 128
                    for m in (64, 32):
                        nb = 64 // m
                        for qq in range(nb):
                            j0, i0 = qq * 2 * m, qq * 2 * m + m
                            lh = t2ap(skL_o + SLOFF[m] + ch * 64 + qq * m, [[1, m]])
                            rh = t2ap(LOFF2[m] + ch * 128 + qq * 2 * m + half * m,
                                      [[1, m]])
                            t.matmul(ps[bank][j0:j0 + m, hcol + i0:hcol + i0 + m],
                                     lh, rh, start=False, stop=False,
                                     tile_position=(0, j0))
                    for qq in range(3):
                        j0, i0 = qq * 32, qq * 32 + 16
                        lh = t2ap(skL_o + SLOFF[16] + ch * 96 + qq * 16, [[1, 16]])
                        rh = t2ap(LOFF2[16] + ch * 128 + qq * 32 + half * 16,
                                  [[1, 16]])
                        t.matmul(ps[bank][j0:j0 + 16, hcol + i0:hcol + i0 + 16],
                                 lh, rh, start=False, stop=False,
                                 tile_position=(0, j0))
                    lh = t2ap(skL_o + SLOFF[16] + ch * 96 + 48, [[1, 48]])
                    rh = t2ap(LOFF2[16] + ch * 128 + 96 + half * 16, [[1, 16]])
                    return t.matmul(ps[bank][64:112, hcol + 112:hcol + 128],
                                    lh, rh, start=False, stop=close,
                                    tile_position=(0, 64))

                ins = scat_mm(hhs[0], True, False)
                ins = scat_mm(hhs[1], False, False)
                ins = scat_mm(hhs[2], False, False)
                for hh in hhs[:3]:
                    ins = rects(hh, False)
                ins = scat_mm(hhs[3], False, False)
                ins = rects(hhs[3], False)
                ins = t.matmul(ps[bank][:, 0:1], t2[0:1, 15424:15552],
                               t2[0:1, 15424:15425], start=False, stop=True)
            return ins
        b.do("tensor", f, waits=("s_sq1", "s_sq2", "s_kT", "s_vT", "s_stge"),
             inc="s_scores")
        if cut(5):
            return

        def f(v):
            ins = None
            for bank in range(4):
                ins = v.tensor_tensor(
                    tri[:, bank * 512:(bank + 1) * 512].rearrange(
                        "p (a c) -> p a c", c=128),
                    ps[bank][:, 0:512].rearrange("p (a c) -> p a c", c=128),
                    bass.AP(trim, 0, [[128, 128], [0, 4], [1, 128]]), ALU.mult)
            return ins
        b.do("vector", f, waits=("s_scores", "f_tri", "s_dbg"), inc="s_tri")
        b.do("vector", lambda v: v.engine_nop(), inc="f_sc")
        b.do("vector", lambda v: v.engine_nop(), inc="f_sq")

        def f(t):
            ins = None
            for hh in range(16):
                pb = ps[4] if hh < 8 else ps[6]
                col = (hh % 8) * 64
                ins = t.matmul(pb[:, col:col + 64],
                               wbap(WA + 28672 + hh * 128, [[1, 128]]),
                               vTc[:, hh * 64:(hh + 1) * 64],
                               start=False, stop=(hh % 8 == 7))
            return ins
        b.do("tensor", f, waits=("s_tri", "s_bmm"), inc="s_apply")
        b.do("vector", lambda v: v.engine_nop(), waits=("s_apply",), inc="f_kvT")
        b.do("vector", lambda v: v.engine_nop(), waits=("s_apply",), inc="f_mixv")

        def f(v):
            v.tensor_copy(tri[:, 0:512], ps[4][:, 0:512])
            return v.tensor_copy(tri[:, 512:1024], ps[6][:, 0:512])
        b.do("vector", f, waits=("s_apply", "s_ytr"), inc="s_ytm")
        b.do("vector", lambda v: v.engine_nop(), inc="f_ps4")
        b.do("vector", lambda v: v.engine_nop(), inc="f_ps6")
        if cut(6):
            return

        if debug:
            t0d = (cc - 32) * L
            hdbg = b.pre("s_dbg", 48)
            def fd(g, t0d=t0d, hdbg=hdbg):
                ccc = cc - 32
                doff = (ccc // 4) * 128 * TH + (ccc % 4) * 1024
                g.dma_start(bass.AP(ddbg["dby"], doff, [[TH, 128], [1, 1024]]),
                            tri[:, 0:1024]).then_inc(hdbg, 16)
                g.dma_start(bass.AP(ddbg["dbkt"], doff, [[TH, 128], [1, 1024]]),
                            kTc[:, :]).then_inc(hdbg, 16)
                g.dma_start(bass.AP(ddbg["dbvt"], doff, [[TH, 128], [1, 1024]]),
                            vTc[:, :]).then_inc(hdbg, 16)
            wl1 = [(b.sems["s_ytm"][0], b.sems["s_ytm"][1]),
                   (b.sems["s_kT"][0], b.sems["s_kT"][1]),
                   (b.sems["s_vT"][0], b.sems["s_vT"][1])]
            def wdbg(g, wl1=wl1, fd=fd):
                for h, v in wl1:
                    g.wait_ge(h, v)
                fd(g)
            b.q["gpsimd"].append(wdbg)


        def tail():
            def f(a):
                return a.activation(gnsq[:, :], tri[:, 0:512], AF.Square)
            b.do("scalar", f, waits=("s_ytm", "s_gnr"), inc="s_ysq")

            def f(v):
                v.tensor_reduce(rowsG[:, 0:16].rearrange("p (a c) -> p a c", c=1),
                                wbap(WA + 28672, [[64, 16], [1, 64]]), AX.X, ALU.add)
                return v.tensor_reduce(rowsG[:, 16:24].rearrange("p (a c) -> p a c", c=1),
                                       bass.AP(gnsq, 0, [[512, 128], [64, 8], [1, 64]]),
                                       AX.X, ALU.add)
            b.do("vector", f, waits=("s_ysq",), inc="s_gnr")

            def f(a):
                return a.activation(gnsq[:, :], tri[:, 512:1024], AF.Square)
            b.do("scalar", f, waits=("s_ytm", "s_gnr"), inc="s_ysq")

            def f(v):
                return v.tensor_reduce(rowsG[:, 24:32].rearrange("p (a c) -> p a c", c=1),
                                       bass.AP(gnsq, 0, [[512, 128], [64, 8], [1, 64]]),
                                       AX.X, ALU.add)
            b.do("vector", f, waits=("s_ysq",), inc="s_gnr")

            def f(v):
                v.tensor_scalar_mul(rowsG[:, 32:48], rowsG[:, 0:16], 1.0 / SD)
                return v.tensor_scalar_mul(rowsG[:, 48:64], rowsG[:, 16:32], 1.0 / SD)
            b.do("vector", f, waits=("s_gnr",), inc="s_gnv")

            def f(a):
                return a.activation(rowsG[:, 64:80], rowsG[:, 32:48], AF.Square)
            b.do("scalar", f, waits=("s_gnv",), inc="s_gna")

            def f(v):
                v.tensor_sub(rowsG[:, 48:64], rowsG[:, 48:64], rowsG[:, 64:80])
                return v.engine_nop()
            b.do("vector", f, waits=("s_gna",), inc="s_gnv")

            pga2 = b.chainpt("c_act")

            def f(a):
                a.activation(rowsG[:, 48:64], rowsG[:, 48:64], AF.Ln,
                             bias=smf[:, 129:130]).then_inc(pga2[0], 1)
                a.wait_ge(pga2[0], pga2[1])
                return a.activation(rowsG[:, 48:64], rowsG[:, 48:64], AF.Exp,
                                    scale=-0.5)
            b.do("scalar", f, waits=("s_gnv", "s_zero"), inc="s_gna")

            def f(v):
                return v.tensor_mul(rowsG[:, 32:48], rowsG[:, 32:48], rowsG[:, 48:64])
            b.do("vector", f, waits=("s_gna",), inc="s_gnv")
            if cut(7):
                return

            pga = b.chainpt("c_gp")

            def f(g):
                y3 = wbap(WA + 28672, [[64, 16], [1, 64]])
                g.tensor_tensor(y3, y3, bass.AP(rowsG, 48, [[80, 128], [1, 16], [0, 64]]),
                                ALU.mult).then_inc(pga[0], 1)
                g.wait_ge(pga[0], pga[1])
                return g.tensor_tensor(y3, y3,
                                       bass.AP(rowsG, 32, [[80, 128], [1, 16], [0, 64]]),
                                       ALU.subtract)
            b.do("gpsimd", f, waits=("s_gnv", "s_dbg"), inc="s_gnap")

            def f(t):
                ins = None
                for sl in range(8):
                    ins = t.transpose(pbt[:, sl * 128:(sl + 1) * 128],
                                      tri[:, sl * 128:(sl + 1) * 128], ident[:, :])
                return ins
            b.do("tensor", f, waits=("s_gnap", "f_pbt"), inc="s_ytr")

            def f(a):
                return a.activation(tri[:, 1024:2048], pbt[:, 0:1024], AF.Copy)
            b.do("scalar", f, waits=("s_ytr",), inc="s_yfm")
            b.do("scalar", lambda a: a.nop(), inc="f_pbt")

            def f(a):
                ins = None
                for ch in range(NCH):
                    sl = tri[:, 1024 + ch * 128:1024 + (ch + 1) * 128]
                    ins = a.activation(sl, sl, AF.Identity, bias=LNB[:, ch:ch + 1],
                                       scale=LNW[:, ch:ch + 1])
                return ins
            b.do("scalar", f, waits=("s_yfm",), inc="s_yga")

            def f(g):
                ins = None
                for ch in range(NCH):
                    sl = tri[:, 1024 + ch * 128:1024 + (ch + 1) * 128]
                    ins = g.tensor_tensor(
                        ygF[:, ch * 512 + c0:ch * 512 + c0 + 128], sl,
                        gF[:, ch * 512 + c0:ch * 512 + c0 + 128], ALU.mult)
                return ins
            b.do("gpsimd", f, waits=("s_yga", "s_pjd", "f_ygF"), inc="s_yg")
            b.do("gpsimd", lambda g: g.engine_nop(), inc="f_tri")

            if ci == CPT - 1:
                b.do("vector", lambda v: v.engine_nop(), waits=("s_yg",), inc="f_scr")

        return tail
    def oproj_tile(ti, ph):
        t0 = ti * TILE
        for co in range(NCH):
            bank = co % 2

            def f(t, co=co, bank=bank):
                for ci in range(NCH):
                    ins = t.matmul(ps[bank][:, 0:512],
                                   W_o[:, ci * 1024 + co * 128:ci * 1024 + co * 128 + 128],
                                   ygF[:, ci * 512:(ci + 1) * 512],
                                   start=(ci == 0), stop=(ci == NCH - 1))
                return ins
            b.do("tensor", f, waits=("s_yg", f"f_pj{bank}", "f_sc"), inc="s_mm")

            def f(v, co=co, bank=bank):
                return v.tensor_copy(attf[:, 0:512], ps[bank][:, 0:512])
            b.do("vector", f, waits=("s_mm", "f_at"), inc="s_ev")
            b.do("vector", lambda v: v.engine_nop(), inc=f"f_pj{bank}")

            def f(g, co=co, bank=bank):
                return g.dma_start(
                    bass.AP(d_x1, co * 128 * TH + t0, [[TH, 128], [1, 512]]),
                    attf[:, 0:512], accum_op=ALU.add)
            b.do("gpsimd", f, waits=("s_ev", "s_x1pre"), inc="s_x1acc", k=16)
            b.do("gpsimd", lambda g: g.engine_nop(), waits=("s_x1acc",),
                 inc="f_at")
        if debug:
            hdbg2 = b.pre("s_dbg", 64)
            wl2 = [(b.sems["s_yg"][0], b.sems["s_yg"][1]),
                   (b.sems["s_pjd"][0], b.sems["s_pjd"][1])]
            def fd(g, t0=t0, hdbg2=hdbg2, wl2=wl2):
                for h, v in wl2:
                    g.wait_ge(h, v)
                ap3 = [[TH, 128], [TH * 128, 8], [1, 512]]
                g.dma_start(bass.AP(ddbg["dbr"], t0, ap3),
                            rF.rearrange("p (a c) -> p a c", c=512)).then_inc(hdbg2, 16)
                g.dma_start(bass.AP(ddbg["dbk"], t0, ap3),
                            kF.rearrange("p (a c) -> p a c", c=512)).then_inc(hdbg2, 16)
                g.dma_start(bass.AP(ddbg["dbg"], t0, ap3),
                            gF.rearrange("p (a c) -> p a c", c=512)).then_inc(hdbg2, 16)
                g.dma_start(bass.AP(ddbg["dbyg"], t0, ap3),
                            ygF.rearrange("p (a c) -> p a c", c=512)).then_inc(hdbg2, 16)
            b.q["gpsimd"].append(fd)
        b.do("tensor", lambda t: t.nop(), waits=("s_mm",), inc="f_ygF")
        b.do("tensor", lambda t: t.nop(), waits=("s_mm",), inc="f_mix")

        def f(s):
            return s.dma_start(bass.AP(d_out, t0, [[TH, 128], [TH * 128, 8], [1, 512]]),
                               bass.AP(d_x1, t0, [[TH, 128], [TH * 128, 8], [1, 512]]))
        b.do("sync", f, waits=("s_x1acc",), inc="s_outpre", k=16)

    def ffn_tile(ti):
        t0 = ti * TILE
        ph = "fn"
        ln_xhat(ti, d_x1, ph, x1b, 0, hb[:, 0:4096], 4096, "t2")

        for co in range(24):
            bk = co % 2

            def f(t, co=co, bk=bk):
                for ci in range(NCH):
                    ins = t.matmul(ps[bk][:, 0:512],
                                   Fg[:, ci * FFH + co * 128:ci * FFH + co * 128 + 128],
                                   x1b[:, ci * 512:(ci + 1) * 512],
                                   start=(ci == 0), stop=(ci == NCH - 1))
                return ins
            b.do("tensor", f, waits=("s_xhat", f"f_fg{bk}", "s_fw",
                                     "f_sc", "f_pj0", "f_pj1"),
                 inc="s_mm")

            def f(a, co=co, bk=bk):
                return a.activation(hb[:, co * 512:(co + 1) * 512], ps[bk][:, 0:512],
                                    AF.Silu, scale=1.0 / 127.0)
            b.do("scalar", f, waits=("s_mm",), inc=f"f_fg{bk}")

            def f(t, co=co, bk=bk):
                for ci in range(NCH):
                    ins = t.matmul(ps[2 + bk][:, 0:512],
                                   Fu[:, ci * FFH + co * 128:ci * FFH + co * 128 + 128],
                                   x1b[:, ci * 512:(ci + 1) * 512],
                                   start=(ci == 0), stop=(ci == NCH - 1))
                return ins
            b.do("tensor", f, waits=("s_xhat", f"f_fu{bk}", "s_fw",
                                     "s_kT", "s_vT"),
                 inc="s_mm")

            def f(v, co=co, bk=bk):
                sl = hb[:, co * 512:(co + 1) * 512]
                return v.tensor_tensor(sl, sl, ps[2 + bk][:, 0:512], ALU.mult)
            b.do("vector", f, waits=("s_mm", f"f_fg{bk}"), inc=f"f_fu{bk}")
        b.do("vector", lambda v: v.engine_nop(), waits=(f"f_fu0", "f_fu1"),
             inc="s_h")

        for co in range(NCH):
            bk = 4 + co % 2

            def f(t, co=co, bk=bk):
                for ci in range(24):
                    ins = t.matmul(ps[bk][:, 0:512],
                                   Fd[:, ci * 1024 + co * 128:ci * 1024 + co * 128 + 128],
                                   hb[:, ci * 512:(ci + 1) * 512],
                                   start=(ci == 0), stop=(ci == 23))
                return ins
            b.do("tensor", f, waits=("s_h", f"f_fd{bk}", "f_ps4", "f_ps5"),
                 inc="s_mm")

            def f(v, co=co, bk=bk):
                return v.tensor_scalar_mul(attf[:, 0:512],
                                           ps[bk][:, 0:512], 1.0 / (127.0 * 127.0))
            b.do("vector", f, waits=("s_mm", "f_fat"), inc="s_ev")
            b.do("vector", lambda v: v.engine_nop(), inc=f"f_fd{bk}")

            def f(g, co=co, bk=bk):
                return g.dma_start(
                    bass.AP(d_out, co * 128 * TH + t0, [[TH, 128], [1, 512]]),
                    attf[:, 0:512], accum_op=ALU.add)
            b.do("gpsimd", f, waits=("s_ev", "s_outpre"), inc="s_outacc", k=16)
            b.do("gpsimd", lambda g: g.engine_nop(), waits=("s_outacc",),
                 inc="f_fat")
        b.do("vector", lambda v: v.engine_nop(), waits=("s_h",),
             inc="f_xh")
        b.do("vector", lambda v: v.engine_nop(), waits=("s_mm",),
             inc="f_scr")

    setup()
    b.do("gpsimd", lambda g: g.memset(onesb[:, :], 1.0), inc="s_onesb")

    cc = 0
    npre = NTIL if trunc != 1 else 0
    for ti in range(npre):
        ln_xhat(ti, d_px, "pre", xh, 0, rF, 0, "wb")

        def fbc(g, sl=ti % 2):
            return g.tensor_copy(bcolb[:, sl * 8:sl * 8 + 8], xh[:, 511::512])
        b.do("gpsimd", fbc, waits=("s_xhat",), inc="s_bcol")
        mix_full("pre", mixk, 1, "s_xhat", inc="s_mixk", bslot=(ti + 1) % 2,
                 extraw=("s_bcol",))
        mix_full("pre", mixr, 2, "s_xhat", inc="s_mixv", bslot=(ti + 1) % 2)
        b.do("gpsimd", lambda g: g.engine_nop(),
             waits=("s_mixk", "s_mixv"), inc="f_xh")
        for ci in range(CPT):
            scan_chunk(cc, ci, "pre", False)
            cc += 1
        b.do("tensor", lambda t: t.nop(), waits=("s_tmproj",), inc="f_mix")

    if trunc in (1, 2):
        nmain = 0
    elif 31 <= trunc <= 49:
        nmain = 1
    else:
        nmain = NTIL
    pend = None
    for ti in range(nmain):
        ln_xhat(ti, d_mx, "mn", xh, 0, rF, 0, "wb")

        def fbc(g, sl=ti % 2):
            return g.tensor_copy(bcolb[:, sl * 8:sl * 8 + 8], xh[:, 511::512])
        b.do("gpsimd", fbc, waits=("s_xhat",), inc="s_bcol")
        mix_full("mn", mixr, 0, "s_xhat", inc="s_mixr", bslot=(ti + 1) % 2,
                 extraw=("s_bcol",))
        mix_full("mn", mixg, 3, "s_xhat", inc="s_mixg", bslot=(ti + 1) % 2)
        mix_full("mn", mixk, 1, "s_xhat", inc="s_mixk", bslot=(ti + 1) % 2)
        if trunc == 31:
            break
        fm_proj("mn", "r", W_r, mixr, rF, AF.Copy, ("s_mixr", "f_sc"))
        fm_proj("mn", "g", W_g, mixg, gF, AF.Silu, ("s_mixg", "f_sc"))
        fm_proj("mn", "k", W_k, mixk, kF, AF.Copy, ("s_mixk", "f_sc"))
        mix_full("mn", mixr, 2, "s_xhat", inc="s_mixv", bslot=(ti + 1) % 2,
                 extraw=("s_pjd",))
        b.do("gpsimd", lambda g: g.engine_nop(),
             waits=("s_mixr", "s_mixg", "s_mixk", "s_mixv"), inc="f_xh")
        if trunc == 32:
            break
        nchunks = 1 if (trunc == 33 or 41 <= trunc <= 48) else CPT
        for ci in range(nchunks):
            pend = scan_chunk(cc, ci, "mn", True, pending=pend)
            cc += 1
        if trunc in (33, 34) or 41 <= trunc <= 48:
            break
        if pend is not None:
            pend()
            pend = None
        oproj_tile(ti, "mn")

    if trunc in (0, 4):
        fw_w = ("s_yg", "f_tri", "s_mm", "s_dbg")
        b.do("gpsimd", lambda g: g.dma_start(
            Fg.rearrange("p (a c) -> p a c", c=FFH),
            bass.AP(d_fg, 0, [[FFH, 128], [FFH * 128, 8], [1, FFH]])),
            waits=fw_w, inc="s_fw", k=16)
        b.do("gpsimd", lambda g: g.dma_start(
            Fu.rearrange("p (a c) -> p a c", c=FFH),
            bass.AP(d_fu, 0, [[FFH, 128], [FFH * 128, 8], [1, FFH]])),
            inc="s_fw", k=16)
        b.do("gpsimd", lambda g: g.dma_start(
            Fd.rearrange("p (a c) -> p a c", c=1024),
            bass.AP(d_fd, 0, [[1024, 128], [1024 * 128, 24], [1, 1024]])),
            inc="s_fw", k=16)
        b.do("vector", lambda v: v.engine_nop(), waits=("s_x1acc",), inc="f_xh")
        b.do("vector", lambda v: v.engine_nop(), waits=("s_x1acc",), inc="f_scr")

    nffn = NTIL if trunc == 0 else 0
    for ti in range(nffn):
        ffn_tile(ti)

    if trunc:
        b.waitonly("sync", *list(b.sems))
    else:
        b.waitonly("sync", "s_outacc", "s_outpre", "s_x1acc")

    with nc.Block() as blk:
        blk.sync(lambda e: [f(e) for f in b.q["sync"]] and None)
        blk.gpsimd(lambda e: [f(e) for f in b.q["gpsimd"]] and None)
        blk.tensor(lambda e: [f(e) for f in b.q["tensor"]] and None)
        blk.vector(lambda e: [f(e) for f in b.q["vector"]] and None)
        blk.scalar(lambda e: [f(e) for f in b.q["scalar"]] and None)
    st.close()
    return nc


def _consts():
    con = np.zeros((128, 1024), np.float32)
    con[:, 0:128] = np.eye(128, dtype=np.float32)
    tm = np.zeros((128, 128), np.float32)
    for m in LEVELS:
        for q in range(0, 128, 2 * m):
            tm[q:q + m, q + m:q + 2 * m] = 1.0
    con[:, 128:256] = tm
    cv = np.zeros((128, 64), np.float32)
    t = np.arange(128)
    for d in range(4):
        if d == 0:
            val = np.ones(128)
        else:
            val = ((t + d) <= 127) & (((t + d) % 4) >= d)
        cv[:, d * 16:(d + 1) * 16] = val.astype(np.float32)[:, None]
    con[:, 256:320] = cv
    for d in range(4):
        z = np.zeros((128, 128), np.float32)
        j = np.arange(128 - d)
        z[j, j + d] = 1.0
        con[:, 320 + d * 128:320 + (d + 1) * 128] = z
    zs = np.zeros((128, 128), np.float32)
    for li, m in enumerate((8, 4)):
        for p in range(64):
            q, jj = p // m, p % m
            zs[li * 64 + p, 2 * m * q + jj] = 1.0
    con[:, 832:960] = zs
    sm = np.zeros((128, 64), np.float32)
    for li, m in enumerate((8, 4)):
        for p in range(64):
            qj = p // m
            sm[li * 64 + p, qj * m:(qj + 1) * m] = 1.0
    con[:, 960:1024] = sm
    return con


_CACHE = {}


def kernel(**inputs):
    x0 = np.ascontiguousarray(np.asarray(inputs["L0_attn_xq"], np.float32))
    wT = {z: np.ascontiguousarray(np.asarray(inputs[f"L0_attn_W{z}"], np.float32).T)
          for z in ("r", "k", "v", "g", "o")}
    fgT = np.ascontiguousarray(np.asarray(inputs["L0_ffn_w_gate"], np.int32).T)
    fuT = np.ascontiguousarray(np.asarray(inputs["L0_ffn_w_up"], np.int32).T)
    fdT = np.ascontiguousarray(np.asarray(inputs["L0_ffn_w_down"], np.int32).T)
    mix = np.ascontiguousarray(np.stack([
        np.asarray(inputs["L0_attn_time_mix_r"], np.float32).ravel(),
        np.asarray(inputs["L0_attn_time_mix_k"], np.float32).ravel(),
        np.asarray(inputs["L0_attn_time_mix_v"], np.float32).ravel(),
        np.asarray(inputs["L0_attn_time_mix_g"], np.float32).ravel()]))
    dec = np.ascontiguousarray(np.asarray(inputs["L0_attn_time_decay"], np.float32).ravel())
    faa = np.ascontiguousarray(np.asarray(inputs["L0_attn_time_faaaa"], np.float32).ravel())
    lnw = np.ascontiguousarray(np.asarray(inputs["L0_attn_ln_w"], np.float32).ravel())
    lnb = np.ascontiguousarray(np.asarray(inputs["L0_attn_ln_b"], np.float32).ravel())
    con = _consts()

    if "nc" not in _CACHE:
        _CACHE["nc"] = build_program()
    nc = _CACHE["nc"]

    zero = np.zeros((C, TH), np.float32)
    in_maps = []
    for core in range(8):
        bb, j = core // 2, core % 2
        xb = x0[bb]
        px = zero if j == 0 else np.ascontiguousarray(xb[0:TH].T)
        mx = np.ascontiguousarray(xb[j * TH:(j + 1) * TH].T)
        in_maps.append(dict(px=px, mx=mx,
                            wrT=wT["r"], wkT=wT["k"], wvT=wT["v"], wgT=wT["g"],
                            woT=wT["o"], fgT=fgT, fuT=fuT, fdT=fdT, mix=mix,
                            dec=dec, faa=faa, lnw=lnw, lnb=lnb, consts=con))
    res = run_bass_kernel_spmd(nc, in_maps, core_ids=list(range(8)))
    out = np.empty((4, 8192, 1024), np.float32)
    for core in range(8):
        bb, j = core // 2, core % 2
        out[bb, j * TH:(j + 1) * TH] = res.results[core]["out"].T
    return out


if __name__ == "__main__":
    nc = build_program()
    print("built ok; instructions:", nc.n_instructions())


# version 13
# speedup vs baseline: 1.0131x; 1.0076x over previous
import numpy as np
import ml_dtypes
from contextlib import ExitStack

import concourse.bass as bass
import concourse.mybir as mybir
from concourse.bass_utils import run_bass_kernel_spmd

F32 = mybir.dt.float32
BF16 = mybir.dt.bfloat16
I32 = mybir.dt.int32
AF = mybir.ActivationFunctionType
ALU = mybir.AluOpType
AX = mybir.AxisListType
BF = ml_dtypes.bfloat16

C, H, SD, NCH = 1024, 16, 64, 8
TH, L, TILE = 4096, 128, 512
NTIL, CPT = TH // TILE, TILE // L
FFH = 3072
EPS = 1e-5
GN_EPS = 64.0 * 1e-5
LEVELS = (64, 32, 16, 8, 4)
LOFF = {64: 0, 32: 512, 16: 1024, 8: 1536, 4: 2048}
WA = 40960


class B:
    def __init__(self, nc, stack):
        self.nc, self.st = nc, stack
        self.sems = {}
        self.q = {e: [] for e in ("sync", "gpsimd", "tensor", "vector", "scalar")}

    def sem(self, name):
        if name not in self.sems:
            self.sems[name] = [self.st.enter_context(self.nc.semaphore(name)), 0]
        return self.sems[name]

    def do(self, eng, fn, waits=(), inc=None, k=1):
        wl = []
        for wn in waits:
            e = self.sem(wn)
            if e[1] > 0:
                wl.append((e[0], e[1]))
        hk = None
        if inc is not None:
            e = self.sem(inc)
            e[1] += k
            hk = (e[0], k)

        def run(eh):
            for h, v in wl:
                eh.wait_ge(h, v)
            ins = fn(eh)
            if hk is not None:
                ins.then_inc(hk[0], hk[1])
        self.q[eng].append(run)

    def chainpt(self, name):
        e = self.sem(name)
        e[1] += 1
        return (e[0], e[1])

    def pre(self, name, k):
        e = self.sem(name)
        e[1] += k
        return e[0]

    def waitonly(self, eng, *names):
        wl = []
        for wn in names:
            e = self.sem(wn)
            if e[1] > 0:
                wl.append((e[0], e[1]))

        def run(eh):
            for h, v in wl:
                eh.wait_ge(h, v)
        self.q[eng].append(run)


def build_program(debug=False, trunc=0):
    nc = bass.Bass("TRN2", target_bir_lowering=False)
    st = ExitStack()
    b = B(nc, st)

    d_px = nc.dram_tensor("px", [C, TH], F32, kind="ExternalInput")
    d_mx = nc.dram_tensor("mx", [C, TH], F32, kind="ExternalInput")
    dW = {z: nc.dram_tensor(z, [C, C], F32, kind="ExternalInput")
          for z in ("wrT", "wkT", "wvT", "wgT", "woT")}
    d_fg = nc.dram_tensor("fgT", [C, FFH], I32, kind="ExternalInput")
    d_fu = nc.dram_tensor("fuT", [C, FFH], I32, kind="ExternalInput")
    d_fd = nc.dram_tensor("fdT", [FFH, C], I32, kind="ExternalInput")
    d_mix = nc.dram_tensor("mix", [4, C], F32, kind="ExternalInput")
    d_dec = nc.dram_tensor("dec", [C], F32, kind="ExternalInput")
    d_faa = nc.dram_tensor("faa", [C], F32, kind="ExternalInput")
    d_lnw = nc.dram_tensor("lnw", [C], F32, kind="ExternalInput")
    d_lnb = nc.dram_tensor("lnb", [C], F32, kind="ExternalInput")
    d_con = nc.dram_tensor("consts", [128, 1024], F32, kind="ExternalInput")
    d_x1 = nc.dram_tensor("x1buf", [C, TH], F32)
    d_out = nc.dram_tensor("out", [C, TH], F32, kind="ExternalOutput")
    ddbg = {}
    if debug:
        for nm in ("dbr", "dbk", "dbg", "dby", "dbyg", "dbkt", "dbvt"):
            ddbg[nm] = nc.dram_tensor(nm, [C, TH], F32, kind="ExternalOutput")

    wbig = st.enter_context(nc.sbuf_tensor("wbig", [128, 73728], BF16))
    t2 = st.enter_context(nc.sbuf_tensor("t2", [128, 19200], BF16))
    smf = st.enter_context(nc.sbuf_tensor("smf", [128, 132], F32))
    rowsA = st.enter_context(nc.sbuf_tensor("rowsA", [128, 1280], F32))
    rowsG = st.enter_context(nc.sbuf_tensor("rowsG", [128, 80], F32))
    ident = st.enter_context(nc.sbuf_tensor("identb", [128, 128], BF16))
    zsb = st.enter_context(nc.sbuf_tensor("zsb", [128, 384], BF16))
    trim = st.enter_context(nc.sbuf_tensor("trim", [128, 128], BF16))
    cval = st.enter_context(nc.sbuf_tensor("cval", [128, 64], F32))
    stgm = st.enter_context(nc.sbuf_tensor("stgm", [128, 64], F32))
    selband = st.enter_context(nc.sbuf_tensor("selband", [128, 512], BF16))
    pqs = {m: st.enter_context(nc.sbuf_tensor(f"pq{m}", [128, 8 * m], BF16))
           for m in LEVELS}
    pks = {m: st.enter_context(nc.sbuf_tensor(f"pk{m}", [128, 8 * m], BF16))
           for m in LEVELS}
    pq_int = st.enter_context(nc.sbuf_tensor("pq_int", [128, 1024], BF16))
    pk_tm = st.enter_context(nc.sbuf_tensor("pk_tm", [128, 1024], BF16))
    sstate = st.enter_context(nc.sbuf_tensor("sstate", [128, 512], F32))
    sstateb = st.enter_context(nc.sbuf_tensor("sstateb", [128, 2048], BF16))
    bcolb = st.enter_context(nc.sbuf_tensor("bcolb", [128, 16], BF16))
    attf = st.enter_context(nc.sbuf_tensor("attf", [128, 512], F32))
    onesb = st.enter_context(nc.sbuf_tensor("onesb", [128, 1], BF16))
    zscat = st.enter_context(nc.sbuf_tensor("zscat", [128, 128], BF16))
    gnsq = st.enter_context(nc.sbuf_tensor("gnsq", [128, 512], BF16))
    ps = [st.enter_context(nc.psum_tensor(f"ps{i}", [128, 512], F32)) for i in range(7)]
    pbt = st.enter_context(nc.psum_tensor("pbt", [128, 1024], BF16))

    def wvv(off, n):
        return wbig[:, off:off + n]

    W_r, W_k, W_v, W_g, W_o = (wvv(i * 8192, 8192) for i in range(5))
    xh = wvv(WA, 4096)
    mixr = wvv(WA + 4096, 4096)
    mixg = wvv(WA + 8192, 4096)
    rF = wvv(WA + 12288, 4096)
    kF = wvv(WA + 16384, 4096)
    gF = wvv(WA + 20480, 4096)
    ygF = wvv(WA + 24576, 4096)
    tri = wvv(WA + 28672, 2048)
    stgb = wvv(WA + 30720, 2048)
    Fg, Fu, Fd = wvv(0, 24576), wvv(24576, 24576), wvv(49152, 24576)
    sqL_o, skL_o, sqi_o, pdb_o = 0, 16384, 5120, 6144
    LOFF2 = {64: 0, 32: 1024, 16: 2048, 8: 3072, 4: 4096}
    SLOFF = {64: 0, 32: 512, 16: 1024, 8: 1792, 4: 2304}
    mixk = t2[:, 7168:11264]
    mixvc = t2[:, 11264:12288]
    kTc = t2[:, 12288:13312]
    vTc = t2[:, 13312:14336]
    vsh = t2[:, 14336:15360]
    csb = t2[:, 15360:15424]
    sqi = t2[:, 5120:6144]
    pdb = t2[:, 6144:7168]
    x1b = t2[:, 0:4096]
    hb = t2[:, 4096:16384]

    M_R, M_K, M_V, M_G = (smf[:, 8 * i:8 * i + 8] for i in range(4))
    LNW, LNB = smf[:, 32:40], smf[:, 40:48]
    DEC, UU = smf[:, 48:56], smf[:, 56:64]
    LW, W1 = smf[:, 64:72], smf[:, 72:80]
    W2, W128 = smf[:, 80:88], smf[:, 88:96]
    M1 = {0: smf[:, 96:104], 1: smf[:, 104:112], 2: smf[:, 112:120], 3: smf[:, 120:128]}
    MM = {0: M_R, 1: M_K, 2: M_V, 3: M_G}

    def t2ap(off, dims, p0=0, np_=128):
        return bass.AP(t2, p0 * 19200 + off, [[19200, np_]] + dims)

    def wbap(off, dims, p0=0, np_=128):
        return bass.AP(wbig, p0 * 73728 + off, [[73728, np_]] + dims)

    def setup():
        def f(g):
            g.memset(t2[:, 15424:16384], 0)
            g.memset(t2[:, 0:5120], 0)
            g.memset(t2[:, 17408:18176], 0)
            g.memset(smf[:, 130:131], 1.0)
            g.memset(stgb[:, :], 0)
            g.memset(selband[:, :], 0)
            g.memset(pdb[:, :], 0)
            g.memset(csb[:, :], 0)
            g.memset(bcolb[:, :], 0)
            g.memset(rowsA[0:1, 1024:1152], 1.0)
            g.memset(smf[:, 128:129], EPS)
            g.memset(smf[:, 129:130], GN_EPS)
            g.memset(rowsA[0:1, 1152:1280], -1.0)
            return g.engine_nop()
        b.do("gpsimd", f, waits=("s_small",), inc="s_zero")

        def f(v):
            v.memset(sstate[:, :], 0)
            v.memset(sstateb[:, :], 0)
            for i in range(7):
                v.memset(ps[i][:, :], 0)
            return v.engine_nop()
        b.do("vector", f, inc="s_zero")

        h_small = b.pre("s_small", 16 * 11)

        def f(s):
            s.dma_start(cval[:, :], bass.AP(d_con, 256, [[1024, 128], [1, 64]])).then_inc(h_small, 16)
            s.dma_start(stgm[:, :], bass.AP(d_con, 960, [[1024, 128], [1, 64]])).then_inc(h_small, 16)
            for z in range(4):
                s.dma_start(smf[:, z * 8:z * 8 + 8],
                            bass.AP(d_mix, z * C, [[1, 128], [128, 8]]),
                            allow_slow_non_contiguous=True).then_inc(h_small, 16)
            s.dma_start(LNW, bass.AP(d_lnw, 0, [[1, 128], [128, 8]]), allow_slow_non_contiguous=True).then_inc(h_small, 16)
            s.dma_start(LNB, bass.AP(d_lnb, 0, [[1, 128], [128, 8]]), allow_slow_non_contiguous=True).then_inc(h_small, 16)
            s.dma_start(DEC, bass.AP(d_dec, 0, [[1, 128], [128, 8]]), allow_slow_non_contiguous=True).then_inc(h_small, 16)
            s.dma_start(UU, bass.AP(d_faa, 0, [[1, 128], [128, 8]]), allow_slow_non_contiguous=True).then_inc(h_small, 16)
            s.dma_start(rowsA[0:1, 0:1024], bass.AP(d_dec, 0, [[0, 1], [1, 1024]])).then_inc(h_small, 16)
            return None

        b.q["sync"].append(f)

        b.do("sync", lambda s: s.dma_start(d_x1.ap(), d_mx.ap()), inc="s_x1pre", k=16)

        h_wl = b.pre("s_wload", 16 * 9)

        def f(g):
            sems_w = b.sems["s_small"]
            g.wait_ge(sems_w[0], sems_w[1])
            g.dma_start(ident[:, :], bass.AP(d_con, 0, [[1024, 128], [1, 128]])).then_inc(h_wl, 16)
            g.dma_start(zsb[:, :].rearrange("p (a c) -> p a c", c=128),
                        bass.AP(d_con, 448, [[1024, 128], [128, 3], [1, 128]])).then_inc(h_wl, 16)
            g.dma_start(zscat[:, :], bass.AP(d_con, 832, [[1024, 128], [1, 128]])).then_inc(h_wl, 16)
            g.dma_start(trim[:, :], bass.AP(d_con, 128, [[1024, 128], [1, 128]])).then_inc(h_wl, 16)
            for dram, view in ((dW["wrT"], W_r), (dW["wkT"], W_k), (dW["wvT"], W_v),
                               (dW["wgT"], W_g), (dW["woT"], W_o)):
                g.dma_start(view.rearrange("p (a c) -> p a c", c=1024),
                            bass.AP(dram, 0, [[1024, 128], [1024 * 128, 8], [1, 1024]])).then_inc(h_wl, 16)
        b.q["gpsimd"].append(f)

        p3 = b.chainpt("c_act")

        def f(a):
            a.activation(rowsA[0:1, 0:1024], rowsA[0:1, 0:1024],
                         AF.Exp).then_inc(p3[0], 1)
            a.wait_ge(p3[0], p3[1])
            return a.mul(rowsA[0:1, 0:1024], rowsA[0:1, 0:1024], -1.0)
        b.do("scalar", f, waits=("s_small",), inc="s_lwrow")

        def f(g):
            return g.iota(rowsG[:, 64:65], [[0, 1]], base=127, channel_multiplier=-1,
                          allow_small_or_imprecise_dtypes=True)
        b.do("gpsimd", f, inc="s_riota")

        for piece in (1, 0):
            def fmm(t, piece=piece):
                return t.matmul(ps[6][:, 0:512], rowsA[0:1, 1024:1152],
                                rowsA[0:1, piece * 512:piece * 512 + 512],
                                start=True, stop=True)
            b.do("tensor", fmm, waits=("s_lwrow", "s_lwbc", "s_zero"),
                 inc="s_rbmm")

            def fcp(a, piece=piece):
                return a.activation(rowsA[:, 512:1024], ps[6][:, 0:512], AF.Copy)
            b.do("scalar", fcp, waits=("s_rbmm", "s_pktm"), inc="s_lwbc")

            def f(a, piece=piece):
                return a.activation(pk_tm[:, piece * 512:piece * 512 + 512],
                                    rowsA[:, 512:1024], AF.Exp,
                                    scale=rowsG[:, 64:65])
            b.do("scalar", f, waits=("s_lwbc", "s_riota"), inc="s_pktm")

        p1 = b.chainpt("c_act")
        p2 = b.chainpt("c_act")

        def f(a):
            a.activation(rowsA[:, 128:136], DEC, AF.Exp).then_inc(p1[0], 1)
            a.wait_ge(p1[0], p1[1])
            a.mul(LW, rowsA[:, 128:136], -1.0).then_inc(p2[0], 1)
            a.wait_ge(p2[0], p2[1])
            a.activation(W1, LW, AF.Exp)
            a.activation(W2, LW, AF.Exp, scale=2.0)
            return a.activation(W128, LW, AF.Exp, scale=128.0)
        b.do("scalar", f, waits=("s_small", "s_pktm"), inc="s_lw")

        def f(v):
            for i in range(4):
                v.tensor_scalar(M1[i], MM[i], -1.0, 1.0, ALU.mult, ALU.add)
            return v.engine_nop()
        b.do("vector", f, waits=("s_small",), inc="s_m1")

        ROFF = {64: 0, 32: 64, 16: 96, 8: 112, 4: 120}

        pm1 = b.chainpt("c_gp")

        def f(g):
            g.memset(rowsA[0:1, 128:256], 0).then_inc(pm1[0], 1)
            g.wait_ge(pm1[0], pm1[1])
            for m in LEVELS:
                g.iota(rowsA[0:1, 128 + ROFF[m]:128 + ROFF[m] + m], [[0, 1], [1, m]],
                       channel_multiplier=0, allow_small_or_imprecise_dtypes=True)
            return g.engine_nop()
        b.do("gpsimd", f, waits=("s_small", "s_lw", "s_pktm"), inc="s_ramp")

        def fmm(t):
            return t.matmul(ps[6][:, 0:128], rowsA[0:1, 1024:1152],
                            rowsA[0:1, 128:256], start=True, stop=True)
        b.do("tensor", fmm, waits=("s_ramp", "s_zero", "s_lwbc"), inc="s_rbmm")

        def fcp(a):
            return a.activation(rowsA[:, 256:384], ps[6][:, 0:128], AF.Copy)
        b.do("scalar", fcp, waits=("s_rbmm", "s_pktm"), inc="s_rbc")

        def f(a):
            for m in LEVELS:
                for ch in range(NCH):
                    ins = a.activation(pqs[m][:, ch * m:(ch + 1) * m],
                                       rowsA[:, 256 + ROFF[m]:256 + ROFF[m] + m],
                                       AF.Exp, scale=LW[:, ch:ch + 1])
            return ins
        b.do("scalar", f, waits=("s_rbc", "s_lw"), inc="s_pow")

        pm2 = b.chainpt("c_gp")

        def f(g):
            g.memset(rowsA[0:1, 128:256], 0).then_inc(pm2[0], 1)
            g.wait_ge(pm2[0], pm2[1])
            for m in LEVELS:
                g.iota(rowsA[0:1, 128 + ROFF[m]:128 + ROFF[m] + m], [[0, 1], [-1, m]],
                       base=m - 1, channel_multiplier=0,
                       allow_small_or_imprecise_dtypes=True)
            return g.engine_nop()
        b.do("gpsimd", f, waits=("s_pow",), inc="s_ramp")

        def fmm(t):
            return t.matmul(ps[6][:, 0:128], rowsA[0:1, 1024:1152],
                            rowsA[0:1, 128:256], start=True, stop=True)
        b.do("tensor", fmm, waits=("s_ramp", "s_pow"), inc="s_rbmm")

        def fcp(a):
            return a.activation(rowsA[:, 256:384], ps[6][:, 0:128], AF.Copy)
        b.do("scalar", fcp, waits=("s_rbmm", "s_pow"), inc="s_rbc")

        def f(a):
            for m in LEVELS:
                for ch in range(NCH):
                    ins = a.activation(pks[m][:, ch * m:(ch + 1) * m],
                                       rowsA[:, 256 + ROFF[m]:256 + ROFF[m] + m],
                                       AF.Exp, scale=LW[:, ch:ch + 1])
            return ins
        b.do("scalar", f, waits=("s_rbc",), inc="s_pow")

        def f(g):
            return g.iota(rowsA[0:1, 128:256], [[0, 1], [1, 128]],
                          channel_multiplier=0, allow_small_or_imprecise_dtypes=True)
        b.do("gpsimd", f, waits=("s_pow",), inc="s_ramp")

        def fmm(t):
            return t.matmul(ps[6][:, 0:128], rowsA[0:1, 1024:1152],
                            rowsA[0:1, 128:256], start=True, stop=True)
        b.do("tensor", fmm, waits=("s_ramp", "s_pow"), inc="s_rbmm")

        def fcp(a):
            return a.activation(rowsA[:, 256:384], ps[6][:, 0:128], AF.Copy)
        b.do("scalar", fcp, waits=("s_rbmm", "s_pow"), inc="s_rbc")

        def f(a):
            for ch in range(NCH):
                ins = a.activation(pq_int[:, ch * 128:(ch + 1) * 128],
                                   rowsA[:, 256:384], AF.Exp,
                                   scale=LW[:, ch:ch + 1])
            return ins
        b.do("scalar", f, waits=("s_rbc",), inc="s_pow")

        b.do("vector", lambda v: v.engine_nop(), waits=("s_pow",), inc="f_ps6")

        def f(v):
            for ch in range(NCH):
                for d in range(4):
                    base = ch * 64 + d * 16
                    for half in range(2):
                        hh = 2 * ch + half
                        o = selband[64 * half:64 * half + 64, base + hh:base + hh + 1]
                        if d == 0:
                            v.tensor_copy(o, UU[64 * half:64 * half + 64, ch:ch + 1])
                        elif d == 1:
                            v.tensor_copy(o, smf[64 * half:64 * half + 64, 130:131])
                        elif d == 2:
                            v.tensor_copy(o, W1[64 * half:64 * half + 64, ch:ch + 1])
                        else:
                            v.tensor_copy(o, W2[64 * half:64 * half + 64, ch:ch + 1])
            return v.engine_nop()
        b.do("vector", f, waits=("s_zero", "s_lw"), inc="s_sel")

    def ln_xhat(ti, d_src, ph, dst, dst_o, scratch, scratch_o, arena):
        t0 = ti * TILE

        d3l = dst.rearrange("p (a c) -> p a c", c=512)

        def f(g):
            return g.dma_start(
                d3l[:, 0:4, :],
                bass.AP(d_src, t0, [[TH, 128], [TH * 128, 4], [1, 512]]))
        b.do("gpsimd", f, waits=("f_xh",), inc="s_xl", k=16)

        def f(g):
            return g.dma_start(
                d3l[:, 4:8, :],
                bass.AP(d_src, t0 + 4 * TH * 128, [[TH, 128], [TH * 128, 4], [1, 512]]))
        b.do("gpsimd", f, inc="s_xl2", k=16)

        if ph == "fn":
            for ch in range(NCH):
                def f(a, ch=ch):
                    return a.activation(gnsq[:, :], dst[:, ch * 512:(ch + 1) * 512],
                                        AF.Square)
                b.do("scalar", f, waits=("s_xl", "s_xl2", "s_fsqm", "s_gnr"), inc="s_fsq")

                def f(t, ch=ch):
                    return t.matmul(ps[6][32:33, 0:512], onesb[:, 0:1], gnsq[:, :],
                                    start=(ch == 0), stop=(ch == NCH - 1),
                                    tile_position=(0, 32))
                b.do("tensor", f, waits=("s_fsq", "f_ps6" if ch == 0 else "s_fsq",),
                     inc="s_fsqm")
        else:
            def f(a):
                return a.activation(scratch[:, 0:2048], dst[:, 0:2048], AF.Square)
            b.do("scalar", f, waits=("s_xl", "f_scr", "s_dbg"), inc="s_xsq")

            def f(a):
                return a.activation(scratch[:, 2048:4096], dst[:, 2048:4096],
                                    AF.Square)
            b.do("scalar", f, waits=("s_xl2",), inc="s_xsq")

        if ph == "pre":
            stat, ab, bb = ps[6], ps[4], ps[6]
            w_stat = ("s_xsq", "f_ps6", "s_onesb")
            w_musq = ("s_lrv", "f_ps4")
            w_bc = ("s_lrv", "f_ps4", "f_ps6")
            rel = ("f_ps4", "f_ps6")
        elif ph == "mn":
            stat, ab, bb = ps[2], ps[3], ps[2]
            w_stat = ("s_xsq", "s_vT", "s_onesb")
            w_musq = ("s_lrv", "s_vT")
            w_bc = ("s_lrv",)
            rel = ()
        else:
            stat, ab, bb = ps[6], ps[5], ps[4]
            w_stat = ("f_ps6", "s_onesb")
            w_musq = ("s_lrv", "f_ps5")
            w_bc = ("s_lrv", "f_ps4", "f_ps5")
            rel = ("f_ps4", "f_ps5", "f_ps6")

        if ph != "fn":
            h_xsq = b.sems["s_xsq"]

            def f(t, w1=h_xsq[1] - 1):
                t.wait_ge(h_xsq[0], w1)
                for ch in range(4):
                    t.matmul(stat[0:1, 0:512], onesb[:, 0:1],
                             dst[:, ch * 512:(ch + 1) * 512],
                             start=(ch == 0), stop=False)
                for ch in range(4):
                    ins = t.matmul(stat[32:33, 0:512], onesb[:, 0:1],
                                   scratch[:, ch * 512:(ch + 1) * 512],
                                   start=(ch == 0), stop=False,
                                   tile_position=(0, 32))
                return ins
            b.do("tensor", f, waits=tuple(w for w in w_stat if w != "s_xsq"),
                 inc="s_lnmm")

            def f(t):
                for ch in range(4, NCH):
                    t.matmul(stat[0:1, 0:512], onesb[:, 0:1],
                             dst[:, ch * 512:(ch + 1) * 512],
                             start=False, stop=(ch == NCH - 1))
                for ch in range(4, NCH):
                    ins = t.matmul(stat[32:33, 0:512], onesb[:, 0:1],
                                   scratch[:, ch * 512:(ch + 1) * 512],
                                   start=False, stop=(ch == NCH - 1),
                                   tile_position=(0, 32))
                return ins
            b.do("tensor", f, waits=("s_xsq",), inc="s_lnmm")
        else:
            def f(t):
                for ch in range(NCH):
                    ins = t.matmul(stat[0:1, 0:512], onesb[:, 0:1],
                                   dst[:, ch * 512:(ch + 1) * 512],
                                   start=(ch == 0), stop=(ch == NCH - 1))
                return ins
            b.do("tensor", f, waits=w_stat + ("s_fsqm",), inc="s_lnmm")

        def f(v):
            v.tensor_scalar_mul(rowsA[0:1, 0:512], stat[0:1, 0:512], 1.0 / C)
            return v.tensor_scalar_mul(rowsA[0:1, 512:1024], stat[32:33, 0:512], 1.0 / C)
        b.do("vector", f, waits=("s_lnmm", "s_pktm"), inc="s_lrv")
        if ph == "fn":
            b.do("vector", lambda v: v.engine_nop(), inc="f_ps6")

        def f(a):
            return a.activation(ab[0:1, 0:512], rowsA[0:1, 0:512], AF.Square)
        b.do("scalar", f, waits=w_musq, inc="s_lra")

        def f(v):
            v.tensor_sub(rowsA[0:1, 512:1024], rowsA[0:1, 512:1024], ab[0:1, 0:512])
            return v.engine_nop()
        b.do("vector", f, waits=("s_lra",), inc="s_lrv")

        pca = b.chainpt("c_act")

        def f(a):
            a.activation(rowsA[0:1, 512:1024], rowsA[0:1, 512:1024],
                         AF.Ln, bias=smf[0:1, 128:129]).then_inc(pca[0], 1)
            a.wait_ge(pca[0], pca[1])
            return a.activation(rowsA[0:1, 512:1024], rowsA[0:1, 512:1024],
                                AF.Exp, scale=-0.5)
        b.do("scalar", f, waits=("s_lrv",), inc="s_lra")

        def f(v):
            return v.tensor_mul(rowsA[0:1, 0:512], rowsA[0:1, 0:512],
                                rowsA[0:1, 512:1024])
        b.do("vector", f, waits=("s_lra",), inc="s_lrv")

        def f(t):
            t.matmul(ab[:, 0:512], rowsA[0:1, 1024:1152], rowsA[0:1, 512:1024],
                     start=True, stop=True)
            return t.matmul(bb[:, 0:512], rowsA[0:1, 1152:1280], rowsA[0:1, 0:512],
                            start=True, stop=True)
        b.do("tensor", f, waits=w_bc, inc="s_lnbc")

        pch = b.chainpt("c_dve")

        def f(v):
            a_bc = bass.AP(ab, 0, [[512, 128], [0, 8], [1, 512]])
            b_bc = bass.AP(bb, 0, [[512, 128], [0, 8], [1, 512]])
            d3 = dst.rearrange("p (a c) -> p a c", c=512)
            v.tensor_tensor(d3, d3, a_bc, ALU.mult).then_inc(pch[0], 1)
            v.wait_ge(pch[0], pch[1])
            return v.tensor_tensor(d3, d3, b_bc, ALU.add)
        b.do("vector", f, waits=("s_lnbc",), inc="s_xhat")
        for rr in rel:
            b.do("vector", lambda v: v.engine_nop(), waits=("s_xhat",), inc=rr)

    def mix_full(ph, out, mi, waitx, inc="s_mixf", extraw=(), bslot=0):
        def fa(a):
            o3 = out.rearrange("p (a c) -> p a c", c=512)
            x3 = xh.rearrange("p (a c) -> p a c", c=512)
            for ch in range(NCH):
                ins = a.activation(o3[:, ch:ch + 1, :], x3[:, ch:ch + 1, :],
                                   AF.Copy, scale=MM[mi][:, ch:ch + 1])
            return ins
        b.do("scalar", fa, waits=(waitx, "s_m1", "s_tmproj") + extraw, inc="s_mixa")

        def fv(v):
            o3 = out.rearrange("p (a c) -> p a c", c=512)
            x3 = xh.rearrange("p (a c) -> p a c", c=512)
            for ch in range(NCH):
                v.scalar_tensor_tensor(o3[:, ch:ch + 1, 1:512], x3[:, ch:ch + 1, 0:511],
                                       M1[mi][:, ch:ch + 1], o3[:, ch:ch + 1, 1:512],
                                       ALU.mult, ALU.add)
                ins = v.scalar_tensor_tensor(
                    o3[:, ch:ch + 1, 0:1],
                    bcolb[:, bslot * 8 + ch:bslot * 8 + ch + 1].unsqueeze(2),
                    M1[mi][:, ch:ch + 1], o3[:, ch:ch + 1, 0:1], ALU.mult, ALU.add)
            return ins
        b.do("vector", fv, waits=("s_mixa",), inc=inc)

    def fm_proj(ph, nm, W, src, dst, act, waits):
        for co in range(NCH):
            bank = co % 2

            def f(t, co=co, bank=bank):
                for ci in range(NCH):
                    ins = t.matmul(ps[bank][:, 0:512],
                                   W[:, ci * 1024 + co * 128:ci * 1024 + co * 128 + 128],
                                   src[:, ci * 512:(ci + 1) * 512],
                                   start=(ci == 0), stop=(ci == NCH - 1))
                return ins
            b.do("tensor", f, waits=waits + (f"f_pj{bank}",), inc="s_mm")

            def f(a, co=co, bank=bank):
                return a.activation(dst[:, co * 512:(co + 1) * 512],
                                    ps[bank][:, 0:512], act)
            b.do("scalar", f, waits=("s_mm",), inc=f"f_pj{bank}")
        b.do("tensor", lambda t: t.nop(), waits=("f_pj0", "f_pj1"),
             inc="s_pjd")

    def scan_chunk(cc, ci, ph, full, pending=None):
        def cut(n):
            return full and trunc == 40 + n
        c0 = ci * 128
        par = cc % 2

        def f(t):
            for cin in range(NCH):
                stk = mixk[:, cin * 512 + c0:cin * 512 + c0 + 128]
                for gs in range(2):
                    ins = t.matmul(ps[gs][:, 0:512], stk,
                                   W_k[:, cin * 1024 + gs * 512:cin * 1024 + gs * 512 + 512],
                                   start=(cin == 0), stop=(cin == NCH - 1))
            return ins
        b.do("tensor", f, waits=("s_mixk", "f_sc", "f_pj0", "f_pj1", "s_wload"),
             inc="s_tmk")

        def f(t):
            for cin in range(NCH):
                stv = mixr[:, cin * 512 + c0:cin * 512 + c0 + 128]
                for gs in range(2):
                    ins = t.matmul(ps[2 + gs][:, 0:512], stv,
                                   W_v[:, cin * 1024 + gs * 512:cin * 1024 + gs * 512 + 512],
                                   start=(cin == 0), stop=(cin == NCH - 1))
            return ins
        b.do("tensor", f, waits=("s_mixv", "s_tmk"), inc="s_tmproj")

        def f(v):
            v.tensor_tensor(kTc[:, 0:512], ps[0][:, 0:512], pk_tm[:, 0:512], ALU.mult)
            return v.tensor_tensor(kTc[:, 512:1024], ps[1][:, 0:512],
                                   pk_tm[:, 512:1024], ALU.mult)
        b.do("vector", f, waits=("s_tmk", "f_kvT", "s_pktm", "s_dbg"), inc="s_kT")

        def f(a):
            a.activation(vTc[:, 0:512], ps[2][:, 0:512], AF.Copy)
            return a.activation(vTc[:, 512:1024], ps[3][:, 0:512], AF.Copy)
        b.do("scalar", f, waits=("s_tmproj", "f_kvT", "s_dbg"), inc="s_vT")

        def f(t):
            ins = None
            for p in range(8):
                for half in range(2):
                    hh = 2 * p + half
                    ins = t.matmul(
                        ps[5][64 * half:64 * half + 64, p * 64:(p + 1) * 64],
                        kTc[:, hh * 64:(hh + 1) * 64], vTc[:, hh * 64:(hh + 1) * 64],
                        start=True, stop=True,
                        tile_position=(0, 64 * half))
            return ins
        b.do("tensor", f, waits=("s_kT", "s_vT", "f_ps5"), inc="s_umm")

        mirror = full or cc == 31
        pch2 = b.chainpt("c_dve") if mirror else None

        def f(v):
            for p in range(8):
                ins = v.scalar_tensor_tensor(
                    sstate[:, p * 64:(p + 1) * 64], sstate[:, p * 64:(p + 1) * 64],
                    W128[:, p:p + 1], ps[5][:, p * 64:(p + 1) * 64], ALU.mult, ALU.add)
            if mirror:
                ins.then_inc(pch2[0], 1)
                v.wait_ge(pch2[0], pch2[1])
                npo = (1 - par) * 1024
                v.tensor_copy(bass.AP(sstateb, npo, [[2048, 64], [128, 8], [1, 64]]),
                              bass.AP(sstate, 0, [[512, 64], [64, 8], [1, 64]]))
                ins = v.tensor_copy(
                    bass.AP(sstateb, 64 * 2048 + npo + 64,
                            [[2048, 64], [128, 8], [1, 64]]),
                    bass.AP(sstate, 64 * 512, [[512, 64], [64, 8], [1, 64]]))
            return ins
        b.do("vector", f, waits=("s_umm", "s_intery"), inc="s_chain")
        b.do("vector", lambda v: v.engine_nop(), inc="f_ps5")

        if not full:
            b.do("scalar", lambda a: a.nop(), waits=("s_umm",), inc="f_kvT")
            b.do("scalar", lambda a: a.nop(), waits=("s_kT", "s_vT"), inc="f_sc")
            if ci == CPT - 1:
                b.do("vector", lambda v: v.engine_nop(), waits=("s_chain",),
                     inc="f_scr")
            return

        def f(g):
            g.tensor_tensor(t2ap(sqi_o, [[128, 8], [1, 128]]),
                            wbap(WA + 12288 + c0, [[512, 8], [1, 128]]),
                            bass.AP(pq_int, 0, [[1024, 128], [128, 8], [1, 128]]),
                            ALU.mult)
            for m in (64, 32):
                nb = 64 // m
                for half in range(2):
                    g.tensor_tensor(
                        t2ap(LOFF2[m] + half * m, [[128, 8], [2 * m, nb], [1, m]],
                             p0=64 * half, np_=64),
                        wbap(WA + 12288 + c0 + m, [[512, 8], [2 * m, nb], [1, m]],
                             p0=64 * half, np_=64),
                        bass.AP(pqs[m], 64 * half * 8 * m,
                                [[8 * m, 64], [m, 8], [0, nb], [1, m]]),
                        ALU.mult)
            return g.engine_nop()
        b.do("gpsimd", f, waits=("s_pjd", "f_sq", "s_pow", "s_zero"), inc="s_sq1")

        def f(v):
            for m in (16, 8, 4):
                nb = 64 // m
                for half in range(2):
                    v.tensor_tensor(
                        t2ap(LOFF2[m] + half * m, [[128, 8], [2 * m, nb], [1, m]],
                             p0=64 * half, np_=64),
                        wbap(WA + 12288 + c0 + m, [[512, 8], [2 * m, nb], [1, m]],
                             p0=64 * half, np_=64),
                        bass.AP(pqs[m], 64 * half * 8 * m,
                                [[8 * m, 64], [m, 8], [0, nb], [1, m]]),
                        ALU.mult)
            for m in (64, 32, 8, 4):
                nb = 64 // m
                v.tensor_tensor(
                    t2ap(skL_o + SLOFF[m], [[64, 8], [m, nb], [1, m]]),
                    wbap(WA + 16384 + c0, [[512, 8], [2 * m, nb], [1, m]]),
                    bass.AP(pks[m], 0, [[8 * m, 128], [m, 8], [0, nb], [1, m]]),
                    ALU.mult)
            v.tensor_tensor(
                t2ap(skL_o + SLOFF[16], [[96, 8], [16, 3], [1, 16]]),
                wbap(WA + 16384 + c0, [[512, 8], [32, 3], [1, 16]]),
                bass.AP(pks[16], 0, [[128, 128], [16, 8], [0, 3], [1, 16]]),
                ALU.mult)
            v.tensor_tensor(
                t2ap(skL_o + SLOFF[16] + 80, [[96, 8], [1, 16]]),
                wbap(WA + 16384 + c0 + 96, [[512, 8], [1, 16]]),
                bass.AP(pks[16], 0, [[128, 128], [16, 8], [1, 16]]),
                ALU.mult)
            return v.engine_nop()
        b.do("vector", f, waits=("s_pjd", "f_sq", "s_pow", "s_zero"), inc="s_sq2")
        if cut(1):
            return

        for d in range(4):
            def f(g, d=d):
                r3 = rF.rearrange("p (a c) -> p a c", c=512)
                k3 = kF.rearrange("p (a c) -> p a c", c=512)
                return g.tensor_tensor(
                    pdb.rearrange("p (a c) -> p a c", c=128)[:, :, d:128],
                    r3[:, :, c0 + d:c0 + 128], k3[:, :, c0:c0 + 128 - d], ALU.mult)
            b.do("gpsimd", f, waits=("s_pjd", "s_pjd",
                                     "f_pdf"), inc="s_pd")

            def f(t, d=d):
                for ch in range(NCH):
                    ins = t.matmul(ps[6][0:128 - d, d * 16:(d + 1) * 16],
                                   pdb[:, ch * 128 + d:ch * 128 + 128],
                                   selband[:, ch * 64 + d * 16:ch * 64 + (d + 1) * 16],
                                   start=(ch == 0), stop=(ch == NCH - 1))
                return ins
            b.do("tensor", f, waits=("s_pd", "f_ps6", "s_sel"), inc="s_cmm")
            b.do("gpsimd", lambda g: g.engine_nop(), waits=("s_cmm",),
                 inc="f_pdf")

        def f(v):
            return v.tensor_tensor(csb[:, :], ps[6][:, 0:64], cval[:, :], ALU.mult)
        b.do("vector", f, waits=("s_cmm",), inc="s_csb")
        b.do("vector", lambda v: v.engine_nop(), inc="f_ps6")
        if cut(2):
            return

        if pending is not None:
            pending()

        def f(t):
            for ch in range(8):
                pb = ps[4] if ch < 4 else ps[6]
                col = (ch % 4) * 128
                ins = t.matmul(pb[:, col:col + 128],
                               t2ap(sqi_o + ch * 128, [[1, 128]]),
                               sstateb[:, par * 1024 + ch * 128:
                                       par * 1024 + ch * 128 + 128],
                               start=(ch % 4 == 0), stop=False)
            return ins
        b.do("tensor", f, waits=("s_sq1", "s_csb", "f_ps4", "f_ps6", "s_chain"),
             inc="s_intery")

        for d in range(4):
            vb = vsh if d % 2 == 0 else t2[:, 11264:12288]
            vbo = 14336 if d % 2 == 0 else 11264

            def f(g, d=d, vb=vb):
                return g.tensor_tensor(
                    vb.rearrange("p (a c) -> p a c", c=64),
                    vTc.rearrange("p (a c) -> p a c", c=64),
                    t2ap(15360 + d * 16, [[1, 16], [0, 64]]), ALU.mult)
            b.do("gpsimd", f, waits=("s_csb", "s_vT", f"f_vs{d % 2}"),
                 inc="s_vsh")

            def f(t, d=d, vb=vb):
                z = ident[:, :] if d == 0 else zsb[:, (d - 1) * 128:d * 128]
                for gs in range(2):
                    pb = ps[4] if gs == 0 else ps[6]
                    ins = t.matmul(pb[:, 0:512], z, vb[:, gs * 512:(gs + 1) * 512],
                                   start=False, stop=False)
                return ins
            b.do("tensor", f, waits=("s_vsh", "s_intery"), inc="s_bmm")
            b.do("gpsimd", lambda g: g.engine_nop(), waits=("s_bmm",),
                 inc=f"f_vs{d % 2}")
        if cut(3):
            return

        def f(t):
            ins = None
            for li, m in enumerate((8, 4)):
                for ch in range(8):
                    pb = ps[ch // 4]
                    col = (ch % 4) * 128
                    lh = t2ap(skL_o + SLOFF[m] + ch * 64, [[1, 64]])
                    rh = t2ap(LOFF2[m] + ch * 128, [[1, 128]])
                    ins = t.matmul(pb[li * 64:(li + 1) * 64, col:col + 128],
                                   lh, rh, start=True, stop=True,
                                   tile_position=(0, li * 64))
            return ins
        b.do("tensor", f, waits=("s_sq1", "s_sq2", "s_kT", "s_vT"), inc="s_stg")

        def f(v):
            ins = None
            for li, m in ((0, 8), (1, 4)):
                nq = 128 // (2 * m)
                for bk in range(2):
                    for half in range(2):
                        oap = bass.AP(wbig, (li * 64) * 73728 + WA + 30720
                                      + bk * 1024 + half * 128 + m,
                                      [[73728, 64], [256, 4], [2 * m, nq], [1, m]])
                        iap = bass.AP(ps[bk], (li * 64) * 512 + half * m,
                                      [[512, 64], [128, 4], [2 * m, nq], [1, m]])
                        map_ = bass.AP(stgm, (li * 64) * 64,
                                       [[64, 64], [0, 4], [m, nq], [1, m]])
                        ins = v.tensor_tensor(oap, iap, map_, ALU.mult)
            return ins
        b.do("vector", f, waits=("s_stg",), inc="s_stge")
        if cut(4):
            return

        def f(t):
            ins = None
            for bank in range(4):
                hhs = list(range(bank * 4, bank * 4 + 4))

                def scat_mm(hh, start, stop):
                    sg = stgb[:, hh * 128:(hh + 1) * 128]
                    hcol = (hh % 4) * 128
                    return t.matmul(ps[bank][:, hcol:hcol + 128], zscat[:, :], sg,
                                    start=start, stop=stop)

                def rects(hh, close):
                    ch, half = hh // 2, hh % 2
                    hcol = (hh % 4) * 128
                    for m in (64, 32):
                        nb = 64 // m
                        for qq in range(nb):
                            j0, i0 = qq * 2 * m, qq * 2 * m + m
                            lh = t2ap(skL_o + SLOFF[m] + ch * 64 + qq * m, [[1, m]])
                            rh = t2ap(LOFF2[m] + ch * 128 + qq * 2 * m + half * m,
                                      [[1, m]])
                            t.matmul(ps[bank][j0:j0 + m, hcol + i0:hcol + i0 + m],
                                     lh, rh, start=False, stop=False,
                                     tile_position=(0, j0))
                    for qq in range(3):
                        j0, i0 = qq * 32, qq * 32 + 16
                        lh = t2ap(skL_o + SLOFF[16] + ch * 96 + qq * 16, [[1, 16]])
                        rh = t2ap(LOFF2[16] + ch * 128 + qq * 32 + half * 16,
                                  [[1, 16]])
                        t.matmul(ps[bank][j0:j0 + 16, hcol + i0:hcol + i0 + 16],
                                 lh, rh, start=False, stop=False,
                                 tile_position=(0, j0))
                    lh = t2ap(skL_o + SLOFF[16] + ch * 96 + 48, [[1, 48]])
                    rh = t2ap(LOFF2[16] + ch * 128 + 96 + half * 16, [[1, 16]])
                    return t.matmul(ps[bank][64:112, hcol + 112:hcol + 128],
                                    lh, rh, start=False, stop=close,
                                    tile_position=(0, 64))

                ins = scat_mm(hhs[0], True, False)
                ins = scat_mm(hhs[1], False, False)
                ins = scat_mm(hhs[2], False, False)
                for hh in hhs[:3]:
                    ins = rects(hh, False)
                ins = scat_mm(hhs[3], False, False)
                ins = rects(hhs[3], False)
                ins = t.matmul(ps[bank][:, 0:1], t2[0:1, 15424:15552],
                               t2[0:1, 15424:15425], start=False, stop=True)
            return ins
        b.do("tensor", f, waits=("s_sq1", "s_sq2", "s_kT", "s_vT", "s_stge"),
             inc="s_scores")
        if cut(5):
            return

        def f(v):
            ins = None
            for bank in range(4):
                ins = v.tensor_tensor(
                    tri[:, bank * 512:(bank + 1) * 512].rearrange(
                        "p (a c) -> p a c", c=128),
                    ps[bank][:, 0:512].rearrange("p (a c) -> p a c", c=128),
                    bass.AP(trim, 0, [[128, 128], [0, 4], [1, 128]]), ALU.mult)
            return ins
        b.do("vector", f, waits=("s_scores", "f_tri", "s_dbg"), inc="s_tri")
        b.do("vector", lambda v: v.engine_nop(), inc="f_sc")
        b.do("vector", lambda v: v.engine_nop(), inc="f_sq")

        def f(t):
            ins = None
            for hh in range(16):
                pb = ps[4] if hh < 8 else ps[6]
                col = (hh % 8) * 64
                ins = t.matmul(pb[:, col:col + 64],
                               wbap(WA + 28672 + hh * 128, [[1, 128]]),
                               vTc[:, hh * 64:(hh + 1) * 64],
                               start=False, stop=(hh % 8 == 7))
            return ins
        b.do("tensor", f, waits=("s_tri", "s_bmm"), inc="s_apply")
        b.do("vector", lambda v: v.engine_nop(), waits=("s_apply",), inc="f_kvT")
        b.do("vector", lambda v: v.engine_nop(), waits=("s_apply",), inc="f_mixv")

        def f(v):
            v.tensor_copy(tri[:, 0:512], ps[4][:, 0:512])
            return v.tensor_copy(tri[:, 512:1024], ps[6][:, 0:512])
        b.do("vector", f, waits=("s_apply", "s_ytr"), inc="s_ytm")
        b.do("vector", lambda v: v.engine_nop(), inc="f_ps4")
        b.do("vector", lambda v: v.engine_nop(), inc="f_ps6")
        if cut(6):
            return

        if debug:
            t0d = (cc - 32) * L
            hdbg = b.pre("s_dbg", 48)
            def fd(g, t0d=t0d, hdbg=hdbg):
                ccc = cc - 32
                doff = (ccc // 4) * 128 * TH + (ccc % 4) * 1024
                g.dma_start(bass.AP(ddbg["dby"], doff, [[TH, 128], [1, 1024]]),
                            tri[:, 0:1024]).then_inc(hdbg, 16)
                g.dma_start(bass.AP(ddbg["dbkt"], doff, [[TH, 128], [1, 1024]]),
                            kTc[:, :]).then_inc(hdbg, 16)
                g.dma_start(bass.AP(ddbg["dbvt"], doff, [[TH, 128], [1, 1024]]),
                            vTc[:, :]).then_inc(hdbg, 16)
            wl1 = [(b.sems["s_ytm"][0], b.sems["s_ytm"][1]),
                   (b.sems["s_kT"][0], b.sems["s_kT"][1]),
                   (b.sems["s_vT"][0], b.sems["s_vT"][1])]
            def wdbg(g, wl1=wl1, fd=fd):
                for h, v in wl1:
                    g.wait_ge(h, v)
                fd(g)
            b.q["gpsimd"].append(wdbg)


        def tail():
            def f(a):
                return a.activation(gnsq[:, :], tri[:, 0:512], AF.Square)
            b.do("scalar", f, waits=("s_ytm", "s_gnr"), inc="s_ysq")

            def f(v):
                v.tensor_reduce(rowsG[:, 0:16].rearrange("p (a c) -> p a c", c=1),
                                wbap(WA + 28672, [[64, 16], [1, 64]]), AX.X, ALU.add)
                return v.tensor_reduce(rowsG[:, 16:24].rearrange("p (a c) -> p a c", c=1),
                                       bass.AP(gnsq, 0, [[512, 128], [64, 8], [1, 64]]),
                                       AX.X, ALU.add)
            b.do("vector", f, waits=("s_ysq",), inc="s_gnr")

            def f(a):
                return a.activation(gnsq[:, :], tri[:, 512:1024], AF.Square)
            b.do("scalar", f, waits=("s_ytm", "s_gnr"), inc="s_ysq")

            def f(v):
                return v.tensor_reduce(rowsG[:, 24:32].rearrange("p (a c) -> p a c", c=1),
                                       bass.AP(gnsq, 0, [[512, 128], [64, 8], [1, 64]]),
                                       AX.X, ALU.add)
            b.do("vector", f, waits=("s_ysq",), inc="s_gnr")

            def f(v):
                v.tensor_scalar_mul(rowsG[:, 32:48], rowsG[:, 0:16], 1.0 / SD)
                return v.tensor_scalar_mul(rowsG[:, 48:64], rowsG[:, 16:32], 1.0 / SD)
            b.do("vector", f, waits=("s_gnr",), inc="s_gnv")

            def f(a):
                return a.activation(rowsG[:, 64:80], rowsG[:, 32:48], AF.Square)
            b.do("scalar", f, waits=("s_gnv",), inc="s_gna")

            def f(v):
                v.tensor_sub(rowsG[:, 48:64], rowsG[:, 48:64], rowsG[:, 64:80])
                return v.engine_nop()
            b.do("vector", f, waits=("s_gna",), inc="s_gnv")

            pga2 = b.chainpt("c_act")

            def f(a):
                a.activation(rowsG[:, 48:64], rowsG[:, 48:64], AF.Ln,
                             bias=smf[:, 129:130]).then_inc(pga2[0], 1)
                a.wait_ge(pga2[0], pga2[1])
                return a.activation(rowsG[:, 48:64], rowsG[:, 48:64], AF.Exp,
                                    scale=-0.5)
            b.do("scalar", f, waits=("s_gnv", "s_zero"), inc="s_gna")

            def f(v):
                return v.tensor_mul(rowsG[:, 32:48], rowsG[:, 32:48], rowsG[:, 48:64])
            b.do("vector", f, waits=("s_gna",), inc="s_gnv")
            if cut(7):
                return

            pga = b.chainpt("c_gp")

            def f(g):
                y3 = wbap(WA + 28672, [[64, 16], [1, 64]])
                g.tensor_tensor(y3, y3, bass.AP(rowsG, 48, [[80, 128], [1, 16], [0, 64]]),
                                ALU.mult).then_inc(pga[0], 1)
                g.wait_ge(pga[0], pga[1])
                return g.tensor_tensor(y3, y3,
                                       bass.AP(rowsG, 32, [[80, 128], [1, 16], [0, 64]]),
                                       ALU.subtract)
            b.do("gpsimd", f, waits=("s_gnv", "s_dbg"), inc="s_gnap")

            def f(t):
                ins = None
                for sl in range(8):
                    ins = t.transpose(pbt[:, sl * 128:(sl + 1) * 128],
                                      tri[:, sl * 128:(sl + 1) * 128], ident[:, :])
                return ins
            b.do("tensor", f, waits=("s_gnap", "f_pbt"), inc="s_ytr")

            def f(a):
                return a.activation(tri[:, 1024:2048], pbt[:, 0:1024], AF.Copy)
            b.do("scalar", f, waits=("s_ytr",), inc="s_yfm")
            b.do("scalar", lambda a: a.nop(), inc="f_pbt")

            def f(a):
                ins = None
                for ch in range(NCH):
                    sl = tri[:, 1024 + ch * 128:1024 + (ch + 1) * 128]
                    ins = a.activation(sl, sl, AF.Identity, bias=LNB[:, ch:ch + 1],
                                       scale=LNW[:, ch:ch + 1])
                return ins
            b.do("scalar", f, waits=("s_yfm",), inc="s_yga")

            def f(g):
                ins = None
                for ch in range(NCH):
                    sl = tri[:, 1024 + ch * 128:1024 + (ch + 1) * 128]
                    ins = g.tensor_tensor(
                        ygF[:, ch * 512 + c0:ch * 512 + c0 + 128], sl,
                        gF[:, ch * 512 + c0:ch * 512 + c0 + 128], ALU.mult)
                return ins
            b.do("gpsimd", f, waits=("s_yga", "s_pjd", "f_ygF"), inc="s_yg")
            b.do("gpsimd", lambda g: g.engine_nop(), inc="f_tri")

            if ci == CPT - 1:
                b.do("vector", lambda v: v.engine_nop(), waits=("s_yg",), inc="f_scr")

        return tail
    def oproj_tile(ti, ph):
        t0 = ti * TILE
        for co in range(NCH):
            bank = co % 2

            def f(t, co=co, bank=bank):
                for ci in range(NCH):
                    ins = t.matmul(ps[bank][:, 0:512],
                                   W_o[:, ci * 1024 + co * 128:ci * 1024 + co * 128 + 128],
                                   ygF[:, ci * 512:(ci + 1) * 512],
                                   start=(ci == 0), stop=(ci == NCH - 1))
                return ins
            b.do("tensor", f, waits=("s_yg", f"f_pj{bank}", "f_sc"), inc="s_mm")

            def f(v, co=co, bank=bank):
                return v.tensor_copy(attf[:, 0:512], ps[bank][:, 0:512])
            b.do("vector", f, waits=("s_mm", "f_at"), inc="s_ev")
            b.do("vector", lambda v: v.engine_nop(), inc=f"f_pj{bank}")

            def f(g, co=co, bank=bank):
                return g.dma_start(
                    bass.AP(d_x1, co * 128 * TH + t0, [[TH, 128], [1, 512]]),
                    attf[:, 0:512], accum_op=ALU.add)
            b.do("gpsimd", f, waits=("s_ev", "s_x1pre"), inc="s_x1acc", k=16)
            b.do("gpsimd", lambda g: g.engine_nop(), waits=("s_x1acc",),
                 inc="f_at")
        if debug:
            hdbg2 = b.pre("s_dbg", 64)
            wl2 = [(b.sems["s_yg"][0], b.sems["s_yg"][1]),
                   (b.sems["s_pjd"][0], b.sems["s_pjd"][1])]
            def fd(g, t0=t0, hdbg2=hdbg2, wl2=wl2):
                for h, v in wl2:
                    g.wait_ge(h, v)
                ap3 = [[TH, 128], [TH * 128, 8], [1, 512]]
                g.dma_start(bass.AP(ddbg["dbr"], t0, ap3),
                            rF.rearrange("p (a c) -> p a c", c=512)).then_inc(hdbg2, 16)
                g.dma_start(bass.AP(ddbg["dbk"], t0, ap3),
                            kF.rearrange("p (a c) -> p a c", c=512)).then_inc(hdbg2, 16)
                g.dma_start(bass.AP(ddbg["dbg"], t0, ap3),
                            gF.rearrange("p (a c) -> p a c", c=512)).then_inc(hdbg2, 16)
                g.dma_start(bass.AP(ddbg["dbyg"], t0, ap3),
                            ygF.rearrange("p (a c) -> p a c", c=512)).then_inc(hdbg2, 16)
            b.q["gpsimd"].append(fd)
        b.do("tensor", lambda t: t.nop(), waits=("s_mm",), inc="f_ygF")
        b.do("tensor", lambda t: t.nop(), waits=("s_mm",), inc="f_mix")

        def f(s):
            return s.dma_start(bass.AP(d_out, t0, [[TH, 128], [TH * 128, 8], [1, 512]]),
                               bass.AP(d_x1, t0, [[TH, 128], [TH * 128, 8], [1, 512]]))
        b.do("sync", f, waits=("s_x1acc",), inc="s_outpre", k=16)

    def ffn_tile(ti):
        t0 = ti * TILE
        ph = "fn"
        ln_xhat(ti, d_x1, ph, x1b, 0, hb[:, 0:4096], 4096, "t2")

        for co in range(24):
            bk = co % 2

            def f(t, co=co, bk=bk):
                for ci in range(NCH):
                    ins = t.matmul(ps[bk][:, 0:512],
                                   Fg[:, ci * FFH + co * 128:ci * FFH + co * 128 + 128],
                                   x1b[:, ci * 512:(ci + 1) * 512],
                                   start=(ci == 0), stop=(ci == NCH - 1))
                return ins
            b.do("tensor", f, waits=("s_xhat", f"f_fg{bk}", "s_fw",
                                     "f_sc", "f_pj0", "f_pj1"),
                 inc="s_mm")

            def f(a, co=co, bk=bk):
                return a.activation(hb[:, co * 512:(co + 1) * 512], ps[bk][:, 0:512],
                                    AF.Silu, scale=1.0 / 127.0)
            b.do("scalar", f, waits=("s_mm",), inc=f"f_fg{bk}")

            def f(t, co=co, bk=bk):
                for ci in range(NCH):
                    ins = t.matmul(ps[2 + bk][:, 0:512],
                                   Fu[:, ci * FFH + co * 128:ci * FFH + co * 128 + 128],
                                   x1b[:, ci * 512:(ci + 1) * 512],
                                   start=(ci == 0), stop=(ci == NCH - 1))
                return ins
            b.do("tensor", f, waits=("s_xhat", f"f_fu{bk}", "s_fw",
                                     "s_kT", "s_vT"),
                 inc="s_mm")

            def f(v, co=co, bk=bk):
                sl = hb[:, co * 512:(co + 1) * 512]
                return v.tensor_tensor(sl, sl, ps[2 + bk][:, 0:512], ALU.mult)
            b.do("vector", f, waits=("s_mm", f"f_fg{bk}"), inc=f"f_fu{bk}")
        b.do("vector", lambda v: v.engine_nop(), waits=(f"f_fu0", "f_fu1"),
             inc="s_h")

        for co in range(NCH):
            bk = 4 + co % 2

            def f(t, co=co, bk=bk):
                for ci in range(24):
                    ins = t.matmul(ps[bk][:, 0:512],
                                   Fd[:, ci * 1024 + co * 128:ci * 1024 + co * 128 + 128],
                                   hb[:, ci * 512:(ci + 1) * 512],
                                   start=(ci == 0), stop=(ci == 23))
                return ins
            b.do("tensor", f, waits=("s_h", f"f_fd{bk}", "f_ps4", "f_ps5"),
                 inc="s_mm")

            def f(v, co=co, bk=bk):
                return v.tensor_scalar_mul(attf[:, 0:512],
                                           ps[bk][:, 0:512], 1.0 / (127.0 * 127.0))
            b.do("vector", f, waits=("s_mm", "f_fat"), inc="s_ev")
            b.do("vector", lambda v: v.engine_nop(), inc=f"f_fd{bk}")

            def f(g, co=co, bk=bk):
                return g.dma_start(
                    bass.AP(d_out, co * 128 * TH + t0, [[TH, 128], [1, 512]]),
                    attf[:, 0:512], accum_op=ALU.add)
            b.do("gpsimd", f, waits=("s_ev", "s_outpre"), inc="s_outacc", k=16)
            b.do("gpsimd", lambda g: g.engine_nop(), waits=("s_outacc",),
                 inc="f_fat")
        b.do("vector", lambda v: v.engine_nop(), waits=("s_h",),
             inc="f_xh")
        b.do("vector", lambda v: v.engine_nop(), waits=("s_mm",),
             inc="f_scr")

    setup()
    b.do("gpsimd", lambda g: g.memset(onesb[:, :], 1.0), inc="s_onesb")

    cc = 0
    npre = NTIL if trunc != 1 else 0
    for ti in range(npre):
        ln_xhat(ti, d_px, "pre", xh, 0, rF, 0, "wb")

        def fbc(g, sl=ti % 2):
            return g.tensor_copy(bcolb[:, sl * 8:sl * 8 + 8], xh[:, 511::512])
        b.do("gpsimd", fbc, waits=("s_xhat",), inc="s_bcol")
        mix_full("pre", mixk, 1, "s_xhat", inc="s_mixk", bslot=(ti + 1) % 2,
                 extraw=("s_bcol",))
        mix_full("pre", mixr, 2, "s_xhat", inc="s_mixv", bslot=(ti + 1) % 2)
        b.do("gpsimd", lambda g: g.engine_nop(),
             waits=("s_mixk", "s_mixv"), inc="f_xh")
        for ci in range(CPT):
            scan_chunk(cc, ci, "pre", False)
            cc += 1
        b.do("tensor", lambda t: t.nop(), waits=("s_tmproj",), inc="f_mix")

    if trunc in (1, 2):
        nmain = 0
    elif 31 <= trunc <= 49:
        nmain = 1
    else:
        nmain = NTIL
    pend = None
    for ti in range(nmain):
        ln_xhat(ti, d_mx, "mn", xh, 0, rF, 0, "wb")

        def fbc(g, sl=ti % 2):
            return g.tensor_copy(bcolb[:, sl * 8:sl * 8 + 8], xh[:, 511::512])
        b.do("gpsimd", fbc, waits=("s_xhat",), inc="s_bcol")
        mix_full("mn", mixr, 0, "s_xhat", inc="s_mixr", bslot=(ti + 1) % 2,
                 extraw=("s_bcol",))
        mix_full("mn", mixg, 3, "s_xhat", inc="s_mixg", bslot=(ti + 1) % 2)
        mix_full("mn", mixk, 1, "s_xhat", inc="s_mixk", bslot=(ti + 1) % 2)
        if trunc == 31:
            break
        fm_proj("mn", "r", W_r, mixr, rF, AF.Copy, ("s_mixr", "f_sc"))
        fm_proj("mn", "g", W_g, mixg, gF, AF.Silu, ("s_mixg", "f_sc"))
        fm_proj("mn", "k", W_k, mixk, kF, AF.Copy, ("s_mixk", "f_sc"))
        mix_full("mn", mixr, 2, "s_xhat", inc="s_mixv", bslot=(ti + 1) % 2,
                 extraw=("s_pjd",))
        b.do("gpsimd", lambda g: g.engine_nop(),
             waits=("s_mixr", "s_mixg", "s_mixk", "s_mixv"), inc="f_xh")
        if trunc == 32:
            break
        nchunks = 1 if (trunc == 33 or 41 <= trunc <= 48) else CPT
        for ci in range(nchunks):
            pend = scan_chunk(cc, ci, "mn", True, pending=pend)
            cc += 1
        if trunc in (33, 34) or 41 <= trunc <= 48:
            break
        if pend is not None:
            pend()
            pend = None
        oproj_tile(ti, "mn")

    if trunc in (0, 4):
        fw_w = ("s_yg", "f_tri", "s_mm", "s_dbg")
        b.do("gpsimd", lambda g: g.dma_start(
            Fg.rearrange("p (a c) -> p a c", c=FFH),
            bass.AP(d_fg, 0, [[FFH, 128], [FFH * 128, 8], [1, FFH]])),
            waits=fw_w, inc="s_fw", k=16)
        b.do("gpsimd", lambda g: g.dma_start(
            Fu.rearrange("p (a c) -> p a c", c=FFH),
            bass.AP(d_fu, 0, [[FFH, 128], [FFH * 128, 8], [1, FFH]])),
            inc="s_fw", k=16)
        b.do("gpsimd", lambda g: g.dma_start(
            Fd.rearrange("p (a c) -> p a c", c=1024),
            bass.AP(d_fd, 0, [[1024, 128], [1024 * 128, 24], [1, 1024]])),
            inc="s_fw", k=16)
        b.do("vector", lambda v: v.engine_nop(), waits=("s_x1acc",), inc="f_xh")
        b.do("vector", lambda v: v.engine_nop(), waits=("s_x1acc",), inc="f_scr")

    nffn = NTIL if trunc == 0 else 0
    for ti in range(nffn):
        ffn_tile(ti)

    if trunc:
        b.waitonly("sync", *list(b.sems))
    else:
        b.waitonly("sync", "s_outacc", "s_outpre", "s_x1acc")

    with nc.Block() as blk:
        blk.sync(lambda e: [f(e) for f in b.q["sync"]] and None)
        blk.gpsimd(lambda e: [f(e) for f in b.q["gpsimd"]] and None)
        blk.tensor(lambda e: [f(e) for f in b.q["tensor"]] and None)
        blk.vector(lambda e: [f(e) for f in b.q["vector"]] and None)
        blk.scalar(lambda e: [f(e) for f in b.q["scalar"]] and None)
    st.close()
    return nc


def _consts():
    con = np.zeros((128, 1024), np.float32)
    con[:, 0:128] = np.eye(128, dtype=np.float32)
    tm = np.zeros((128, 128), np.float32)
    for m in LEVELS:
        for q in range(0, 128, 2 * m):
            tm[q:q + m, q + m:q + 2 * m] = 1.0
    con[:, 128:256] = tm
    cv = np.zeros((128, 64), np.float32)
    t = np.arange(128)
    for d in range(4):
        if d == 0:
            val = np.ones(128)
        else:
            val = ((t + d) <= 127) & (((t + d) % 4) >= d)
        cv[:, d * 16:(d + 1) * 16] = val.astype(np.float32)[:, None]
    con[:, 256:320] = cv
    for d in range(4):
        z = np.zeros((128, 128), np.float32)
        j = np.arange(128 - d)
        z[j, j + d] = 1.0
        con[:, 320 + d * 128:320 + (d + 1) * 128] = z
    zs = np.zeros((128, 128), np.float32)
    for li, m in enumerate((8, 4)):
        for p in range(64):
            q, jj = p // m, p % m
            zs[li * 64 + p, 2 * m * q + jj] = 1.0
    con[:, 832:960] = zs
    sm = np.zeros((128, 64), np.float32)
    for li, m in enumerate((8, 4)):
        for p in range(64):
            qj = p // m
            sm[li * 64 + p, qj * m:(qj + 1) * m] = 1.0
    con[:, 960:1024] = sm
    return con


_CACHE = {}


def kernel(**inputs):
    x0 = np.ascontiguousarray(np.asarray(inputs["L0_attn_xq"], np.float32))
    wT = {z: np.ascontiguousarray(np.asarray(inputs[f"L0_attn_W{z}"], np.float32).T)
          for z in ("r", "k", "v", "g", "o")}
    fgT = np.ascontiguousarray(np.asarray(inputs["L0_ffn_w_gate"], np.int32).T)
    fuT = np.ascontiguousarray(np.asarray(inputs["L0_ffn_w_up"], np.int32).T)
    fdT = np.ascontiguousarray(np.asarray(inputs["L0_ffn_w_down"], np.int32).T)
    mix = np.ascontiguousarray(np.stack([
        np.asarray(inputs["L0_attn_time_mix_r"], np.float32).ravel(),
        np.asarray(inputs["L0_attn_time_mix_k"], np.float32).ravel(),
        np.asarray(inputs["L0_attn_time_mix_v"], np.float32).ravel(),
        np.asarray(inputs["L0_attn_time_mix_g"], np.float32).ravel()]))
    dec = np.ascontiguousarray(np.asarray(inputs["L0_attn_time_decay"], np.float32).ravel())
    faa = np.ascontiguousarray(np.asarray(inputs["L0_attn_time_faaaa"], np.float32).ravel())
    lnw = np.ascontiguousarray(np.asarray(inputs["L0_attn_ln_w"], np.float32).ravel())
    lnb = np.ascontiguousarray(np.asarray(inputs["L0_attn_ln_b"], np.float32).ravel())
    con = _consts()

    if "nc" not in _CACHE:
        _CACHE["nc"] = build_program()
    nc = _CACHE["nc"]

    zero = np.zeros((C, TH), np.float32)
    in_maps = []
    for core in range(8):
        bb, j = core // 2, core % 2
        xb = x0[bb]
        px = zero if j == 0 else np.ascontiguousarray(xb[0:TH].T)
        mx = np.ascontiguousarray(xb[j * TH:(j + 1) * TH].T)
        in_maps.append(dict(px=px, mx=mx,
                            wrT=wT["r"], wkT=wT["k"], wvT=wT["v"], wgT=wT["g"],
                            woT=wT["o"], fgT=fgT, fuT=fuT, fdT=fdT, mix=mix,
                            dec=dec, faa=faa, lnw=lnw, lnb=lnb, consts=con))
    res = run_bass_kernel_spmd(nc, in_maps, core_ids=list(range(8)))
    out = np.empty((4, 8192, 1024), np.float32)
    for core in range(8):
        bb, j = core // 2, core % 2
        out[bb, j * TH:(j + 1) * TH] = res.results[core]["out"].T
    return out


if __name__ == "__main__":
    nc = build_program()
    print("built ok; instructions:", nc.n_instructions())


# version 14 (final)
# speedup vs baseline: 1.0305x; 1.0172x over previous
import numpy as np
import ml_dtypes
from contextlib import ExitStack

import concourse.bass as bass
import concourse.mybir as mybir
from concourse.bass_utils import run_bass_kernel_spmd

F32 = mybir.dt.float32
BF16 = mybir.dt.bfloat16
I32 = mybir.dt.int32
AF = mybir.ActivationFunctionType
ALU = mybir.AluOpType
AX = mybir.AxisListType
BF = ml_dtypes.bfloat16

C, H, SD, NCH = 1024, 16, 64, 8
TH, L, TILE = 4096, 128, 512
NTIL, CPT = TH // TILE, TILE // L
FFH = 3072
EPS = 1e-5
GN_EPS = 64.0 * 1e-5
LEVELS = (64, 32, 16, 8, 4)
LOFF = {64: 0, 32: 512, 16: 1024, 8: 1536, 4: 2048}
WA = 40960


class B:
    def __init__(self, nc, stack):
        self.nc, self.st = nc, stack
        self.sems = {}
        self.q = {e: [] for e in ("sync", "gpsimd", "tensor", "vector", "scalar")}

    def sem(self, name):
        if name not in self.sems:
            self.sems[name] = [self.st.enter_context(self.nc.semaphore(name)), 0]
        return self.sems[name]

    def do(self, eng, fn, waits=(), inc=None, k=1):
        wl = []
        for wn in waits:
            e = self.sem(wn)
            if e[1] > 0:
                wl.append((e[0], e[1]))
        hk = None
        if inc is not None:
            e = self.sem(inc)
            e[1] += k
            hk = (e[0], k)

        def run(eh):
            for h, v in wl:
                eh.wait_ge(h, v)
            ins = fn(eh)
            if hk is not None:
                ins.then_inc(hk[0], hk[1])
        self.q[eng].append(run)

    def chainpt(self, name):
        e = self.sem(name)
        e[1] += 1
        return (e[0], e[1])

    def pre(self, name, k):
        e = self.sem(name)
        e[1] += k
        return e[0]

    def waitonly(self, eng, *names):
        wl = []
        for wn in names:
            e = self.sem(wn)
            if e[1] > 0:
                wl.append((e[0], e[1]))

        def run(eh):
            for h, v in wl:
                eh.wait_ge(h, v)
        self.q[eng].append(run)


def build_program(debug=False, trunc=0):
    nc = bass.Bass("TRN2", target_bir_lowering=False)
    st = ExitStack()
    b = B(nc, st)

    d_px = nc.dram_tensor("px", [C, TH], F32, kind="ExternalInput")
    d_mx = nc.dram_tensor("mx", [C, TH], F32, kind="ExternalInput")
    dW = {z: nc.dram_tensor(z, [C, C], F32, kind="ExternalInput")
          for z in ("wrT", "wkT", "wvT", "wgT", "woT")}
    d_fg = nc.dram_tensor("fgT", [C, FFH], I32, kind="ExternalInput")
    d_fu = nc.dram_tensor("fuT", [C, FFH], I32, kind="ExternalInput")
    d_fd = nc.dram_tensor("fdT", [FFH, C], I32, kind="ExternalInput")
    d_mix = nc.dram_tensor("mix", [4, C], F32, kind="ExternalInput")
    d_dec = nc.dram_tensor("dec", [C], F32, kind="ExternalInput")
    d_faa = nc.dram_tensor("faa", [C], F32, kind="ExternalInput")
    d_lnw = nc.dram_tensor("lnw", [C], F32, kind="ExternalInput")
    d_lnb = nc.dram_tensor("lnb", [C], F32, kind="ExternalInput")
    d_con = nc.dram_tensor("consts", [128, 1024], F32, kind="ExternalInput")
    d_x1 = nc.dram_tensor("x1buf", [C, TH], F32)
    d_out = nc.dram_tensor("out", [C, TH], F32, kind="ExternalOutput")
    ddbg = {}
    if debug:
        for nm in ("dbr", "dbk", "dbg", "dby", "dbyg", "dbkt", "dbvt"):
            ddbg[nm] = nc.dram_tensor(nm, [C, TH], F32, kind="ExternalOutput")

    wbig = st.enter_context(nc.sbuf_tensor("wbig", [128, 73728], BF16))
    t2 = st.enter_context(nc.sbuf_tensor("t2", [128, 19200], BF16))
    smf = st.enter_context(nc.sbuf_tensor("smf", [128, 132], F32))
    rowsA = st.enter_context(nc.sbuf_tensor("rowsA", [128, 1280], F32))
    rowsG = st.enter_context(nc.sbuf_tensor("rowsG", [128, 80], F32))
    ident = st.enter_context(nc.sbuf_tensor("identb", [128, 128], BF16))
    zsb = st.enter_context(nc.sbuf_tensor("zsb", [128, 384], BF16))
    trim = st.enter_context(nc.sbuf_tensor("trim", [128, 128], BF16))
    cval = st.enter_context(nc.sbuf_tensor("cval", [128, 64], F32))
    stgm = st.enter_context(nc.sbuf_tensor("stgm", [128, 64], F32))
    selband = st.enter_context(nc.sbuf_tensor("selband", [128, 512], BF16))
    pqs = {m: st.enter_context(nc.sbuf_tensor(f"pq{m}", [128, 8 * m], BF16))
           for m in LEVELS}
    pks = {m: st.enter_context(nc.sbuf_tensor(f"pk{m}", [128, 8 * m], BF16))
           for m in LEVELS}
    pq_int = st.enter_context(nc.sbuf_tensor("pq_int", [128, 1024], BF16))
    pk_tm = st.enter_context(nc.sbuf_tensor("pk_tm", [128, 1024], BF16))
    sstate = st.enter_context(nc.sbuf_tensor("sstate", [128, 512], F32))
    sstateb = st.enter_context(nc.sbuf_tensor("sstateb", [128, 2048], BF16))
    bcolb = st.enter_context(nc.sbuf_tensor("bcolb", [128, 16], BF16))
    attf = st.enter_context(nc.sbuf_tensor("attf", [128, 512], F32))
    onesb = st.enter_context(nc.sbuf_tensor("onesb", [128, 1], BF16))
    zscat = st.enter_context(nc.sbuf_tensor("zscat", [128, 128], BF16))
    gnsq = st.enter_context(nc.sbuf_tensor("gnsq", [128, 512], BF16))
    ps = [st.enter_context(nc.psum_tensor(f"ps{i}", [128, 512], F32)) for i in range(7)]
    pbt = st.enter_context(nc.psum_tensor("pbt", [128, 1024], BF16))

    def wvv(off, n):
        return wbig[:, off:off + n]

    W_r, W_k, W_v, W_g, W_o = (wvv(i * 8192, 8192) for i in range(5))
    xh = wvv(WA, 4096)
    mixr = wvv(WA + 4096, 4096)
    mixg = wvv(WA + 8192, 4096)
    rF = wvv(WA + 12288, 4096)
    kF = wvv(WA + 16384, 4096)
    gF = wvv(WA + 20480, 4096)
    ygF = wvv(WA + 24576, 4096)
    tri = wvv(WA + 28672, 2048)
    stgb = wvv(WA + 30720, 2048)
    Fg, Fu, Fd = wvv(0, 24576), wvv(24576, 24576), wvv(49152, 24576)
    sqL_o, skL_o, sqi_o, pdb_o = 0, 16384, 5120, 6144
    LOFF2 = {64: 0, 32: 1024, 16: 2048, 8: 3072, 4: 4096}
    SLOFF = {64: 0, 32: 512, 16: 1024, 8: 1792, 4: 2304}
    mixk = t2[:, 7168:11264]
    mixvc = t2[:, 11264:12288]
    kTc = t2[:, 12288:13312]
    vTc = t2[:, 13312:14336]
    vsh = t2[:, 14336:15360]
    csb = t2[:, 15360:15424]
    sqi = t2[:, 5120:6144]
    pdb = t2[:, 6144:7168]
    x1b = t2[:, 0:4096]
    hb = t2[:, 4096:16384]

    M_R, M_K, M_V, M_G = (smf[:, 8 * i:8 * i + 8] for i in range(4))
    LNW, LNB = smf[:, 32:40], smf[:, 40:48]
    DEC, UU = smf[:, 48:56], smf[:, 56:64]
    LW, W1 = smf[:, 64:72], smf[:, 72:80]
    W2, W128 = smf[:, 80:88], smf[:, 88:96]
    M1 = {0: smf[:, 96:104], 1: smf[:, 104:112], 2: smf[:, 112:120], 3: smf[:, 120:128]}
    MM = {0: M_R, 1: M_K, 2: M_V, 3: M_G}

    def t2ap(off, dims, p0=0, np_=128):
        return bass.AP(t2, p0 * 19200 + off, [[19200, np_]] + dims)

    def wbap(off, dims, p0=0, np_=128):
        return bass.AP(wbig, p0 * 73728 + off, [[73728, np_]] + dims)

    def setup():
        def f(g):
            g.memset(t2[:, 15424:16384], 0)
            g.memset(t2[:, 0:5120], 0)
            g.memset(t2[:, 17408:18176], 0)
            g.memset(smf[:, 130:131], 1.0)
            g.memset(stgb[:, :], 0)
            g.memset(selband[:, :], 0)
            g.memset(pdb[:, :], 0)
            g.memset(csb[:, :], 0)
            g.memset(bcolb[:, :], 0)
            g.memset(rowsA[0:1, 1024:1152], 1.0)
            g.memset(smf[:, 128:129], EPS)
            g.memset(smf[:, 129:130], GN_EPS)
            g.memset(rowsA[0:1, 1152:1280], -1.0)
            return g.engine_nop()
        b.do("gpsimd", f, waits=("s_small",), inc="s_zero")

        def f(v):
            v.memset(sstate[:, :], 0)
            v.memset(sstateb[:, :], 0)
            for i in range(7):
                v.memset(ps[i][:, :], 0)
            return v.engine_nop()
        b.do("vector", f, inc="s_zero")

        h_small = b.pre("s_small", 16 * 11)

        def f(s):
            s.dma_start(cval[:, :], bass.AP(d_con, 256, [[1024, 128], [1, 64]])).then_inc(h_small, 16)
            s.dma_start(stgm[:, :], bass.AP(d_con, 960, [[1024, 128], [1, 64]])).then_inc(h_small, 16)
            for z in range(4):
                s.dma_start(smf[:, z * 8:z * 8 + 8],
                            bass.AP(d_mix, z * C, [[1, 128], [128, 8]]),
                            allow_slow_non_contiguous=True).then_inc(h_small, 16)
            s.dma_start(LNW, bass.AP(d_lnw, 0, [[1, 128], [128, 8]]), allow_slow_non_contiguous=True).then_inc(h_small, 16)
            s.dma_start(LNB, bass.AP(d_lnb, 0, [[1, 128], [128, 8]]), allow_slow_non_contiguous=True).then_inc(h_small, 16)
            s.dma_start(DEC, bass.AP(d_dec, 0, [[1, 128], [128, 8]]), allow_slow_non_contiguous=True).then_inc(h_small, 16)
            s.dma_start(UU, bass.AP(d_faa, 0, [[1, 128], [128, 8]]), allow_slow_non_contiguous=True).then_inc(h_small, 16)
            s.dma_start(rowsA[0:1, 0:1024], bass.AP(d_dec, 0, [[0, 1], [1, 1024]])).then_inc(h_small, 16)
            return None

        b.q["sync"].append(f)

        b.do("sync", lambda s: s.dma_start(d_x1.ap(), d_mx.ap()), inc="s_x1pre", k=16)

        h_wl = b.pre("s_wload", 16 * 9)

        def f(g):
            sems_w = b.sems["s_small"]
            g.wait_ge(sems_w[0], sems_w[1])
            g.dma_start(ident[:, :], bass.AP(d_con, 0, [[1024, 128], [1, 128]])).then_inc(h_wl, 16)
            g.dma_start(zsb[:, :].rearrange("p (a c) -> p a c", c=128),
                        bass.AP(d_con, 448, [[1024, 128], [128, 3], [1, 128]])).then_inc(h_wl, 16)
            g.dma_start(zscat[:, :], bass.AP(d_con, 832, [[1024, 128], [1, 128]])).then_inc(h_wl, 16)
            g.dma_start(trim[:, :], bass.AP(d_con, 128, [[1024, 128], [1, 128]])).then_inc(h_wl, 16)
            for dram, view in ((dW["wrT"], W_r), (dW["wkT"], W_k), (dW["wvT"], W_v),
                               (dW["wgT"], W_g), (dW["woT"], W_o)):
                g.dma_start(view.rearrange("p (a c) -> p a c", c=1024),
                            bass.AP(dram, 0, [[1024, 128], [1024 * 128, 8], [1, 1024]])).then_inc(h_wl, 16)
        b.q["gpsimd"].append(f)

        p3 = b.chainpt("c_act")

        def f(a):
            a.activation(rowsA[0:1, 0:1024], rowsA[0:1, 0:1024],
                         AF.Exp).then_inc(p3[0], 1)
            a.wait_ge(p3[0], p3[1])
            return a.mul(rowsA[0:1, 0:1024], rowsA[0:1, 0:1024], -1.0)
        b.do("scalar", f, waits=("s_small",), inc="s_lwrow")

        def f(g):
            return g.iota(rowsG[:, 64:65], [[0, 1]], base=127, channel_multiplier=-1,
                          allow_small_or_imprecise_dtypes=True)
        b.do("gpsimd", f, inc="s_riota")

        for piece in (1, 0):
            def fmm(t, piece=piece):
                return t.matmul(ps[6][:, 0:512], rowsA[0:1, 1024:1152],
                                rowsA[0:1, piece * 512:piece * 512 + 512],
                                start=True, stop=True)
            b.do("tensor", fmm, waits=("s_lwrow", "s_lwbc", "s_zero"),
                 inc="s_rbmm")

            def fcp(a, piece=piece):
                return a.activation(rowsA[:, 512:1024], ps[6][:, 0:512], AF.Copy)
            b.do("scalar", fcp, waits=("s_rbmm", "s_pktm"), inc="s_lwbc")

            def f(a, piece=piece):
                return a.activation(pk_tm[:, piece * 512:piece * 512 + 512],
                                    rowsA[:, 512:1024], AF.Exp,
                                    scale=rowsG[:, 64:65])
            b.do("scalar", f, waits=("s_lwbc", "s_riota"), inc="s_pktm")

        p1 = b.chainpt("c_act")
        p2 = b.chainpt("c_act")

        def f(a):
            a.activation(rowsA[:, 128:136], DEC, AF.Exp).then_inc(p1[0], 1)
            a.wait_ge(p1[0], p1[1])
            a.mul(LW, rowsA[:, 128:136], -1.0).then_inc(p2[0], 1)
            a.wait_ge(p2[0], p2[1])
            a.activation(W1, LW, AF.Exp)
            a.activation(W2, LW, AF.Exp, scale=2.0)
            return a.activation(W128, LW, AF.Exp, scale=128.0)
        b.do("scalar", f, waits=("s_small", "s_pktm"), inc="s_lw")

        def f(v):
            for i in range(4):
                v.tensor_scalar(M1[i], MM[i], -1.0, 1.0, ALU.mult, ALU.add)
            return v.engine_nop()
        b.do("vector", f, waits=("s_small",), inc="s_m1")

        ROFF = {64: 0, 32: 64, 16: 96, 8: 112, 4: 120}

        pm1 = b.chainpt("c_gp")

        def f(g):
            g.memset(rowsA[0:1, 128:256], 0).then_inc(pm1[0], 1)
            g.wait_ge(pm1[0], pm1[1])
            for m in LEVELS:
                g.iota(rowsA[0:1, 128 + ROFF[m]:128 + ROFF[m] + m], [[0, 1], [1, m]],
                       channel_multiplier=0, allow_small_or_imprecise_dtypes=True)
            return g.engine_nop()
        b.do("gpsimd", f, waits=("s_small", "s_lw", "s_pktm"), inc="s_ramp")

        def fmm(t):
            return t.matmul(ps[6][:, 0:128], rowsA[0:1, 1024:1152],
                            rowsA[0:1, 128:256], start=True, stop=True)
        b.do("tensor", fmm, waits=("s_ramp", "s_zero", "s_lwbc"), inc="s_rbmm")

        def fcp(a):
            return a.activation(rowsA[:, 256:384], ps[6][:, 0:128], AF.Copy)
        b.do("scalar", fcp, waits=("s_rbmm", "s_pktm"), inc="s_rbc")

        def f(a):
            for m in LEVELS:
                for ch in range(NCH):
                    ins = a.activation(pqs[m][:, ch * m:(ch + 1) * m],
                                       rowsA[:, 256 + ROFF[m]:256 + ROFF[m] + m],
                                       AF.Exp, scale=LW[:, ch:ch + 1])
            return ins
        b.do("scalar", f, waits=("s_rbc", "s_lw"), inc="s_pow")

        pm2 = b.chainpt("c_gp")

        def f(g):
            g.memset(rowsA[0:1, 128:256], 0).then_inc(pm2[0], 1)
            g.wait_ge(pm2[0], pm2[1])
            for m in LEVELS:
                g.iota(rowsA[0:1, 128 + ROFF[m]:128 + ROFF[m] + m], [[0, 1], [-1, m]],
                       base=m - 1, channel_multiplier=0,
                       allow_small_or_imprecise_dtypes=True)
            return g.engine_nop()
        b.do("gpsimd", f, waits=("s_pow",), inc="s_ramp")

        def fmm(t):
            return t.matmul(ps[6][:, 0:128], rowsA[0:1, 1024:1152],
                            rowsA[0:1, 128:256], start=True, stop=True)
        b.do("tensor", fmm, waits=("s_ramp", "s_pow"), inc="s_rbmm")

        def fcp(a):
            return a.activation(rowsA[:, 256:384], ps[6][:, 0:128], AF.Copy)
        b.do("scalar", fcp, waits=("s_rbmm", "s_pow"), inc="s_rbc")

        def f(a):
            for m in LEVELS:
                for ch in range(NCH):
                    ins = a.activation(pks[m][:, ch * m:(ch + 1) * m],
                                       rowsA[:, 256 + ROFF[m]:256 + ROFF[m] + m],
                                       AF.Exp, scale=LW[:, ch:ch + 1])
            return ins
        b.do("scalar", f, waits=("s_rbc",), inc="s_pow")

        def f(g):
            return g.iota(rowsA[0:1, 128:256], [[0, 1], [1, 128]],
                          channel_multiplier=0, allow_small_or_imprecise_dtypes=True)
        b.do("gpsimd", f, waits=("s_pow",), inc="s_ramp")

        def fmm(t):
            return t.matmul(ps[6][:, 0:128], rowsA[0:1, 1024:1152],
                            rowsA[0:1, 128:256], start=True, stop=True)
        b.do("tensor", fmm, waits=("s_ramp", "s_pow"), inc="s_rbmm")

        def fcp(a):
            return a.activation(rowsA[:, 256:384], ps[6][:, 0:128], AF.Copy)
        b.do("scalar", fcp, waits=("s_rbmm", "s_pow"), inc="s_rbc")

        def f(a):
            for ch in range(NCH):
                ins = a.activation(pq_int[:, ch * 128:(ch + 1) * 128],
                                   rowsA[:, 256:384], AF.Exp,
                                   scale=LW[:, ch:ch + 1])
            return ins
        b.do("scalar", f, waits=("s_rbc",), inc="s_pow")

        b.do("vector", lambda v: v.engine_nop(), waits=("s_pow",), inc="f_ps6")

        def f(v):
            for ch in range(NCH):
                for d in range(4):
                    base = ch * 64 + d * 16
                    for half in range(2):
                        hh = 2 * ch + half
                        o = selband[64 * half:64 * half + 64, base + hh:base + hh + 1]
                        if d == 0:
                            v.tensor_copy(o, UU[64 * half:64 * half + 64, ch:ch + 1])
                        elif d == 1:
                            v.tensor_copy(o, smf[64 * half:64 * half + 64, 130:131])
                        elif d == 2:
                            v.tensor_copy(o, W1[64 * half:64 * half + 64, ch:ch + 1])
                        else:
                            v.tensor_copy(o, W2[64 * half:64 * half + 64, ch:ch + 1])
            return v.engine_nop()
        b.do("vector", f, waits=("s_zero", "s_lw"), inc="s_sel")

    def ln_xhat(ti, d_src, ph, dst, dst_o, scratch, scratch_o, arena):
        t0 = ti * TILE

        d3l = dst.rearrange("p (a c) -> p a c", c=512)

        def f(g):
            return g.dma_start(
                d3l[:, 0:4, :],
                bass.AP(d_src, t0, [[TH, 128], [TH * 128, 4], [1, 512]]))
        b.do("gpsimd", f, waits=("f_xh",), inc="s_xl", k=16)

        def f(g):
            return g.dma_start(
                d3l[:, 4:8, :],
                bass.AP(d_src, t0 + 4 * TH * 128, [[TH, 128], [TH * 128, 4], [1, 512]]))
        b.do("gpsimd", f, inc="s_xl2", k=16)

        if ph == "fn":
            for ch in range(NCH):
                def f(a, ch=ch):
                    return a.activation(gnsq[:, :], dst[:, ch * 512:(ch + 1) * 512],
                                        AF.Square)
                b.do("scalar", f,
                     waits=("s_xl" if ch < 4 else "s_xl2", "s_fsqm", "s_gnr"),
                     inc="s_fsq")

                def f(t, ch=ch):
                    return t.matmul(ps[6][32:33, 0:512], onesb[:, 0:1], gnsq[:, :],
                                    start=(ch == 0), stop=(ch == NCH - 1),
                                    tile_position=(0, 32))
                b.do("tensor", f, waits=("s_fsq", "f_ps6" if ch == 0 else "s_fsq",),
                     inc="s_fsqm")
        else:
            def f(a):
                return a.activation(scratch[:, 0:2048], dst[:, 0:2048], AF.Square)
            b.do("scalar", f, waits=("s_xl", "f_scr", "s_dbg"), inc="s_xsq")

            def f(a):
                return a.activation(scratch[:, 2048:4096], dst[:, 2048:4096],
                                    AF.Square)
            b.do("scalar", f, waits=("s_xl2",), inc="s_xsq")

        if ph == "pre":
            stat, ab, bb = ps[6], ps[4], ps[6]
            w_stat = ("s_xsq", "f_ps6", "s_onesb")
            w_musq = ("s_lrv", "f_ps4")
            w_bc = ("s_lrv", "f_ps4", "f_ps6")
            rel = ("f_ps4", "f_ps6")
        elif ph == "mn":
            stat, ab, bb = ps[2], ps[3], ps[2]
            w_stat = ("s_xsq", "s_vT", "s_onesb")
            w_musq = ("s_lrv", "s_vT")
            w_bc = ("s_lrv",)
            rel = ()
        else:
            stat, ab, bb = ps[6], ps[5], ps[4]
            w_stat = ("f_ps6", "s_onesb")
            w_musq = ("s_lrv", "f_ps5")
            w_bc = ("s_lrv", "f_ps4", "f_ps5")
            rel = ("f_ps4", "f_ps5", "f_ps6")

        if ph != "fn":
            h_xsq = b.sems["s_xsq"]

            def f(t, w1=h_xsq[1] - 1):
                t.wait_ge(h_xsq[0], w1)
                for ch in range(4):
                    t.matmul(stat[0:1, 0:512], onesb[:, 0:1],
                             dst[:, ch * 512:(ch + 1) * 512],
                             start=(ch == 0), stop=False)
                for ch in range(4):
                    ins = t.matmul(stat[32:33, 0:512], onesb[:, 0:1],
                                   scratch[:, ch * 512:(ch + 1) * 512],
                                   start=(ch == 0), stop=False,
                                   tile_position=(0, 32))
                return ins
            b.do("tensor", f, waits=tuple(w for w in w_stat if w != "s_xsq"),
                 inc="s_lnmm")

            def f(t):
                for ch in range(4, NCH):
                    t.matmul(stat[0:1, 0:512], onesb[:, 0:1],
                             dst[:, ch * 512:(ch + 1) * 512],
                             start=False, stop=(ch == NCH - 1))
                for ch in range(4, NCH):
                    ins = t.matmul(stat[32:33, 0:512], onesb[:, 0:1],
                                   scratch[:, ch * 512:(ch + 1) * 512],
                                   start=False, stop=(ch == NCH - 1),
                                   tile_position=(0, 32))
                return ins
            b.do("tensor", f, waits=("s_xsq",), inc="s_lnmm")
        else:
            def f(t):
                for ch in range(NCH):
                    ins = t.matmul(stat[0:1, 0:512], onesb[:, 0:1],
                                   dst[:, ch * 512:(ch + 1) * 512],
                                   start=(ch == 0), stop=(ch == NCH - 1))
                return ins
            b.do("tensor", f, waits=w_stat + ("s_fsqm",), inc="s_lnmm")

        def f(v):
            v.tensor_scalar_mul(rowsA[0:1, 0:512], stat[0:1, 0:512], 1.0 / C)
            return v.tensor_scalar_mul(rowsA[0:1, 512:1024], stat[32:33, 0:512], 1.0 / C)
        b.do("vector", f, waits=("s_lnmm", "s_pktm"), inc="s_lrv")
        if ph == "fn":
            b.do("vector", lambda v: v.engine_nop(), inc="f_ps6")

        def f(a):
            return a.activation(ab[0:1, 0:512], rowsA[0:1, 0:512], AF.Square)
        b.do("scalar", f, waits=w_musq, inc="s_lra")

        def f(v):
            v.tensor_sub(rowsA[0:1, 512:1024], rowsA[0:1, 512:1024], ab[0:1, 0:512])
            return v.engine_nop()
        b.do("vector", f, waits=("s_lra",), inc="s_lrv")

        pca = b.chainpt("c_act")

        def f(a):
            a.activation(rowsA[0:1, 512:1024], rowsA[0:1, 512:1024],
                         AF.Ln, bias=smf[0:1, 128:129]).then_inc(pca[0], 1)
            a.wait_ge(pca[0], pca[1])
            return a.activation(rowsA[0:1, 512:1024], rowsA[0:1, 512:1024],
                                AF.Exp, scale=-0.5)
        b.do("scalar", f, waits=("s_lrv",), inc="s_lra")

        def f(v):
            return v.tensor_mul(rowsA[0:1, 0:512], rowsA[0:1, 0:512],
                                rowsA[0:1, 512:1024])
        b.do("vector", f, waits=("s_lra",), inc="s_lrv")

        def f(t):
            t.matmul(ab[:, 0:512], rowsA[0:1, 1024:1152], rowsA[0:1, 512:1024],
                     start=True, stop=True)
            return t.matmul(bb[:, 0:512], rowsA[0:1, 1152:1280], rowsA[0:1, 0:512],
                            start=True, stop=True)
        b.do("tensor", f, waits=w_bc, inc="s_lnbc")

        pch = b.chainpt("c_dve")

        def f(v):
            a_bc = bass.AP(ab, 0, [[512, 128], [0, 8], [1, 512]])
            b_bc = bass.AP(bb, 0, [[512, 128], [0, 8], [1, 512]])
            d3 = dst.rearrange("p (a c) -> p a c", c=512)
            v.tensor_tensor(d3, d3, a_bc, ALU.mult).then_inc(pch[0], 1)
            v.wait_ge(pch[0], pch[1])
            return v.tensor_tensor(d3, d3, b_bc, ALU.add)
        b.do("vector", f, waits=("s_lnbc",), inc="s_xhat")
        for rr in rel:
            b.do("vector", lambda v: v.engine_nop(), waits=("s_xhat",), inc=rr)

    def mix_full(ph, out, mi, waitx, inc="s_mixf", extraw=(), bslot=0):
        def fa(a):
            o3 = out.rearrange("p (a c) -> p a c", c=512)
            x3 = xh.rearrange("p (a c) -> p a c", c=512)
            for ch in range(NCH):
                ins = a.activation(o3[:, ch:ch + 1, :], x3[:, ch:ch + 1, :],
                                   AF.Copy, scale=MM[mi][:, ch:ch + 1])
            return ins
        b.do("scalar", fa, waits=(waitx, "s_m1", "s_tmproj") + extraw, inc="s_mixa")

        def fv(v):
            o3 = out.rearrange("p (a c) -> p a c", c=512)
            x3 = xh.rearrange("p (a c) -> p a c", c=512)
            for ch in range(NCH):
                v.scalar_tensor_tensor(o3[:, ch:ch + 1, 1:512], x3[:, ch:ch + 1, 0:511],
                                       M1[mi][:, ch:ch + 1], o3[:, ch:ch + 1, 1:512],
                                       ALU.mult, ALU.add)
                ins = v.scalar_tensor_tensor(
                    o3[:, ch:ch + 1, 0:1],
                    bcolb[:, bslot * 8 + ch:bslot * 8 + ch + 1].unsqueeze(2),
                    M1[mi][:, ch:ch + 1], o3[:, ch:ch + 1, 0:1], ALU.mult, ALU.add)
            return ins
        b.do("vector", fv, waits=("s_mixa",), inc=inc)

    def fm_proj(ph, nm, W, src, dst, act, waits):
        for co in range(NCH):
            bank = co % 2

            def f(t, co=co, bank=bank):
                for ci in range(NCH):
                    ins = t.matmul(ps[bank][:, 0:512],
                                   W[:, ci * 1024 + co * 128:ci * 1024 + co * 128 + 128],
                                   src[:, ci * 512:(ci + 1) * 512],
                                   start=(ci == 0), stop=(ci == NCH - 1))
                return ins
            b.do("tensor", f, waits=waits + (f"f_pj{bank}",), inc="s_mm")

            def f(a, co=co, bank=bank):
                return a.activation(dst[:, co * 512:(co + 1) * 512],
                                    ps[bank][:, 0:512], act)
            b.do("scalar", f, waits=("s_mm",), inc=f"f_pj{bank}")
        b.do("tensor", lambda t: t.nop(), waits=("f_pj0", "f_pj1"),
             inc="s_pjd")

    def scan_chunk(cc, ci, ph, full, pending=None):
        def cut(n):
            return full and trunc == 40 + n
        c0 = ci * 128
        par = cc % 2

        def f(t):
            for cin in range(NCH):
                stk = mixk[:, cin * 512 + c0:cin * 512 + c0 + 128]
                for gs in range(2):
                    ins = t.matmul(ps[gs][:, 0:512], stk,
                                   W_k[:, cin * 1024 + gs * 512:cin * 1024 + gs * 512 + 512],
                                   start=(cin == 0), stop=(cin == NCH - 1))
            return ins
        b.do("tensor", f, waits=("s_mixk", "f_sc", "f_pj0", "f_pj1", "s_wload"),
             inc="s_tmk")

        def f(t):
            for cin in range(NCH):
                stv = mixr[:, cin * 512 + c0:cin * 512 + c0 + 128]
                for gs in range(2):
                    ins = t.matmul(ps[2 + gs][:, 0:512], stv,
                                   W_v[:, cin * 1024 + gs * 512:cin * 1024 + gs * 512 + 512],
                                   start=(cin == 0), stop=(cin == NCH - 1))
            return ins
        b.do("tensor", f, waits=("s_mixv", "s_tmk"), inc="s_tmproj")

        def f(v):
            v.tensor_tensor(kTc[:, 0:512], ps[0][:, 0:512], pk_tm[:, 0:512], ALU.mult)
            return v.tensor_tensor(kTc[:, 512:1024], ps[1][:, 0:512],
                                   pk_tm[:, 512:1024], ALU.mult)
        b.do("vector", f, waits=("s_tmk", "f_kvT", "s_pktm", "s_dbg"), inc="s_kT")

        def f(a):
            a.activation(vTc[:, 0:512], ps[2][:, 0:512], AF.Copy)
            return a.activation(vTc[:, 512:1024], ps[3][:, 0:512], AF.Copy)
        b.do("scalar", f, waits=("s_tmproj", "f_kvT", "s_dbg"), inc="s_vT")

        def f(t):
            ins = None
            for p in range(8):
                for half in range(2):
                    hh = 2 * p + half
                    ins = t.matmul(
                        ps[5][64 * half:64 * half + 64, p * 64:(p + 1) * 64],
                        kTc[:, hh * 64:(hh + 1) * 64], vTc[:, hh * 64:(hh + 1) * 64],
                        start=True, stop=True,
                        tile_position=(0, 64 * half))
            return ins
        b.do("tensor", f, waits=("s_kT", "s_vT", "f_ps5"), inc="s_umm")

        mirror = full or cc == 31
        pch2 = b.chainpt("c_dve") if mirror else None

        def f(v):
            for p in range(8):
                ins = v.scalar_tensor_tensor(
                    sstate[:, p * 64:(p + 1) * 64], sstate[:, p * 64:(p + 1) * 64],
                    W128[:, p:p + 1], ps[5][:, p * 64:(p + 1) * 64], ALU.mult, ALU.add)
            if mirror:
                ins.then_inc(pch2[0], 1)
                v.wait_ge(pch2[0], pch2[1])
                npo = (1 - par) * 1024
                v.tensor_copy(bass.AP(sstateb, npo, [[2048, 64], [128, 8], [1, 64]]),
                              bass.AP(sstate, 0, [[512, 64], [64, 8], [1, 64]]))
                ins = v.tensor_copy(
                    bass.AP(sstateb, 64 * 2048 + npo + 64,
                            [[2048, 64], [128, 8], [1, 64]]),
                    bass.AP(sstate, 64 * 512, [[512, 64], [64, 8], [1, 64]]))
            return ins
        b.do("vector", f, waits=("s_umm", "s_intery"), inc="s_chain")
        b.do("vector", lambda v: v.engine_nop(), inc="f_ps5")

        if not full:
            b.do("scalar", lambda a: a.nop(), waits=("s_umm",), inc="f_kvT")
            b.do("scalar", lambda a: a.nop(), waits=("s_kT", "s_vT"), inc="f_sc")
            if ci == CPT - 1:
                b.do("vector", lambda v: v.engine_nop(), waits=("s_chain",),
                     inc="f_scr")
            return

        def f(g):
            g.tensor_tensor(t2ap(sqi_o, [[128, 8], [1, 128]]),
                            wbap(WA + 12288 + c0, [[512, 8], [1, 128]]),
                            bass.AP(pq_int, 0, [[1024, 128], [128, 8], [1, 128]]),
                            ALU.mult)
            for m in (64, 32):
                nb = 64 // m
                for half in range(2):
                    g.tensor_tensor(
                        t2ap(LOFF2[m] + half * m, [[128, 8], [2 * m, nb], [1, m]],
                             p0=64 * half, np_=64),
                        wbap(WA + 12288 + c0 + m, [[512, 8], [2 * m, nb], [1, m]],
                             p0=64 * half, np_=64),
                        bass.AP(pqs[m], 64 * half * 8 * m,
                                [[8 * m, 64], [m, 8], [0, nb], [1, m]]),
                        ALU.mult)
            return g.engine_nop()
        b.do("gpsimd", f, waits=("s_pjd", "f_sq", "s_pow", "s_zero"), inc="s_sq1")

        def f(v):
            for m in (16, 8, 4):
                nb = 64 // m
                for half in range(2):
                    v.tensor_tensor(
                        t2ap(LOFF2[m] + half * m, [[128, 8], [2 * m, nb], [1, m]],
                             p0=64 * half, np_=64),
                        wbap(WA + 12288 + c0 + m, [[512, 8], [2 * m, nb], [1, m]],
                             p0=64 * half, np_=64),
                        bass.AP(pqs[m], 64 * half * 8 * m,
                                [[8 * m, 64], [m, 8], [0, nb], [1, m]]),
                        ALU.mult)
            for m in (64, 32, 8, 4):
                nb = 64 // m
                v.tensor_tensor(
                    t2ap(skL_o + SLOFF[m], [[64, 8], [m, nb], [1, m]]),
                    wbap(WA + 16384 + c0, [[512, 8], [2 * m, nb], [1, m]]),
                    bass.AP(pks[m], 0, [[8 * m, 128], [m, 8], [0, nb], [1, m]]),
                    ALU.mult)
            v.tensor_tensor(
                t2ap(skL_o + SLOFF[16], [[96, 8], [16, 3], [1, 16]]),
                wbap(WA + 16384 + c0, [[512, 8], [32, 3], [1, 16]]),
                bass.AP(pks[16], 0, [[128, 128], [16, 8], [0, 3], [1, 16]]),
                ALU.mult)
            v.tensor_tensor(
                t2ap(skL_o + SLOFF[16] + 80, [[96, 8], [1, 16]]),
                wbap(WA + 16384 + c0 + 96, [[512, 8], [1, 16]]),
                bass.AP(pks[16], 0, [[128, 128], [16, 8], [1, 16]]),
                ALU.mult)
            return v.engine_nop()
        b.do("vector", f, waits=("s_pjd", "f_sq", "s_pow", "s_zero"), inc="s_sq2")
        if cut(1):
            return

        for d in range(4):
            def f(g, d=d):
                r3 = rF.rearrange("p (a c) -> p a c", c=512)
                k3 = kF.rearrange("p (a c) -> p a c", c=512)
                return g.tensor_tensor(
                    pdb.rearrange("p (a c) -> p a c", c=128)[:, :, d:128],
                    r3[:, :, c0 + d:c0 + 128], k3[:, :, c0:c0 + 128 - d], ALU.mult)
            b.do("gpsimd", f, waits=("s_pjd", "s_pjd",
                                     "f_pdf"), inc="s_pd")

            def f(t, d=d):
                for ch in range(NCH):
                    ins = t.matmul(ps[6][0:128 - d, d * 16:(d + 1) * 16],
                                   pdb[:, ch * 128 + d:ch * 128 + 128],
                                   selband[:, ch * 64 + d * 16:ch * 64 + (d + 1) * 16],
                                   start=(ch == 0), stop=(ch == NCH - 1))
                return ins
            b.do("tensor", f, waits=("s_pd", "f_ps6", "s_sel"), inc="s_cmm")
            b.do("gpsimd", lambda g: g.engine_nop(), waits=("s_cmm",),
                 inc="f_pdf")

        def f(v):
            return v.tensor_tensor(csb[:, :], ps[6][:, 0:64], cval[:, :], ALU.mult)
        b.do("vector", f, waits=("s_cmm",), inc="s_csb")
        b.do("vector", lambda v: v.engine_nop(), inc="f_ps6")
        if cut(2):
            return

        if pending is not None:
            pending()

        def f(t):
            for ch in range(8):
                pb = ps[4] if ch < 4 else ps[6]
                col = (ch % 4) * 128
                ins = t.matmul(pb[:, col:col + 128],
                               t2ap(sqi_o + ch * 128, [[1, 128]]),
                               sstateb[:, par * 1024 + ch * 128:
                                       par * 1024 + ch * 128 + 128],
                               start=(ch % 4 == 0), stop=False)
            return ins
        b.do("tensor", f, waits=("s_sq1", "s_csb", "f_ps4", "f_ps6", "s_chain"),
             inc="s_intery")

        for d in range(4):
            vb = vsh if d % 2 == 0 else t2[:, 11264:12288]
            vbo = 14336 if d % 2 == 0 else 11264

            def f(g, d=d, vb=vb):
                return g.tensor_tensor(
                    vb.rearrange("p (a c) -> p a c", c=64),
                    vTc.rearrange("p (a c) -> p a c", c=64),
                    t2ap(15360 + d * 16, [[1, 16], [0, 64]]), ALU.mult)
            b.do("gpsimd", f, waits=("s_csb", "s_vT", f"f_vs{d % 2}"),
                 inc="s_vsh")

            def f(t, d=d, vb=vb):
                z = ident[:, :] if d == 0 else zsb[:, (d - 1) * 128:d * 128]
                for gs in range(2):
                    pb = ps[4] if gs == 0 else ps[6]
                    ins = t.matmul(pb[:, 0:512], z, vb[:, gs * 512:(gs + 1) * 512],
                                   start=False, stop=False)
                return ins
            b.do("tensor", f, waits=("s_vsh", "s_intery"), inc="s_bmm")
            b.do("gpsimd", lambda g: g.engine_nop(), waits=("s_bmm",),
                 inc=f"f_vs{d % 2}")
        if cut(3):
            return

        def f(t):
            ins = None
            for li, m in enumerate((8, 4)):
                for ch in range(8):
                    pb = ps[ch // 4]
                    col = (ch % 4) * 128
                    lh = t2ap(skL_o + SLOFF[m] + ch * 64, [[1, 64]])
                    rh = t2ap(LOFF2[m] + ch * 128, [[1, 128]])
                    ins = t.matmul(pb[li * 64:(li + 1) * 64, col:col + 128],
                                   lh, rh, start=True, stop=True,
                                   tile_position=(0, li * 64))
            return ins
        b.do("tensor", f, waits=("s_sq1", "s_sq2", "s_kT", "s_vT"), inc="s_stg")

        def f(v):
            ins = None
            for li, m in ((0, 8), (1, 4)):
                nq = 128 // (2 * m)
                for bk in range(2):
                    for half in range(2):
                        oap = bass.AP(wbig, (li * 64) * 73728 + WA + 30720
                                      + bk * 1024 + half * 128 + m,
                                      [[73728, 64], [256, 4], [2 * m, nq], [1, m]])
                        iap = bass.AP(ps[bk], (li * 64) * 512 + half * m,
                                      [[512, 64], [128, 4], [2 * m, nq], [1, m]])
                        map_ = bass.AP(stgm, (li * 64) * 64,
                                       [[64, 64], [0, 4], [m, nq], [1, m]])
                        ins = v.tensor_tensor(oap, iap, map_, ALU.mult)
            return ins
        b.do("vector", f, waits=("s_stg",), inc="s_stge")
        if cut(4):
            return

        def f(t):
            ins = None
            for bank in range(4):
                hhs = list(range(bank * 4, bank * 4 + 4))

                def scat_mm(hh, start, stop):
                    sg = stgb[:, hh * 128:(hh + 1) * 128]
                    hcol = (hh % 4) * 128
                    return t.matmul(ps[bank][:, hcol:hcol + 128], zscat[:, :], sg,
                                    start=start, stop=stop)

                def rects(hh, close):
                    ch, half = hh // 2, hh % 2
                    hcol = (hh % 4) * 128
                    for m in (64, 32):
                        nb = 64 // m
                        for qq in range(nb):
                            j0, i0 = qq * 2 * m, qq * 2 * m + m
                            lh = t2ap(skL_o + SLOFF[m] + ch * 64 + qq * m, [[1, m]])
                            rh = t2ap(LOFF2[m] + ch * 128 + qq * 2 * m + half * m,
                                      [[1, m]])
                            t.matmul(ps[bank][j0:j0 + m, hcol + i0:hcol + i0 + m],
                                     lh, rh, start=False, stop=False,
                                     tile_position=(0, j0))
                    for qq in range(3):
                        j0, i0 = qq * 32, qq * 32 + 16
                        lh = t2ap(skL_o + SLOFF[16] + ch * 96 + qq * 16, [[1, 16]])
                        rh = t2ap(LOFF2[16] + ch * 128 + qq * 32 + half * 16,
                                  [[1, 16]])
                        t.matmul(ps[bank][j0:j0 + 16, hcol + i0:hcol + i0 + 16],
                                 lh, rh, start=False, stop=False,
                                 tile_position=(0, j0))
                    lh = t2ap(skL_o + SLOFF[16] + ch * 96 + 48, [[1, 48]])
                    rh = t2ap(LOFF2[16] + ch * 128 + 96 + half * 16, [[1, 16]])
                    return t.matmul(ps[bank][64:112, hcol + 112:hcol + 128],
                                    lh, rh, start=False, stop=close,
                                    tile_position=(0, 64))

                ins = scat_mm(hhs[0], True, False)
                ins = scat_mm(hhs[1], False, False)
                ins = scat_mm(hhs[2], False, False)
                for hh in hhs[:3]:
                    ins = rects(hh, False)
                ins = scat_mm(hhs[3], False, False)
                ins = rects(hhs[3], False)
                ins = t.matmul(ps[bank][:, 0:1], t2[0:1, 15424:15552],
                               t2[0:1, 15424:15425], start=False, stop=True)
            return ins
        b.do("tensor", f, waits=("s_sq1", "s_sq2", "s_kT", "s_vT", "s_stge"),
             inc="s_scores")
        if cut(5):
            return

        def f(v):
            ins = None
            for bank in range(4):
                ins = v.tensor_tensor(
                    tri[:, bank * 512:(bank + 1) * 512].rearrange(
                        "p (a c) -> p a c", c=128),
                    ps[bank][:, 0:512].rearrange("p (a c) -> p a c", c=128),
                    bass.AP(trim, 0, [[128, 128], [0, 4], [1, 128]]), ALU.mult)
            return ins
        b.do("vector", f, waits=("s_scores", "f_tri", "s_dbg"), inc="s_tri")
        b.do("vector", lambda v: v.engine_nop(), inc="f_sc")
        b.do("vector", lambda v: v.engine_nop(), inc="f_sq")

        def f(t):
            ins = None
            for hh in range(16):
                pb = ps[4] if hh < 8 else ps[6]
                col = (hh % 8) * 64
                ins = t.matmul(pb[:, col:col + 64],
                               wbap(WA + 28672 + hh * 128, [[1, 128]]),
                               vTc[:, hh * 64:(hh + 1) * 64],
                               start=False, stop=(hh % 8 == 7))
            return ins
        b.do("tensor", f, waits=("s_tri", "s_bmm"), inc="s_apply")
        b.do("vector", lambda v: v.engine_nop(), waits=("s_apply",), inc="f_kvT")
        b.do("vector", lambda v: v.engine_nop(), waits=("s_apply",), inc="f_mixv")

        def f(v):
            v.tensor_copy(tri[:, 0:512], ps[4][:, 0:512])
            return v.tensor_copy(tri[:, 512:1024], ps[6][:, 0:512])
        b.do("vector", f, waits=("s_apply", "s_ytr"), inc="s_ytm")
        b.do("vector", lambda v: v.engine_nop(), inc="f_ps4")
        b.do("vector", lambda v: v.engine_nop(), inc="f_ps6")
        if cut(6):
            return

        if debug:
            t0d = (cc - 32) * L
            hdbg = b.pre("s_dbg", 48)
            def fd(g, t0d=t0d, hdbg=hdbg):
                ccc = cc - 32
                doff = (ccc // 4) * 128 * TH + (ccc % 4) * 1024
                g.dma_start(bass.AP(ddbg["dby"], doff, [[TH, 128], [1, 1024]]),
                            tri[:, 0:1024]).then_inc(hdbg, 16)
                g.dma_start(bass.AP(ddbg["dbkt"], doff, [[TH, 128], [1, 1024]]),
                            kTc[:, :]).then_inc(hdbg, 16)
                g.dma_start(bass.AP(ddbg["dbvt"], doff, [[TH, 128], [1, 1024]]),
                            vTc[:, :]).then_inc(hdbg, 16)
            wl1 = [(b.sems["s_ytm"][0], b.sems["s_ytm"][1]),
                   (b.sems["s_kT"][0], b.sems["s_kT"][1]),
                   (b.sems["s_vT"][0], b.sems["s_vT"][1])]
            def wdbg(g, wl1=wl1, fd=fd):
                for h, v in wl1:
                    g.wait_ge(h, v)
                fd(g)
            b.q["gpsimd"].append(wdbg)


        def tail():
            def f(a):
                return a.activation(gnsq[:, :], tri[:, 0:512], AF.Square)
            b.do("scalar", f, waits=("s_ytm", "s_gnr"), inc="s_ysq")

            def f(v):
                v.tensor_reduce(rowsG[:, 0:16].rearrange("p (a c) -> p a c", c=1),
                                wbap(WA + 28672, [[64, 16], [1, 64]]), AX.X, ALU.add)
                return v.tensor_reduce(rowsG[:, 16:24].rearrange("p (a c) -> p a c", c=1),
                                       bass.AP(gnsq, 0, [[512, 128], [64, 8], [1, 64]]),
                                       AX.X, ALU.add)
            b.do("vector", f, waits=("s_ysq",), inc="s_gnr")

            def f(a):
                return a.activation(gnsq[:, :], tri[:, 512:1024], AF.Square)
            b.do("scalar", f, waits=("s_ytm", "s_gnr"), inc="s_ysq")

            def f(v):
                return v.tensor_reduce(rowsG[:, 24:32].rearrange("p (a c) -> p a c", c=1),
                                       bass.AP(gnsq, 0, [[512, 128], [64, 8], [1, 64]]),
                                       AX.X, ALU.add)
            b.do("vector", f, waits=("s_ysq",), inc="s_gnr")

            def f(v):
                v.tensor_scalar_mul(rowsG[:, 32:48], rowsG[:, 0:16], 1.0 / SD)
                return v.tensor_scalar_mul(rowsG[:, 48:64], rowsG[:, 16:32], 1.0 / SD)
            b.do("vector", f, waits=("s_gnr",), inc="s_gnv")

            def f(a):
                return a.activation(rowsG[:, 64:80], rowsG[:, 32:48], AF.Square)
            b.do("scalar", f, waits=("s_gnv",), inc="s_gna")

            def f(v):
                v.tensor_sub(rowsG[:, 48:64], rowsG[:, 48:64], rowsG[:, 64:80])
                return v.engine_nop()
            b.do("vector", f, waits=("s_gna",), inc="s_gnv")

            pga2 = b.chainpt("c_act")

            def f(a):
                a.activation(rowsG[:, 48:64], rowsG[:, 48:64], AF.Ln,
                             bias=smf[:, 129:130]).then_inc(pga2[0], 1)
                a.wait_ge(pga2[0], pga2[1])
                return a.activation(rowsG[:, 48:64], rowsG[:, 48:64], AF.Exp,
                                    scale=-0.5)
            b.do("scalar", f, waits=("s_gnv", "s_zero"), inc="s_gna")

            def f(v):
                return v.tensor_mul(rowsG[:, 32:48], rowsG[:, 32:48], rowsG[:, 48:64])
            b.do("vector", f, waits=("s_gna",), inc="s_gnv")
            if cut(7):
                return

            pga = b.chainpt("c_gp")

            def f(g):
                y3 = wbap(WA + 28672, [[64, 16], [1, 64]])
                g.tensor_tensor(y3, y3, bass.AP(rowsG, 48, [[80, 128], [1, 16], [0, 64]]),
                                ALU.mult).then_inc(pga[0], 1)
                g.wait_ge(pga[0], pga[1])
                return g.tensor_tensor(y3, y3,
                                       bass.AP(rowsG, 32, [[80, 128], [1, 16], [0, 64]]),
                                       ALU.subtract)
            b.do("gpsimd", f, waits=("s_gnv", "s_dbg"), inc="s_gnap")

            def f(t):
                ins = None
                for sl in range(8):
                    ins = t.transpose(pbt[:, sl * 128:(sl + 1) * 128],
                                      tri[:, sl * 128:(sl + 1) * 128], ident[:, :])
                return ins
            b.do("tensor", f, waits=("s_gnap", "f_pbt"), inc="s_ytr")

            def f(a):
                return a.activation(tri[:, 1024:2048], pbt[:, 0:1024], AF.Copy)
            b.do("scalar", f, waits=("s_ytr",), inc="s_yfm")
            b.do("scalar", lambda a: a.nop(), inc="f_pbt")

            def f(a):
                ins = None
                for ch in range(NCH):
                    sl = tri[:, 1024 + ch * 128:1024 + (ch + 1) * 128]
                    ins = a.activation(sl, sl, AF.Identity, bias=LNB[:, ch:ch + 1],
                                       scale=LNW[:, ch:ch + 1])
                return ins
            b.do("scalar", f, waits=("s_yfm",), inc="s_yga")

            def f(g):
                ins = None
                for ch in range(NCH):
                    sl = tri[:, 1024 + ch * 128:1024 + (ch + 1) * 128]
                    ins = g.tensor_tensor(
                        ygF[:, ch * 512 + c0:ch * 512 + c0 + 128], sl,
                        gF[:, ch * 512 + c0:ch * 512 + c0 + 128], ALU.mult)
                return ins
            b.do("gpsimd", f, waits=("s_yga", "s_pjd", "f_ygF"), inc="s_yg")
            b.do("gpsimd", lambda g: g.engine_nop(), inc="f_tri")

            if ci == CPT - 1:
                b.do("vector", lambda v: v.engine_nop(), waits=("s_yg",), inc="f_scr")

        return tail
    def oproj_tile(ti, ph):
        t0 = ti * TILE
        for co in range(NCH):
            bank = co % 2

            def f(t, co=co, bank=bank):
                for ci in range(NCH):
                    ins = t.matmul(ps[bank][:, 0:512],
                                   W_o[:, ci * 1024 + co * 128:ci * 1024 + co * 128 + 128],
                                   ygF[:, ci * 512:(ci + 1) * 512],
                                   start=(ci == 0), stop=(ci == NCH - 1))
                return ins
            b.do("tensor", f, waits=("s_yg", f"f_pj{bank}", "f_sc"), inc="s_mm")

            def f(v, co=co, bank=bank):
                return v.tensor_copy(attf[:, 0:512], ps[bank][:, 0:512])
            b.do("vector", f, waits=("s_mm", "f_at"), inc="s_ev")
            b.do("vector", lambda v: v.engine_nop(), inc=f"f_pj{bank}")

            def f(g, co=co, bank=bank):
                return g.dma_start(
                    bass.AP(d_x1, co * 128 * TH + t0, [[TH, 128], [1, 512]]),
                    attf[:, 0:512], accum_op=ALU.add)
            b.do("gpsimd", f, waits=("s_ev", "s_x1pre"), inc="s_x1acc", k=16)
            b.do("gpsimd", lambda g: g.engine_nop(), waits=("s_x1acc",),
                 inc="f_at")
        if debug:
            hdbg2 = b.pre("s_dbg", 64)
            wl2 = [(b.sems["s_yg"][0], b.sems["s_yg"][1]),
                   (b.sems["s_pjd"][0], b.sems["s_pjd"][1])]
            def fd(g, t0=t0, hdbg2=hdbg2, wl2=wl2):
                for h, v in wl2:
                    g.wait_ge(h, v)
                ap3 = [[TH, 128], [TH * 128, 8], [1, 512]]
                g.dma_start(bass.AP(ddbg["dbr"], t0, ap3),
                            rF.rearrange("p (a c) -> p a c", c=512)).then_inc(hdbg2, 16)
                g.dma_start(bass.AP(ddbg["dbk"], t0, ap3),
                            kF.rearrange("p (a c) -> p a c", c=512)).then_inc(hdbg2, 16)
                g.dma_start(bass.AP(ddbg["dbg"], t0, ap3),
                            gF.rearrange("p (a c) -> p a c", c=512)).then_inc(hdbg2, 16)
                g.dma_start(bass.AP(ddbg["dbyg"], t0, ap3),
                            ygF.rearrange("p (a c) -> p a c", c=512)).then_inc(hdbg2, 16)
            b.q["gpsimd"].append(fd)
        b.do("tensor", lambda t: t.nop(), waits=("s_mm",), inc="f_ygF")
        b.do("tensor", lambda t: t.nop(), waits=("s_mm",), inc="f_mix")

        def f(s):
            return s.dma_start(bass.AP(d_out, t0, [[TH, 128], [TH * 128, 8], [1, 512]]),
                               bass.AP(d_x1, t0, [[TH, 128], [TH * 128, 8], [1, 512]]))
        b.do("sync", f, waits=("s_x1acc",), inc="s_outpre", k=16)

    def ffn_tile(ti):
        t0 = ti * TILE
        ph = "fn"
        ln_xhat(ti, d_x1, ph, x1b, 0, hb[:, 0:4096], 4096, "t2")

        for co in range(24):
            bk = co % 2

            def f(t, co=co, bk=bk):
                for ci in range(NCH):
                    ins = t.matmul(ps[bk][:, 0:512],
                                   Fg[:, ci * FFH + co * 128:ci * FFH + co * 128 + 128],
                                   x1b[:, ci * 512:(ci + 1) * 512],
                                   start=(ci == 0), stop=(ci == NCH - 1))
                return ins
            b.do("tensor", f, waits=("s_xhat", f"f_fg{bk}", "s_fw",
                                     "f_sc", "f_pj0", "f_pj1"),
                 inc="s_mm")

            def f(a, co=co, bk=bk):
                return a.activation(hb[:, co * 512:(co + 1) * 512], ps[bk][:, 0:512],
                                    AF.Silu, scale=1.0 / 127.0)
            b.do("scalar", f, waits=("s_mm",), inc=f"f_fg{bk}")

            def f(t, co=co, bk=bk):
                for ci in range(NCH):
                    ins = t.matmul(ps[2 + bk][:, 0:512],
                                   Fu[:, ci * FFH + co * 128:ci * FFH + co * 128 + 128],
                                   x1b[:, ci * 512:(ci + 1) * 512],
                                   start=(ci == 0), stop=(ci == NCH - 1))
                return ins
            b.do("tensor", f, waits=("s_xhat", f"f_fu{bk}", "s_fw",
                                     "s_kT", "s_vT"),
                 inc="s_mm")

            def f(v, co=co, bk=bk):
                sl = hb[:, co * 512:(co + 1) * 512]
                return v.tensor_tensor(sl, sl, ps[2 + bk][:, 0:512], ALU.mult)
            b.do("vector", f, waits=("s_mm", f"f_fg{bk}"), inc=f"f_fu{bk}")
        b.do("vector", lambda v: v.engine_nop(), waits=(f"f_fu0", "f_fu1"),
             inc="s_h")

        for co in range(NCH):
            bk = 4 + co % 2

            def f(t, co=co, bk=bk):
                for ci in range(24):
                    ins = t.matmul(ps[bk][:, 0:512],
                                   Fd[:, ci * 1024 + co * 128:ci * 1024 + co * 128 + 128],
                                   hb[:, ci * 512:(ci + 1) * 512],
                                   start=(ci == 0), stop=(ci == 23))
                return ins
            b.do("tensor", f, waits=("s_h", f"f_fd{bk}", "f_ps4", "f_ps5"),
                 inc="s_mm")

            def f(v, co=co, bk=bk):
                return v.tensor_scalar_mul(attf[:, 0:512],
                                           ps[bk][:, 0:512], 1.0 / (127.0 * 127.0))
            b.do("vector", f, waits=("s_mm", "f_fat"), inc="s_ev")
            b.do("vector", lambda v: v.engine_nop(), inc=f"f_fd{bk}")

            def f(g, co=co, bk=bk):
                return g.dma_start(
                    bass.AP(d_out, co * 128 * TH + t0, [[TH, 128], [1, 512]]),
                    attf[:, 0:512], accum_op=ALU.add)
            b.do("gpsimd", f, waits=("s_ev", "s_outpre"), inc="s_outacc", k=16)
            b.do("gpsimd", lambda g: g.engine_nop(), waits=("s_outacc",),
                 inc="f_fat")
        b.do("vector", lambda v: v.engine_nop(), waits=("s_h",),
             inc="f_xh")
        b.do("vector", lambda v: v.engine_nop(), waits=("s_mm",),
             inc="f_scr")

    setup()
    b.do("gpsimd", lambda g: g.memset(onesb[:, :], 1.0), inc="s_onesb")

    cc = 0
    npre = NTIL if trunc != 1 else 0
    for ti in range(npre):
        ln_xhat(ti, d_px, "pre", xh, 0, rF, 0, "wb")

        def fbc(g, sl=ti % 2):
            return g.tensor_copy(bcolb[:, sl * 8:sl * 8 + 8], xh[:, 511::512])
        b.do("gpsimd", fbc, waits=("s_xhat",), inc="s_bcol")
        mix_full("pre", mixk, 1, "s_xhat", inc="s_mixk", bslot=(ti + 1) % 2,
                 extraw=("s_bcol",))
        mix_full("pre", mixr, 2, "s_xhat", inc="s_mixv", bslot=(ti + 1) % 2)
        b.do("gpsimd", lambda g: g.engine_nop(),
             waits=("s_mixk", "s_mixv"), inc="f_xh")
        for ci in range(CPT):
            scan_chunk(cc, ci, "pre", False)
            cc += 1
        b.do("tensor", lambda t: t.nop(), waits=("s_tmproj",), inc="f_mix")

    if trunc in (1, 2):
        nmain = 0
    elif 31 <= trunc <= 49:
        nmain = 1
    else:
        nmain = NTIL
    pend = None
    for ti in range(nmain):
        ln_xhat(ti, d_mx, "mn", xh, 0, rF, 0, "wb")

        def fbc(g, sl=ti % 2):
            return g.tensor_copy(bcolb[:, sl * 8:sl * 8 + 8], xh[:, 511::512])
        b.do("gpsimd", fbc, waits=("s_xhat",), inc="s_bcol")
        mix_full("mn", mixr, 0, "s_xhat", inc="s_mixr", bslot=(ti + 1) % 2,
                 extraw=("s_bcol",))
        mix_full("mn", mixg, 3, "s_xhat", inc="s_mixg", bslot=(ti + 1) % 2)
        mix_full("mn", mixk, 1, "s_xhat", inc="s_mixk", bslot=(ti + 1) % 2)
        if trunc == 31:
            break
        fm_proj("mn", "r", W_r, mixr, rF, AF.Copy, ("s_mixr", "f_sc"))
        fm_proj("mn", "g", W_g, mixg, gF, AF.Silu, ("s_mixg", "f_sc"))
        fm_proj("mn", "k", W_k, mixk, kF, AF.Copy, ("s_mixk", "f_sc"))
        mix_full("mn", mixr, 2, "s_xhat", inc="s_mixv", bslot=(ti + 1) % 2,
                 extraw=("s_pjd",))
        b.do("gpsimd", lambda g: g.engine_nop(),
             waits=("s_mixr", "s_mixg", "s_mixk", "s_mixv"), inc="f_xh")
        if trunc == 32:
            break
        nchunks = 1 if (trunc == 33 or 41 <= trunc <= 48) else CPT
        for ci in range(nchunks):
            pend = scan_chunk(cc, ci, "mn", True, pending=pend)
            cc += 1
        if trunc in (33, 34) or 41 <= trunc <= 48:
            break
        if pend is not None:
            pend()
            pend = None
        oproj_tile(ti, "mn")

    if trunc in (0, 4):
        fw_w = ("s_yg", "f_tri", "s_mm", "s_dbg")
        b.do("gpsimd", lambda g: g.dma_start(
            Fg.rearrange("p (a c) -> p a c", c=FFH),
            bass.AP(d_fg, 0, [[FFH, 128], [FFH * 128, 8], [1, FFH]])),
            waits=fw_w, inc="s_fw", k=16)
        b.do("gpsimd", lambda g: g.dma_start(
            Fu.rearrange("p (a c) -> p a c", c=FFH),
            bass.AP(d_fu, 0, [[FFH, 128], [FFH * 128, 8], [1, FFH]])),
            inc="s_fw", k=16)
        b.do("gpsimd", lambda g: g.dma_start(
            Fd.rearrange("p (a c) -> p a c", c=1024),
            bass.AP(d_fd, 0, [[1024, 128], [1024 * 128, 24], [1, 1024]])),
            inc="s_fw", k=16)
        b.do("vector", lambda v: v.engine_nop(), waits=("s_x1acc",), inc="f_xh")
        b.do("vector", lambda v: v.engine_nop(), waits=("s_x1acc",), inc="f_scr")

    nffn = NTIL if trunc == 0 else 0
    for ti in range(nffn):
        ffn_tile(ti)

    if trunc:
        b.waitonly("sync", *list(b.sems))
    else:
        b.waitonly("sync", "s_outacc", "s_outpre", "s_x1acc")

    with nc.Block() as blk:
        blk.sync(lambda e: [f(e) for f in b.q["sync"]] and None)
        blk.gpsimd(lambda e: [f(e) for f in b.q["gpsimd"]] and None)
        blk.tensor(lambda e: [f(e) for f in b.q["tensor"]] and None)
        blk.vector(lambda e: [f(e) for f in b.q["vector"]] and None)
        blk.scalar(lambda e: [f(e) for f in b.q["scalar"]] and None)
    st.close()
    return nc


def _consts():
    con = np.zeros((128, 1024), np.float32)
    con[:, 0:128] = np.eye(128, dtype=np.float32)
    tm = np.zeros((128, 128), np.float32)
    for m in LEVELS:
        for q in range(0, 128, 2 * m):
            tm[q:q + m, q + m:q + 2 * m] = 1.0
    con[:, 128:256] = tm
    cv = np.zeros((128, 64), np.float32)
    t = np.arange(128)
    for d in range(4):
        if d == 0:
            val = np.ones(128)
        else:
            val = ((t + d) <= 127) & (((t + d) % 4) >= d)
        cv[:, d * 16:(d + 1) * 16] = val.astype(np.float32)[:, None]
    con[:, 256:320] = cv
    for d in range(4):
        z = np.zeros((128, 128), np.float32)
        j = np.arange(128 - d)
        z[j, j + d] = 1.0
        con[:, 320 + d * 128:320 + (d + 1) * 128] = z
    zs = np.zeros((128, 128), np.float32)
    for li, m in enumerate((8, 4)):
        for p in range(64):
            q, jj = p // m, p % m
            zs[li * 64 + p, 2 * m * q + jj] = 1.0
    con[:, 832:960] = zs
    sm = np.zeros((128, 64), np.float32)
    for li, m in enumerate((8, 4)):
        for p in range(64):
            qj = p // m
            sm[li * 64 + p, qj * m:(qj + 1) * m] = 1.0
    con[:, 960:1024] = sm
    return con


_CACHE = {}


def kernel(**inputs):
    x0 = np.ascontiguousarray(np.asarray(inputs["L0_attn_xq"], np.float32))
    wT = {z: np.ascontiguousarray(np.asarray(inputs[f"L0_attn_W{z}"], np.float32).T)
          for z in ("r", "k", "v", "g", "o")}
    fgT = np.ascontiguousarray(np.asarray(inputs["L0_ffn_w_gate"], np.int32).T)
    fuT = np.ascontiguousarray(np.asarray(inputs["L0_ffn_w_up"], np.int32).T)
    fdT = np.ascontiguousarray(np.asarray(inputs["L0_ffn_w_down"], np.int32).T)
    mix = np.ascontiguousarray(np.stack([
        np.asarray(inputs["L0_attn_time_mix_r"], np.float32).ravel(),
        np.asarray(inputs["L0_attn_time_mix_k"], np.float32).ravel(),
        np.asarray(inputs["L0_attn_time_mix_v"], np.float32).ravel(),
        np.asarray(inputs["L0_attn_time_mix_g"], np.float32).ravel()]))
    dec = np.ascontiguousarray(np.asarray(inputs["L0_attn_time_decay"], np.float32).ravel())
    faa = np.ascontiguousarray(np.asarray(inputs["L0_attn_time_faaaa"], np.float32).ravel())
    lnw = np.ascontiguousarray(np.asarray(inputs["L0_attn_ln_w"], np.float32).ravel())
    lnb = np.ascontiguousarray(np.asarray(inputs["L0_attn_ln_b"], np.float32).ravel())
    con = _consts()

    if "nc" not in _CACHE:
        _CACHE["nc"] = build_program()
    nc = _CACHE["nc"]

    zero = np.zeros((C, TH), np.float32)
    in_maps = []
    for core in range(8):
        bb, j = core // 2, core % 2
        xb = x0[bb]
        px = zero if j == 0 else np.ascontiguousarray(xb[0:TH].T)
        mx = np.ascontiguousarray(xb[j * TH:(j + 1) * TH].T)
        in_maps.append(dict(px=px, mx=mx,
                            wrT=wT["r"], wkT=wT["k"], wvT=wT["v"], wgT=wT["g"],
                            woT=wT["o"], fgT=fgT, fuT=fuT, fdT=fdT, mix=mix,
                            dec=dec, faa=faa, lnw=lnw, lnb=lnb, consts=con))
    res = run_bass_kernel_spmd(nc, in_maps, core_ids=list(range(8)))
    out = np.empty((4, 8192, 1024), np.float32)
    for core in range(8):
        bb, j = core // 2, core % 2
        out[bb, j * TH:(j + 1) * TH] = res.results[core]["out"].T
    return out


if __name__ == "__main__":
    nc = build_program()
    print("built ok; instructions:", nc.n_instructions())
